# version 19
# baseline (speedup 1.0000x reference)
import numpy as np
import concourse.bass as bass
import concourse.mybir as mybir
from concourse.bass_utils import run_bass_kernel_spmd
from contextlib import ExitStack

F32 = mybir.dt.float32
BF16 = mybir.dt.bfloat16
AF = mybir.ActivationFunctionType
ALU = mybir.AluOpType
AX = mybir.AxisListType

DM = 2048
NDC = 16
T = 1024
S = 4096
NKC = 32
H = 16
HID = 8192
EQKV = 6144
EPS = 1e-5
SCL = float(1.0 / np.sqrt(128.0))
SLC = 256

ENGINES = ("tensor", "scalar", "vector", "gpsimd", "sync")

DEBUG = False
LAST_TAPS = None


class Sem:
    __slots__ = ("name", "n", "h")

    def __init__(self, name):
        self.name = name
        self.n = 0
        self.h = None

    def inc(self, k=1):
        self.n += k
        return self.n


class Prog:
    def __init__(self):
        self.ops = {e: [] for e in ENGINES}

    def op(self, eng, build, waits=(), inc=None, inc_n=1):
        waits = [(s, v) for (s, v) in waits]
        if inc is not None:
            inc.inc(inc_n)

        def emit(engine, _b=build, _w=waits, _i=inc, _n=inc_n):
            for (s, v) in _w:
                engine.wait_ge(s.h, v)
            ins = _b(engine)
            if _i is not None:
                ins.then_inc(_i.h, _n)
        self.ops[eng].append(emit)

    def wait(self, eng, waits):
        waits = [(s, v) for (s, v) in waits]

        def emit(engine, _w=waits):
            for (s, v) in _w:
                engine.wait_ge(s.h, v)
        self.ops[eng].append(emit)


def build_nc():
    nc = bass.Bass("TRN2", target_bir_lowering=False)

    xc = nc.dram_tensor("xc", [DM, T], F32, kind="ExternalInput").ap()
    wqkvT = nc.dram_tensor("wqkvT", [DM, EQKV], F32, kind="ExternalInput").ap()
    woutT = nc.dram_tensor("woutT", [DM, DM], F32, kind="ExternalInput").ap()
    wgT = nc.dram_tensor("wgT", [DM, HID], F32, kind="ExternalInput").ap()
    wuT = nc.dram_tensor("wuT", [DM, HID], F32, kind="ExternalInput").ap()
    wdT = nc.dram_tensor("wdT", [HID, DM], F32, kind="ExternalInput").ap()
    outc = nc.dram_tensor("outc", [DM, T], F32, kind="ExternalOutput").ap()

    taps = {}
    if DEBUG:
        def mktap(name, shape, dt):
            taps[name] = nc.dram_tensor(name, shape, dt, kind="ExternalOutput").ap()
        mktap("tap_xh1", [128, NDC * T], BF16)
        mktap("tap_vout", [T, DM], BF16)
        mktap("tap_kin", [4 * DM, T], BF16)
        mktap("tap_vin", [4 * T, DM], BF16)
        mktap("tap_q", [DM, T], BF16)
        mktap("tap_attn", [128, NDC * T], BF16)
        mktap("tap_y", [128, NDC * T], F32)
        mktap("tap_x2", [DM, T], F32)
        mktap("tap_xh3", [128, NDC * T], BF16)
        mktap("tap_y2", [128, NDC * T], F32)

    kout = nc.dram_tensor("kout", [DM, T], BF16, kind="Internal")
    qout = nc.dram_tensor("qout", [DM, T], BF16, kind="Internal")
    vout = nc.dram_tensor("vout", [T, DM], BF16, kind="Internal")
    kin = nc.dram_tensor("kin", [4 * DM, T], BF16, kind="Internal")
    vin = nc.dram_tensor("vin", [4 * T, DM], BF16, kind="Internal")
    x2d = nc.dram_tensor("x2d", [DM, T], F32, kind="Internal")

    p = Prog()
    stk = ExitStack()
    sbt = lambda name, shape, dt: stk.enter_context(nc.sbuf_tensor(name, shape, dt))

    with stk:
        A = sbt("A", [128, NDC * T], F32)
        B = sbt("B", [128, NDC * T], BF16)
        D = sbt("D", [128, 2 * 8192], BF16)
        WB = [sbt(f"WB{i}", [128, NDC * SLC], BF16) for i in range(3)]
        QS = [sbt(f"QS{i}", [128, T], BF16) for i in range(2)]
        SG = [sbt(f"SG{i}", [128, T], BF16) for i in range(2)]
        XIO = [sbt(f"XIO{i}", [128, T], F32) for i in range(2)]
        SCR = [sbt(f"SCR{i}", [128, T], F32) for i in range(3)]
        KST = [sbt(f"KST{i}", [128, T], BF16) for i in range(2)]
        RB = [sbt(f"RB{i}", [128, 512], F32) for i in range(2)]
        SM = sbt("SM", [1, 4096], F32)
        ONE_F = sbt("ONE_F", [128, 1], F32)
        ONE_B = sbt("ONE_B", [128, 1], BF16)
        ONE_R = sbt("ONE_R", [1, 128], F32)
        BIAS = sbt("BIAS", [128, 1], F32)
        CST = sbt("CST", [1, 2], F32)
        SM_S1 = SM[0:1, 0:1024]
        SM_S2 = SM[0:1, 1024:2048]
        SM_M = SM[0:1, 2048:3072]
        SM_MSQ = SM[0:1, 0:1024]
        SM_VAR = SM[0:1, 3072:4096]
        SM_SUM = [SM[0:1, 512 * i:512 * (i + 1)] for i in range(2)]
        SM_REC = [SM[0:1, 1024 + 512 * i:1024 + 512 * (i + 1)] for i in range(2)]
        SM_KMX = SM[0:1, 2048:2064]
        SM_QMX = SM[0:1, 2064:2080]
        SM_SC = SM[0:1, 2080:2096]

        PSL = [stk.enter_context(nc.psum_tensor(f"PSL{i}", [128, 1024], F32))
               for i in range(2)]
        PSO = [stk.enter_context(nc.psum_tensor(f"PSO{i}", [128, 512], F32))
               for i in range(2)]
        PSS = [stk.enter_context(nc.psum_tensor(f"PSS{i}", [128, 512], F32))
               for i in range(2)]
        STAT = [PSO[0][0:1, :], PSO[1][0:1, :], PSS[0][0:1, :], PSS[1][0:1, :]]

        sPE = Sem("sPE")
        sACT = Sem("sACT")
        sDVE = Sem("sDVE")
        sPOOL = Sem("sPOOL")
        sCC = Sem("sCC")
        XL = [Sem(f"xl{i}") for i in range(8)]
        WL = [Sem(f"wl{i}") for i in range(8)]
        HL = [Sem(f"hl{i}") for i in range(4)]
        for s in (sPE, sACT, sDVE, sPOOL, sCC, *XL, *WL, *HL):
            s.h = stk.enter_context(nc.semaphore(s.name))

        xseq = [0]

        def xdma(build, waits=()):
            s = XL[xseq[0] % len(XL)]
            xseq[0] += 1
            p.op("sync", build, waits=list(waits) + [(s, s.n)], inc=s, inc_n=16)
            return (s, s.n)

        hseq = [0]

        def hdma(build, waits=()):
            s = HL[hseq[0] % len(HL)]
            hseq[0] += 1
            p.op("sync", build, waits=list(waits) + [(s, s.n)], inc=s, inc_n=16)
            return (s, s.n)

        wseq = [0]

        def wdma(build, waits=()):
            s = WL[wseq[0] % len(WL)]
            wseq[0] += 1
            p.op("gpsimd", build, waits=list(waits) + [(s, s.n)], inc=s, inc_n=16)
            return (s, s.n)

        def xall():
            return [(s, s.n) for s in XL]

        def mw(pairs):
            best = {}
            for (s, v) in pairs:
                if s not in best or v > best[s]:
                    best[s] = v
            return list(best.items())

        def barrier():
            marks = [(sPE, sPE.n), (sACT, sACT.n), (sDVE, sDVE.n),
                     (sPOOL, sPOOL.n)] + xall()
            for e in ("tensor", "scalar", "vector", "sync"):
                p.wait(e, marks)

        wb_state = {"seq": 0, "pe": []}

        def wslice(src, c0):
            i = wb_state["seq"]
            wb_state["seq"] += 1
            waits = []
            if i >= 3:
                waits.append(wb_state["pe"][i - 3])
            ap = src[:, c0:c0 + SLC].rearrange("(dc p) e -> p dc e", p=128)
            buf = WB[i % 3][:, :].rearrange("p (dc e) -> p dc e", e=SLC)
            sd = wdma(lambda g, _a=ap, _b=buf: g.dma_start(out=_b, in_=_a),
                      waits=waits)
            return i, sd

        def wslice_consumed(guard):
            wb_state["pe"].append(guard)

        wd_state = {"seq": 0, "g": [], "g0": None}

        def wdslice(g_, dc):
            i = wd_state["seq"]
            wd_state["seq"] += 1
            waits = []
            if i >= 4:
                waits.append(wd_state["g"][i - 4])
            elif i == 0:
                waits.append(wd_state["g0"])
            ap = wdT[g_ * 1024:(g_ + 1) * 1024, dc * 128:(dc + 1) * 128].rearrange(
                "(hc p) e -> p hc e", p=128)
            buf = D[:, 8192 + (i % 4) * 1024:8192 + (i % 4) * 1024 + 1024].rearrange(
                "p (hc e) -> p hc e", e=128)
            sd = wdma(lambda g, _a=ap, _b=buf: g.dma_start(out=_b, in_=_a),
                      waits=waits)
            return i, sd

        p.op("vector", lambda v: v.memset(ONE_F[:, :], 1.0))
        p.op("vector", lambda v: v.memset(ONE_B[:, :], 1.0))
        p.op("vector", lambda v: v.memset(ONE_R[:, :], 1.0))
        p.op("vector", lambda v: v.memset(CST[0:1, 0:1], EPS))
        p.op("vector", lambda v: v.memset(CST[0:1, 1:2], float(np.log(SCL))),
             inc=sDVE)
        ones_ready = sDVE.n

        ldx = {}
        for q4 in range(4):
            ldx[q4] = xdma(lambda s, _q=q4: s.dma_start(
                out=A[:, _q * 4 * T:(_q + 1) * 4 * T].rearrange("p (c t) -> p c t", t=T),
                in_=xc[_q * 512:(_q + 1) * 512, :].rearrange("(c p) t -> p c t", p=128)))

        ln1_pe = {}
        for dc in range(NDC):
            cw = [ldx[dc // 4]]
            if dc >= 2:
                cw.append((sPE, ln1_pe[dc - 2]))
            p.op("scalar", lambda s, _d=dc: s.activation(
                SG[_d % 2][:, :], A[:, _d * T:(_d + 1) * T], AF.Square),
                waits=cw, inc=sACT)
            sq = sACT.n
            st, sp = (dc == 0), (dc == NDC - 1)
            for n in range(2):
                p.op("tensor", lambda t, _d=dc, _n=n, _s=st, _e=sp: t.matmul(
                    STAT[_n], ONE_F[:, :],
                    A[:, _d * T + _n * 512:_d * T + (_n + 1) * 512], start=_s, stop=_e),
                    waits=([(sACT, sq), (sDVE, ones_ready)] if n == 0 else ()))
            for n in range(2):
                p.op("tensor", lambda t, _d=dc, _n=n, _s=st, _e=sp: t.matmul(
                    STAT[2 + _n], ONE_B[:, :],
                    SG[_d % 2][:, _n * 512:(_n + 1) * 512], start=_s, stop=_e),
                    inc=(sPE if n == 1 else None))
            ln1_pe[dc] = sPE.n

        def ln_junction():
            for j in range(2):
                p.op("vector", lambda v, _j=j: v.tensor_copy(
                    SM_S1[:, _j * 512:(_j + 1) * 512], STAT[_j]),
                    waits=([(sPE, sPE.n)] if j == 0 else ()))
            for j in range(2):
                p.op("vector", lambda v, _j=j: v.tensor_copy(
                    SM_S2[:, _j * 512:(_j + 1) * 512], STAT[2 + _j]),
                    inc=(sDVE if j == 1 else None))
            p.op("vector", lambda v: v.tensor_scalar_mul(SM_M[:, :], SM_S1[:, :], 1.0 / DM),
                 waits=[(sDVE, sDVE.n)], inc=sDVE)
            p.op("vector", lambda v: v.tensor_mul(SM_MSQ[:, :], SM_M[:, :], SM_M[:, :]),
                 waits=[(sDVE, sDVE.n)], inc=sDVE)
            p.op("vector", lambda v: v.scalar_tensor_tensor(
                SM_VAR[:, :], SM_S2[:, :], 1.0 / DM, SM_MSQ[:, :], ALU.mult, ALU.subtract),
                waits=[(sDVE, sDVE.n)], inc=sDVE)
            mres = sDVE.n
            p.op("scalar", lambda s: s.activation(SM_MSQ[:, :], SM_VAR[:, :], AF.Ln,
                                                  bias=CST[0:1, 0:1]),
                 waits=[(sDVE, mres)], inc=sACT)
            p.op("scalar", lambda s: s.activation(SM_VAR[:, :], SM_MSQ[:, :], AF.Exp,
                                                  scale=-0.5),
                 waits=[(sACT, sACT.n)], inc=sACT)
            rres = sACT.n
            for n in range(2):
                p.op("tensor", lambda t, _n=n: t.matmul(
                    PSL[0][:, _n * 512:(_n + 1) * 512], ONE_R[:, :],
                    SM_M[:, _n * 512:(_n + 1) * 512], start=True, stop=True),
                    waits=([(sDVE, mres)] if n == 0 else ()))
            for n in range(2):
                p.op("tensor", lambda t, _n=n: t.matmul(
                    PSL[1][:, _n * 512:(_n + 1) * 512], ONE_R[:, :],
                    SM_VAR[:, _n * 512:(_n + 1) * 512], start=True, stop=True),
                    waits=([(sACT, rres)] if n == 0 else ()),
                    inc=(sPE if n == 1 else None))
            bc = sPE.n
            p.op("vector", lambda v: v.tensor_copy(XIO[0][:, :], PSL[0][:, :]),
                 waits=[(sPE, bc)], inc=sDVE)
            p.op("vector", lambda v: v.tensor_copy(XIO[1][:, :], PSL[1][:, :]),
                 inc=sDVE)
            return sDVE.n

        bc1 = ln_junction()

        PT = [D[:, 2048 * i:2048 * (i + 1)].bitcast(F32) for i in range(2)]
        norm_done = {}
        last_norm = {"vector": None, "gpsimd": None}
        for dc in range(NDC):
            pooled = (dc % 3 == 2)
            eng, sem = ("gpsimd", sPOOL) if pooled else ("vector", sDVE)
            scr = PT[0] if pooled else SCR[1][:, :]
            w = [(sDVE, bc1)]
            if last_norm[eng] is not None:
                w.append(last_norm[eng])
            p.op(eng, lambda v, _d=dc, _s=scr: v.tensor_tensor(
                _s, A[:, _d * T:(_d + 1) * T], XIO[0][:, :], ALU.subtract),
                waits=w, inc=sem)
            t1 = sem.n
            p.op(eng, lambda v, _d=dc, _s=scr: v.tensor_tensor(
                B[:, _d * T:(_d + 1) * T], _s, XIO[1][:, :], ALU.mult),
                waits=[(sem, t1)], inc=sem)
            norm_done[dc] = (sem, sem.n)
            last_norm[eng] = norm_done[dc]
        xh1_done = sDVE.n
        xh1_pool = sPOOL.n
        barrier()
        p.wait("tensor", [(sPOOL, xh1_pool)])

        dbg_attn_gate = None
        if DEBUG:
            dbg_attn_gate = xdma(lambda s: s.dma_start(out=taps["tap_xh1"], in_=B[:, :]),
                                 waits=[(sDVE, xh1_done)])

        kq = {"i": 0, "evt": {}, "pend": None}

        def kq_bounds_flush():
            if kq["pend"] is None:
                return
            i, par, sq, slot = kq["pend"]
            kq["pend"] = None
            bmw = [(sACT, sq)]
            if i >= 2:
                bmw.append((sDVE, kq["evt"][i - 2][4]))
            for n in range(2):
                p.op("tensor", lambda t, _n=n, _par=par: t.matmul(
                    PSS[_par][0:1, :], ONE_B[:, :],
                    SG[_par][:, _n * 512:(_n + 1) * 512],
                    start=(_n == 0), stop=(_n == 1)),
                    waits=(bmw if n == 0 else ()), inc=(sPE if n == 1 else None))
            bm = sPE.n
            p.op("vector", lambda v, _par=par, _slot=slot, _i=i: v.reduce_max(
                _slot[:, _i % 16:_i % 16 + 1], PSS[_par][0:1, :], axis=AX.X),
                waits=[(sPE, bm)], inc=sDVE)
            ev, sq_, st_, _, _ = kq["evt"][i]
            kq["evt"][i] = (ev, sq_, st_, bm, sDVE.n)

        def kq_chunk(eh, wb_i, ecl, slice_done, dest_dram, bounds_slot,
                     extra_first=(), extra_evac=()):
            i = kq["i"]
            kq["i"] += 1
            par = i % 2
            w0 = [slice_done, (sDVE, xh1_done)] + list(extra_first)
            if i >= 2:
                w0.append((sDVE, kq["evt"][i - 2][0]))
            for dc in range(NDC):
                for n in range(2):
                    p.op("tensor", lambda t, _dc=dc, _n=n, _w=wb_i, _ecl=ecl,
                         _par=par, _s=(dc == 0), _e=(dc == NDC - 1): t.matmul(
                        PSL[_par][:, _n * 512:(_n + 1) * 512],
                        WB[_w % 3][:, _dc * SLC + _ecl * 128:_dc * SLC + (_ecl + 1) * 128],
                        B[:, _dc * T + _n * 512:_dc * T + (_n + 1) * 512],
                        start=_s, stop=_e),
                        waits=(w0 if (dc == 0 and n == 0) else ()),
                        inc=(sPE if (dc == NDC - 1 and n == 1) else None))
            mm = sPE.n
            kq_bounds_flush()
            kst_w = [(sPE, mm)] + list(extra_evac)
            if i >= 2:
                kst_w += [(sACT, kq["evt"][i - 2][1]), kq["evt"][i - 2][2]]
            for n in range(2):
                p.op("vector", lambda v, _n=n, _par=par: v.tensor_copy(
                    KST[_par][:, _n * 512:(_n + 1) * 512],
                    PSL[_par][:, _n * 512:(_n + 1) * 512]),
                    waits=(kst_w if n == 0 else ()), inc=(sDVE if n == 1 else None))
            ev = sDVE.n
            bw = [(sDVE, ev)]
            if i >= 2:
                bw.append((sPE, kq["evt"][i - 2][3]))
            p.op("scalar", lambda s, _par=par: s.activation(
                SG[_par][:, :], KST[_par][:, :], AF.Square), waits=bw, inc=sACT)
            sq = sACT.n
            st_ = xdma(lambda s, _par=par, _eh=eh, _dst=dest_dram: s.dma_start(
                out=_dst.ap()[_eh * 128:(_eh + 1) * 128, :], in_=KST[_par][:, :]),
                waits=[(sDVE, ev)])
            kq["evt"][i] = (ev, sq, st_, sPE.n, sDVE.n)
            kq["pend"] = (i, par, sq, bounds_slot)
            return mm

        for si in range(8):
            wb_i, sd = wslice(wqkvT, 2048 + si * SLC)
            mm_last = 0
            for ecl in range(2):
                mm_last = kq_chunk(si * 2 + ecl, wb_i, ecl, sd, kout, SM_KMX)
            wslice_consumed((sPE, mm_last))
        kq_bounds_flush()
        k_stage_all = xall()
        k_sq_end = kq["evt"][15][1]
        cc_k = sCC.inc()

        VPS = [PSO[0], PSO[1], PSS[0], PSS[1]]
        v_i = 0
        v_evt = {}
        v_stage = {}
        v_first_w = k_stage_all + [(sACT, k_sq_end), (sDVE, sDVE.n)]
        for si in range(8):
            wb_i, sd = wslice(wqkvT, 4096 + si * SLC)
            mm_last = 0
            for tc in range(8):
                i = v_i
                v_i += 1
                r4 = i % 4
                kcol = r4 * SLC
                w0 = [sd]
                if i >= 4:
                    w0.append((sDVE, v_evt[i - 4]))
                elif i == 0:
                    w0 += v_first_w
                for dc in range(NDC):
                    p.op("tensor", lambda t, _dc=dc, _tc=tc, _r=r4, _w=wb_i,
                         _s=(dc == 0), _e=(dc == NDC - 1): t.matmul(
                        VPS[_r][:, 0:SLC],
                        B[:, _dc * T + _tc * 128:_dc * T + (_tc + 1) * 128],
                        WB[_w % 3][:, _dc * SLC:(_dc + 1) * SLC],
                        start=_s, stop=_e),
                        waits=(w0 if dc == 0 else ()),
                        inc=(sPE if dc == NDC - 1 else None))
                mm_last = sPE.n
                evw = [(sPE, mm_last)]
                if i >= 8:
                    evw.append(v_stage[(i - 8) // 4])
                p.op("vector", lambda v, _i=i: v.tensor_copy(
                    KST[(_i // 4) % 2][:, (_i % 4) * SLC:(_i % 4) * SLC + SLC],
                    VPS[_i % 4][:, 0:SLC]),
                    waits=mw(evw), inc=sDVE)
                v_evt[i] = sDVE.n
                if i % 4 == 3:
                    v_stage[i // 4] = xdma(
                        lambda s, _i=i, _tc=tc, _si=si: s.dma_start(
                            out=vout.ap()[(_tc - 3) * 128:(_tc + 1) * 128,
                                          _si * SLC:(_si + 1) * SLC].rearrange(
                                "(c p) v -> p c v", p=128),
                            in_=KST[(_i // 4) % 2][:, :].rearrange(
                                "p (c v) -> p c v", v=SLC)),
                        waits=[(sDVE, v_evt[i])])
            wslice_consumed((sPE, mm_last))
        v_stage_all = xall()
        v_evt_end = sDVE.n
        if DEBUG:
            xdma(lambda s: s.dma_start(out=taps["tap_vout"], in_=vout.ap()[:, :]),
                 waits=[(sCC, cc_v)])
            xdma(lambda s: s.dma_start(out=taps["tap_kin"], in_=kin.ap()[:, :]))
            xdma(lambda s: s.dma_start(out=taps["tap_vin"], in_=vin.ap()[:, :]))

        head_dma = {}
        head_vdma = {}
        head_pe_done = {}

        def head_loads(h, with_qs=True):
            ph = h % 2
            base = ph * 8192
            waits = [(sCC, cc_k)]
            if h >= 2:
                waits.append((sPE, head_pe_done[h - 2]))
            pairs = []
            for r in range(4):
                pairs.append(hdma(lambda s, _h=h, _r=r, _b=base: s.dma_start(
                    out=D[:, _b + _r * T:_b + (_r + 1) * T],
                    in_=kin.ap()[_r * 2048 + _h * 128:_r * 2048 + (_h + 1) * 128, :]),
                    waits=(waits if r == 0 else ())))
            if with_qs:
                pairs.append(qs_load(h))
            vpairs = []
            vw = [(sCC, cc_v)]
            for r in range(4):
                vpairs.append(hdma(lambda s, _h=h, _r=r, _b=base: s.dma_start(
                    out=D[:, _b + 4096 + _r * T:_b + 4096 + (_r + 1) * T].rearrange(
                        "p (c v) -> p c v", v=128),
                    in_=vin.ap()[_r * T:(_r + 1) * T, _h * 128:(_h + 1) * 128].rearrange(
                        "(c p) v -> p c v", p=128)),
                    waits=(vw if r == 0 else ())))
            head_dma[h] = mw(pairs)
            head_vdma[h] = mw(vpairs)

        def qs_load(h):
            qw = [kq["evt"][16 + h][2]]
            if h >= 2:
                qw.append((sPE, head_pe_done[h - 2]))
            return hdma(lambda s, _h=h, _ph=h % 2: s.dma_start(
                out=QS[_ph][:, :], in_=qout.ap()[_h * 128:(_h + 1) * 128, :]),
                waits=qw)


        q_first_evac = [True]
        for si in range(8):
            wb_i, sd = wslice(wqkvT, 0 + si * SLC)
            mm_last = 0
            for ecl in range(2):
                first = (si == 0 and ecl == 0)
                mm_last = kq_chunk(si * 2 + ecl, wb_i, ecl, sd, qout, SM_QMX,
                                   extra_first=([(sDVE, v_evt_end)] if first else ()),
                                   extra_evac=(mw(v_stage_all) if si == 0 else ()))
            wslice_consumed((sPE, mm_last))
        kq_bounds_flush()
        qproj_pe_done = sPE.n
        sCC.n -= 1
        p.op("gpsimd", lambda g: g.collective_compute(
            "AllGather", ALU.bypass, replica_groups=[[0, 1, 2, 3], [4, 5, 6, 7]],
            ins=[kout.ap().opt()], outs=[kin.ap().opt()]),
            waits=mw(k_stage_all + v_stage_all), inc=sCC)
        assert sCC.n == cc_k
        p.op("gpsimd", lambda g: g.collective_compute(
            "AllGather", ALU.bypass, replica_groups=[[0, 1, 2, 3], [4, 5, 6, 7]],
            ins=[vout.ap().opt()], outs=[vin.ap().opt()]),
            inc=sCC)
        cc_v = sCC.n
        if DEBUG:
            xdma(lambda s: s.dma_start(out=taps["tap_q"], in_=qout.ap()[:, :]),
                 waits=xall())

        p.op("vector", lambda v: v.reduce_max(SM_SC[:, 0:1], SM_KMX[:, :], axis=AX.X),
             waits=[(sDVE, sDVE.n)], inc=sDVE)
        p.op("vector", lambda v: v.reduce_max(SM_SC[:, 1:2], SM_QMX[:, :], axis=AX.X),
             waits=[(sDVE, sDVE.n)], inc=sDVE)
        p.op("vector", lambda v: v.tensor_mul(SM_SC[:, 2:3], SM_SC[:, 0:1], SM_SC[:, 1:2]),
             waits=[(sDVE, sDVE.n)], inc=sDVE)
        p.op("scalar", lambda s: s.activation(SM_SC[:, 3:4], SM_SC[:, 2:3], AF.Ln),
             waits=[(sDVE, sDVE.n)], inc=sACT)
        p.op("scalar", lambda s: s.activation(SM_SC[:, 4:5], SM_SC[:, 3:4], AF.Exp,
                                              scale=0.5, bias=CST[0:1, 1:2]),
             waits=[(sACT, sACT.n)], inc=sACT)
        p.op("vector", lambda v: v.tensor_scalar(SM_SC[:, 5:6], SM_SC[:, 4:5],
                                                 1.01, 1.0, ALU.mult, ALU.add),
             waits=[(sACT, sACT.n)], inc=sDVE)
        p.op("tensor", lambda t: t.matmul(PSS[0][:, 0:1], ONE_R[:, :], SM_SC[:, 5:6],
                                          start=True, stop=True),
             waits=[(sDVE, sDVE.n), (sPE, qproj_pe_done)], inc=sPE)
        p.op("vector", lambda v: v.tensor_scalar_mul(BIAS[:, :], PSS[0][:, 0:1], -1.0),
             waits=[(sPE, sPE.n)], inc=sDVE)
        bias_ready = sDVE.n
        barrier()

        np_done = {0: None, 1: None}
        head_loads(0)
        head_loads(1)

        exp_done = {}
        pvpair_done = {}
        pend_ep = [None]

        def emit_epilogue():
            if pend_ep[0] is None:
                return
            h_, n_, npi, acc = pend_ep[0]
            pend_ep[0] = None
            p.op("vector", lambda v, _np=npi: v.tensor_copy(
                SM_SUM[_np][:, :], PSS[_np][0:1, :]),
                waits=[(sPE, acc)], inc=sDVE)
            p.op("vector", lambda v, _np=npi: v.reciprocal(
                SM_REC[_np][:, :], SM_SUM[_np][:, :]),
                waits=[(sDVE, sDVE.n)], inc=sDVE)
            rec_ = sDVE.n
            p.op("tensor", lambda t, _np=npi: t.matmul(
                PSS[_np][:, :], ONE_R[:, :], SM_REC[_np][:, :],
                start=True, stop=True),
                waits=[(sDVE, rec_)], inc=sPE)
            p.op("vector", lambda v, _np=npi: v.tensor_copy(RB[_np][:, :], PSS[_np][:, :]),
                 waits=[(sPE, sPE.n)], inc=sDVE)
            p.op("vector", lambda v, _np=npi, _h=h_, _n=n_: v.tensor_tensor(
                B[:, _h * T + _n * 512:_h * T + (_n + 1) * 512],
                PSO[_np][:, :], RB[_np][:, :], ALU.mult),
                waits=[(sDVE, sDVE.n)], inc=sDVE)
            np_done[npi] = sDVE.n

        NPAIR = NKC // 2
        for h in range(H):
            ph = h % 2
            base = ph * 8192
            for n in range(2):
                npi = (2 * h + n) % 2
                gj0 = (2 * h + n) * NPAIR
                qk_pe = {}

                def emit_qk_pair(j):
                    gj = gj0 + j
                    qk_w = []
                    if j == 0:
                        qk_w.extend(head_dma[h])
                        if h == 0 and n == 0:
                            qk_w.append((sDVE, bias_ready))
                    if gj >= 2:
                        qk_w.append((sACT, exp_done[gj - 2]))
                    for half in range(2):
                        kc = 2 * j + half
                        p.op("tensor", lambda t, _kc=kc, _b=base, _ph=ph, _n=n,
                             _g=gj: t.matmul(
                            PSL[_g % 2][:, (_kc % 2) * 512:(_kc % 2) * 512 + 512],
                            D[:, _b + _kc * 128:_b + (_kc + 1) * 128],
                            QS[_ph][:, _n * 512:(_n + 1) * 512],
                            start=True, stop=True),
                            waits=(qk_w if half == 0 else ()),
                            inc=(sPE if half == 1 else None))
                    qk_pe[gj] = sPE.n

                def emit_exp(j):
                    gj = gj0 + j
                    ew = [(sPE, qk_pe[gj]), (sPE, pvpair_done.get(gj - 2, 0))]
                    p.op("scalar", lambda s, _g=gj: s.activation(
                        SG[_g % 2][:, :], PSL[_g % 2][:, :], AF.Exp,
                        bias=BIAS[:, :], scale=SCL),
                        waits=mw(ew), inc=sACT)
                    exp_done[gj] = sACT.n

                def emit_sum_pv(j):
                    gj = gj0 + j
                    sw = [(sACT, exp_done[gj])]
                    if j == 0 and np_done[npi] is not None:
                        sw.append((sDVE, np_done[npi]))
                    if j == 0 and n == 0:
                        sw.extend(head_vdma[h])
                    for half in range(2):
                        p.op("tensor", lambda t, _np=npi, _g=gj, _h2=half,
                             _s=(j == 0 and half == 0),
                             _e=(j == NPAIR - 1 and half == 1): t.matmul(
                            PSS[_np][0:1, :], ONE_B[:, :],
                            SG[_g % 2][:, _h2 * 512:(_h2 + 1) * 512],
                            start=_s, stop=_e),
                            waits=(sw if half == 0 else ()))
                    for half in range(2):
                        kc = 2 * j + half
                        p.op("tensor", lambda t, _kc=kc, _b=base, _np=npi, _g=gj,
                             _h2=half, _s=(j == 0 and half == 0),
                             _e=(j == NPAIR - 1 and half == 1): t.matmul(
                            PSO[_np][:, :],
                            D[:, _b + 4096 + _kc * 128:_b + 4096 + (_kc + 1) * 128],
                            SG[_g % 2][:, _h2 * 512:(_h2 + 1) * 512],
                            start=_s, stop=_e),
                            inc=(sPE if half == 1 else None))
                    pvpair_done[gj] = sPE.n

                emit_qk_pair(0)
                emit_exp(0)
                emit_qk_pair(1)
                emit_exp(1)
                for j in range(2, NPAIR):
                    emit_qk_pair(j)
                    if j == 6:
                        emit_epilogue()
                    emit_sum_pv(j - 2)
                    emit_exp(j)
                emit_sum_pv(NPAIR - 2)
                emit_sum_pv(NPAIR - 1)
                pend_ep[0] = (h, n, npi, pvpair_done[gj0 + NPAIR - 1])
            head_pe_done[h] = sPE.n
            if h + 2 < H:
                head_loads(h + 2)
        emit_epilogue()
        attn_done = sDVE.n
        barrier()

        dbg_x2_gate = None
        if DEBUG:
            dbg_x2_gate = xdma(lambda s: s.dma_start(out=taps["tap_attn"], in_=B[:, :]),
                               waits=[(sDVE, attn_done)])

        y_evt = {}
        ln2_pend = [None]
        ln2_pe = {}

        SQB = [SG[0], SG[1], KST[0], KST[1]]
        XB = [D[:, 8192 + 1024 * i:8192 + 1024 * (i + 1)] for i in range(4)]

        def stat_flush(pend, pe_map):
            if pend[0] is None:
                return
            dc, sq = pend[0]
            pend[0] = None
            st, sp = (dc == 0), (dc == NDC - 1)
            for n in range(2):
                p.op("tensor", lambda t, _dc=dc, _n=n, _s=st, _e=sp: t.matmul(
                    STAT[_n], ONE_B[:, :],
                    XB[_dc % 4][:, _n * 512:(_n + 1) * 512],
                    start=_s, stop=_e),
                    waits=([(sACT, sq)] if n == 0 else ()))
            for n in range(2):
                p.op("tensor", lambda t, _dc=dc, _n=n, _s=st, _e=sp: t.matmul(
                    STAT[2 + _n], ONE_B[:, :],
                    SQB[_dc % 4][:, _n * 512:(_n + 1) * 512], start=_s, stop=_e),
                    inc=(sPE if n == 1 else None))
            pe_map[dc] = sPE.n

        for si in range(8):
            wb_i, sd = wslice(woutT, si * SLC)
            mm_last = 0
            for dcl in range(2):
                dc = si * 2 + dcl
                par = dc % 2
                w0 = [sd, (sDVE, attn_done)]
                if dc >= 2:
                    w0.append((sDVE, y_evt[dc - 2]))
                for ec in range(NDC):
                    for n in range(2):
                        p.op("tensor", lambda t, _ec=ec, _n=n, _dcl=dcl, _w=wb_i,
                             _par=par, _s=(ec == 0), _e=(ec == NDC - 1): t.matmul(
                            PSL[_par][:, _n * 512:(_n + 1) * 512],
                            WB[_w % 3][:, _ec * SLC + _dcl * 128:_ec * SLC + (_dcl + 1) * 128],
                            B[:, _ec * T + _n * 512:_ec * T + (_n + 1) * 512],
                            start=_s, stop=_e),
                            waits=(w0 if (ec == 0 and n == 0) else ()),
                            inc=(sPE if (ec == NDC - 1 and n == 1) else None))
                mm_last = sPE.n
                stat_flush(ln2_pend, ln2_pe)
                for n in range(2):
                    p.op("vector", lambda v, _dc=dc, _n=n, _par=par: v.tensor_copy(
                        A[:, _dc * T + _n * 512:_dc * T + (_n + 1) * 512],
                        PSL[_par][:, _n * 512:(_n + 1) * 512]),
                        waits=([(sPE, mm_last)] if n == 0 else ()),
                        inc=(sDVE if n == 1 else None))
                y_evt[dc] = sDVE.n
                cw = [(sDVE, y_evt[dc])]
                if dc >= 4:
                    cw.append((sPE, ln2_pe[dc - 4]))
                p.op("scalar", lambda s, _dc=dc: s.activation(
                    SQB[_dc % 4][:, :], A[:, _dc * T:(_dc + 1) * T], AF.Square),
                    waits=cw, inc=sACT)
                p.op("scalar", lambda s, _dc=dc: s.copy(
                    XB[_dc % 4][:, :], A[:, _dc * T:(_dc + 1) * T]),
                    inc=sACT)
                ln2_pend[0] = (dc, sACT.n)
            wslice_consumed((sPE, mm_last))
        stat_flush(ln2_pend, ln2_pe)
        barrier()
        if DEBUG:
            dbg_y_gate = xdma(lambda s: s.dma_start(out=taps["tap_y"], in_=A[:, :]),
                              waits=[(sDVE, sDVE.n)])
        bc2 = ln_junction()

        PT2 = [D[:, 4096 + 2048 * i:4096 + 2048 * (i + 1)].bitcast(F32)
               for i in range(2)]
        x2_evt = {}
        last_x2 = {"vector": None, "gpsimd": None}
        ln3_pend = [None]
        ln3_pe = {}
        for dc in range(NDC):
            pooled = (dc % 3 == 2)
            eng, sem = ("gpsimd", sPOOL) if pooled else ("vector", sDVE)
            s1 = PT[0] if pooled else SCR[1][:, :]
            s2 = PT[1] if pooled else SCR[2][:, :]
            s0 = PT2[0] if pooled else SCR[0][:, :]
            waits = [(sDVE, bc2)]
            if last_x2[eng] is not None:
                waits = waits + [last_x2[eng]]
            if DEBUG and dc == 0:
                waits = waits + [dbg_y_gate, dbg_x2_gate]
            ld = xdma(lambda s, _d=dc, _b=s0: s.dma_start(
                out=_b, in_=xc[_d * 128:(_d + 1) * 128, :]),
                waits=waits)
            w = [ld, (sDVE, bc2)]
            if last_x2[eng] is not None:
                w.append(last_x2[eng])
            p.op(eng, lambda v, _d=dc, _s=s1: v.tensor_tensor(
                _s, A[:, _d * T:(_d + 1) * T], XIO[0][:, :], ALU.subtract),
                waits=w, inc=sem)
            t1 = sem.n
            p.op(eng, lambda v, _a=s1, _b=s2: v.tensor_tensor(
                _b, _a, XIO[1][:, :], ALU.mult),
                waits=[(sem, t1)], inc=sem)
            t2 = sem.n
            p.op(eng, lambda v, _d=dc, _b=s2, _x=s0: v.tensor_tensor(
                A[:, _d * T:(_d + 1) * T], _x, _b, ALU.add),
                waits=[(sem, t2)], inc=sem)
            x2v = (sem, sem.n)
            last_x2[eng] = x2v
            xdma(lambda s, _d=dc: s.dma_start(
                out=x2d.ap()[_d * 128:(_d + 1) * 128, :], in_=A[:, _d * T:(_d + 1) * T]),
                waits=[x2v])
            cw = [x2v]
            if dc >= 4:
                cw.append((sPE, ln3_pe.get(dc - 4, 0)))
            p.op("scalar", lambda s, _dc=dc: s.activation(
                SQB[_dc % 4][:, :], A[:, _dc * T:(_dc + 1) * T], AF.Square),
                waits=cw, inc=sACT)
            p.op("scalar", lambda s, _dc=dc: s.copy(
                XB[_dc % 4][:, :], A[:, _dc * T:(_dc + 1) * T]),
                inc=sACT)
            sq = sACT.n
            stat_flush(ln3_pend, ln3_pe)
            ln3_pend[0] = (dc, sq)
            x2_evt[dc] = x2v
        stat_flush(ln3_pend, ln3_pe)
        barrier()
        bc3 = ln_junction()
        xh3_evt = {}
        last_x3 = {"vector": None, "gpsimd": None}
        for dc in range(NDC):
            pooled = (dc % 3 == 2)
            eng, sem = ("gpsimd", sPOOL) if pooled else ("vector", sDVE)
            scr = PT[0] if pooled else SCR[1][:, :]
            w = [(sDVE, bc3)]
            if last_x3[eng] is not None:
                w.append(last_x3[eng])
            p.op(eng, lambda v, _d=dc, _s=scr: v.tensor_tensor(
                _s, A[:, _d * T:(_d + 1) * T], XIO[0][:, :], ALU.subtract),
                waits=w, inc=sem)
            t1 = sem.n
            p.op(eng, lambda v, _d=dc, _s=scr: v.tensor_tensor(
                B[:, _d * T:(_d + 1) * T], _s, XIO[1][:, :], ALU.mult),
                waits=[(sem, t1)], inc=sem)
            xh3_evt[dc] = (sem, sem.n)
            last_x3[eng] = xh3_evt[dc]
        xh3_done = sDVE.n
        xh3_pool = sPOOL.n
        barrier()
        if DEBUG:
            xdma(lambda s: s.dma_start(out=taps["tap_x2"], in_=x2d.ap()[:, :]),
                 waits=xall())
            xdma(lambda s: s.dma_start(out=taps["tap_xh3"], in_=B[:, :]),
                 waits=[(sDVE, xh3_done)])

        wd_state["g0"] = (sPE, sPE.n)
        mult_hist = {}
        down_done_pe = 0
        dacc = {}
        for g in range(8):
            for si in range(4):
                wbg_i, sdg = wslice(wgT, g * 1024 + si * SLC)
                wbu_i, sdu = wslice(wuT, g * 1024 + si * SLC)
                gate_last = up_last = 0
                for hcl in range(2):
                    ahc = g * 8 + si * 2 + hcl
                    gw = [sdg, (sDVE, xh3_done), (sACT, sACT.n)]
                    if g > 0 and si == 0 and hcl == 0:
                        gw.append((sPE, down_done_pe))
                    for dc in range(NDC):
                        for n in range(2):
                            p.op("tensor", lambda t, _dc=dc, _n=n, _hcl=hcl, _w=wbg_i,
                                 _s=(dc == 0), _e=(dc == NDC - 1): t.matmul(
                                PSL[0][:, _n * 512:(_n + 1) * 512],
                                WB[_w % 3][:, _dc * SLC + _hcl * 128:_dc * SLC + (_hcl + 1) * 128],
                                B[:, _dc * T + _n * 512:_dc * T + (_n + 1) * 512],
                                start=_s, stop=_e),
                                waits=(gw if (dc == 0 and n == 0) else ()),
                                inc=(sPE if (dc == NDC - 1 and n == 1) else None))
                    gate_last = sPE.n
                    sil_w = [(sPE, gate_last)]
                    if ahc >= 2:
                        sil_w.append((sDVE, mult_hist[ahc - 2]))
                    p.op("scalar", lambda s, _a=ahc: s.activation(
                        SG[_a % 2][:, :], PSL[0][:, :], AF.Silu),
                        waits=sil_w, inc=sACT)
                    sil = sACT.n
                    uw = [sdu, (sACT, sil)]
                    if ahc >= 1:
                        uw.append((sDVE, mult_hist[ahc - 1]))
                    for dc in range(NDC):
                        for n in range(2):
                            p.op("tensor", lambda t, _dc=dc, _n=n, _hcl=hcl, _w=wbu_i,
                                 _s=(dc == 0), _e=(dc == NDC - 1): t.matmul(
                                PSL[1][:, _n * 512:(_n + 1) * 512],
                                WB[_w % 3][:, _dc * SLC + _hcl * 128:_dc * SLC + (_hcl + 1) * 128],
                                B[:, _dc * T + _n * 512:_dc * T + (_n + 1) * 512],
                                start=_s, stop=_e),
                                waits=(uw if (dc == 0 and n == 0) else ()),
                                inc=(sPE if (dc == NDC - 1 and n == 1) else None))
                    up_last = sPE.n
                    for n in range(2):
                        p.op("vector", lambda v, _n=n, _a=ahc: v.tensor_tensor(
                            D[:, (_a % 8) * T + _n * 512:(_a % 8) * T + (_n + 1) * 512],
                            SG[_a % 2][:, _n * 512:(_n + 1) * 512],
                            PSL[1][:, _n * 512:(_n + 1) * 512], ALU.mult),
                            waits=([(sPE, up_last)] if n == 0 else ()),
                            inc=(sDVE if n == 1 else None))
                    mult_hist[ahc] = sDVE.n
                wslice_consumed((sPE, gate_last))
                wslice_consumed((sPE, up_last))
            gu_done = sDVE.n
            for dc in range(NDC):
                gdc = g * NDC + dc
                wd_i, sdd = wdslice(g, dc)
                par = dc % 2
                dw = [sdd, (sDVE, gu_done)]
                if gdc >= 2:
                    dw.append((sDVE, dacc[gdc - 2]))
                for hcl in range(8):
                    for n in range(2):
                        p.op("tensor", lambda t, _h=hcl, _n=n, _w=wd_i, _par=par,
                             _s=(hcl == 0), _e=(hcl == 7): t.matmul(
                            (PSO[_par] if _n == 0 else PSS[_par])[:, :],
                            D[:, 8192 + (_w % 4) * 1024 + _h * 128:
                               8192 + (_w % 4) * 1024 + (_h + 1) * 128],
                            D[:, _h * T + _n * 512:_h * T + (_n + 1) * 512],
                            start=_s, stop=_e),
                            waits=(dw if (hcl == 0 and n == 0) else ()),
                            inc=(sPE if (hcl == 7 and n == 1) else None))
                dmm = sPE.n
                wd_state["g"].append((sPE, dmm))
                for n in range(2):
                    if g == 0:
                        p.op("vector", lambda v, _dc=dc, _n=n, _par=par: v.tensor_copy(
                            A[:, _dc * T + _n * 512:_dc * T + (_n + 1) * 512],
                            (PSO[_par] if _n == 0 else PSS[_par])[:, :]),
                            waits=([(sPE, dmm)] if n == 0 else ()),
                            inc=(sDVE if n == 1 else None))
                    else:
                        p.op("vector", lambda v, _dc=dc, _n=n, _par=par: v.tensor_tensor(
                            A[:, _dc * T + _n * 512:_dc * T + (_n + 1) * 512],
                            A[:, _dc * T + _n * 512:_dc * T + (_n + 1) * 512],
                            (PSO[_par] if _n == 0 else PSS[_par])[:, :], ALU.add),
                            waits=([(sPE, dmm)] if n == 0 else ()),
                            inc=(sDVE if n == 1 else None))
                dacc[gdc] = sDVE.n
            down_done_pe = sPE.n
        barrier()

        if DEBUG:
            xdma(lambda s: s.dma_start(out=taps["tap_y2"], in_=A[:, :]),
                 waits=[(sDVE, sDVE.n)])
        ln4_pe = {}
        for dc in range(NDC):
            cw = []
            if dc >= 4:
                cw.append((sPE, ln4_pe[dc - 4]))
            p.op("scalar", lambda s, _dc=dc: s.activation(
                SQB[_dc % 4][:, :], A[:, _dc * T:(_dc + 1) * T], AF.Square),
                waits=cw, inc=sACT)
            p.op("scalar", lambda s, _dc=dc: s.copy(
                XB[_dc % 4][:, :], A[:, _dc * T:(_dc + 1) * T]),
                inc=sACT)
            sq = sACT.n
            st, sp = (dc == 0), (dc == NDC - 1)
            for n in range(2):
                p.op("tensor", lambda t, _dc=dc, _n=n, _s=st, _e=sp: t.matmul(
                    STAT[_n], ONE_B[:, :],
                    XB[_dc % 4][:, _n * 512:(_n + 1) * 512],
                    start=_s, stop=_e),
                    waits=([(sACT, sq)] if n == 0 else ()))
            for n in range(2):
                p.op("tensor", lambda t, _dc=dc, _n=n, _s=st, _e=sp: t.matmul(
                    STAT[2 + _n], ONE_B[:, :],
                    SQB[_dc % 4][:, _n * 512:(_n + 1) * 512], start=_s, stop=_e),
                    inc=(sPE if n == 1 else None))
            ln4_pe[dc] = sPE.n
        barrier()
        bc4 = ln_junction()
        out_dma = {}
        acc_dma = {}
        prev_t1 = [None]
        for dc in range(NDC):
            sl = 0 if dc % 2 == 0 else 2
            w = [(sDVE, bc4)]
            if prev_t1[0] is not None:
                w.append((sDVE, prev_t1[0]))
            p.op("vector", lambda v, _d=dc: v.tensor_tensor(
                SCR[1][:, :], A[:, _d * T:(_d + 1) * T], XIO[0][:, :], ALU.subtract),
                waits=w, inc=sDVE)
            t1 = sDVE.n
            w2 = [(sDVE, t1)] + ([out_dma[dc - 2]] if dc >= 2 else [])
            p.op("vector", lambda v, _sl=sl: v.tensor_tensor(
                SCR[_sl][:, :], SCR[1][:, :], XIO[1][:, :], ALU.mult),
                waits=w2, inc=sDVE)
            t2 = sDVE.n
            prev_t1[0] = t2
            acc_dma[dc] = wdma(lambda g, _d=dc, _sl=sl: g.dma_start(
                out=SCR[_sl][:, :], in_=x2d.ap()[_d * 128:(_d + 1) * 128, :],
                accum_op=ALU.add),
                waits=[(sDVE, t2)])
            out_dma[dc] = xdma(lambda s, _d=dc, _sl=sl: s.dma_start(
                out=outc[_d * 128:(_d + 1) * 128, :], in_=SCR[_sl][:, :]),
                waits=[acc_dma[dc]])

        p.wait("sync", xall() + [(s, s.n) for s in HL] + [(s, s.n) for s in WL])
        p.wait("gpsimd", xall())

        with nc.Block() as block:
            @block.tensor
            def _(eng):
                for f in p.ops["tensor"]:
                    f(eng)

            @block.scalar
            def _(eng):
                for f in p.ops["scalar"]:
                    f(eng)

            @block.vector
            def _(eng):
                for f in p.ops["vector"]:
                    f(eng)

            @block.gpsimd
            def _(eng):
                for f in p.ops["gpsimd"]:
                    f(eng)

            @block.sync
            def _(eng):
                for f in p.ops["sync"]:
                    f(eng)

    return nc


_NC_CACHE = None


def kernel(L0_attn_x, L0_attn_Wqkv, L0_attn_Wout, L0_ffn_w_gate, L0_ffn_w_up,
           L0_ffn_w_down):
    global _NC_CACHE, LAST_TAPS
    x = np.asarray(L0_attn_x, np.float32)
    WqkvT = np.ascontiguousarray(np.asarray(L0_attn_Wqkv, np.float32).T)
    WoutT = np.ascontiguousarray(np.asarray(L0_attn_Wout, np.float32).T)
    WgT = np.ascontiguousarray(np.asarray(L0_ffn_w_gate, np.float32).T)
    WuT = np.ascontiguousarray(np.asarray(L0_ffn_w_up, np.float32).T)
    WdT = np.ascontiguousarray(np.asarray(L0_ffn_w_down, np.float32).T)

    if DEBUG:
        nc = build_nc()
    else:
        global _NC_CACHE
        if _NC_CACHE is None:
            _NC_CACHE = build_nc()
        nc = _NC_CACHE

    in_maps = []
    for c in range(8):
        b, s = c // 4, c % 4
        xcv = np.ascontiguousarray(x[b, s * T:(s + 1) * T, :].T)
        in_maps.append({"xc": xcv, "wqkvT": WqkvT, "woutT": WoutT,
                        "wgT": WgT, "wuT": WuT, "wdT": WdT})
    res = run_bass_kernel_spmd(nc, in_maps, core_ids=list(range(8)))
    if DEBUG:
        LAST_TAPS = res.results
    out = np.empty((2, S, DM), np.float32)
    for c in range(8):
        b, s = c // 4, c % 4
        out[b, s * T:(s + 1) * T, :] = res.results[c]["outc"].T
    return out


# version 20
# speedup vs baseline: 1.0801x; 1.0801x over previous
import numpy as np
import concourse.bass as bass
import concourse.mybir as mybir
from concourse.bass_utils import run_bass_kernel_spmd
from contextlib import ExitStack

F32 = mybir.dt.float32
BF16 = mybir.dt.bfloat16
AF = mybir.ActivationFunctionType
ALU = mybir.AluOpType
AX = mybir.AxisListType

DM = 2048
NDC = 16
T = 1024
S = 4096
NKC = 32
H = 16
HID = 8192
EQKV = 6144
EPS = 1e-5
SCL = float(1.0 / np.sqrt(128.0))
SLC = 256

ENGINES = ("tensor", "scalar", "vector", "gpsimd", "sync")

DEBUG = False
LAST_TAPS = None


class Sem:
    __slots__ = ("name", "n", "h")

    def __init__(self, name):
        self.name = name
        self.n = 0
        self.h = None

    def inc(self, k=1):
        self.n += k
        return self.n


class Prog:
    def __init__(self):
        self.ops = {e: [] for e in ENGINES}

    def op(self, eng, build, waits=(), inc=None, inc_n=1):
        waits = [(s, v) for (s, v) in waits]
        if inc is not None:
            inc.inc(inc_n)

        def emit(engine, _b=build, _w=waits, _i=inc, _n=inc_n):
            for (s, v) in _w:
                engine.wait_ge(s.h, v)
            ins = _b(engine)
            if _i is not None:
                ins.then_inc(_i.h, _n)
        self.ops[eng].append(emit)

    def wait(self, eng, waits):
        waits = [(s, v) for (s, v) in waits]

        def emit(engine, _w=waits):
            for (s, v) in _w:
                engine.wait_ge(s.h, v)
        self.ops[eng].append(emit)


def build_nc():
    nc = bass.Bass("TRN2", target_bir_lowering=False)

    xc = nc.dram_tensor("xc", [DM, T], F32, kind="ExternalInput").ap()
    wqkvT = nc.dram_tensor("wqkvT", [DM, EQKV], F32, kind="ExternalInput").ap()
    woutT = nc.dram_tensor("woutT", [DM, DM], F32, kind="ExternalInput").ap()
    wgT = nc.dram_tensor("wgT", [DM, HID], F32, kind="ExternalInput").ap()
    wuT = nc.dram_tensor("wuT", [DM, HID], F32, kind="ExternalInput").ap()
    wdT = nc.dram_tensor("wdT", [HID, DM], F32, kind="ExternalInput").ap()
    outc = nc.dram_tensor("outc", [DM, T], F32, kind="ExternalOutput").ap()

    taps = {}
    if DEBUG:
        def mktap(name, shape, dt):
            taps[name] = nc.dram_tensor(name, shape, dt, kind="ExternalOutput").ap()
        mktap("tap_xh1", [128, NDC * T], BF16)
        mktap("tap_vout", [T, DM], BF16)
        mktap("tap_kin", [4 * DM, T], BF16)
        mktap("tap_vin", [4 * T, DM], BF16)
        mktap("tap_q", [DM, T], BF16)
        mktap("tap_attn", [128, NDC * T], BF16)
        mktap("tap_y", [128, NDC * T], F32)
        mktap("tap_x2", [DM, T], F32)
        mktap("tap_xh3", [128, NDC * T], BF16)
        mktap("tap_y2", [128, NDC * T], F32)

    kout = nc.dram_tensor("kout", [DM, T], BF16, kind="Internal")
    qout = nc.dram_tensor("qout", [DM, T], BF16, kind="Internal")
    vout = nc.dram_tensor("vout", [T, DM], BF16, kind="Internal")
    kin = nc.dram_tensor("kin", [4 * DM, T], BF16, kind="Internal")
    vin = nc.dram_tensor("vin", [4 * T, DM], BF16, kind="Internal")
    x2d = nc.dram_tensor("x2d", [DM, T], F32, kind="Internal")

    p = Prog()
    stk = ExitStack()
    sbt = lambda name, shape, dt: stk.enter_context(nc.sbuf_tensor(name, shape, dt))

    with stk:
        A = sbt("A", [128, NDC * T], F32)
        B = sbt("B", [128, NDC * T], BF16)
        D = sbt("D", [128, 2 * 8192], BF16)
        WB = [sbt(f"WB{i}", [128, NDC * SLC], BF16) for i in range(3)]
        QS = [sbt(f"QS{i}", [128, T], BF16) for i in range(2)]
        SG = [sbt(f"SG{i}", [128, T], BF16) for i in range(2)]
        XIO = [sbt(f"XIO{i}", [128, T], F32) for i in range(2)]
        SCR = [sbt(f"SCR{i}", [128, T], F32) for i in range(3)]
        KST = [sbt(f"KST{i}", [128, T], BF16) for i in range(2)]
        RB = [sbt(f"RB{i}", [128, 512], F32) for i in range(2)]
        SM = sbt("SM", [1, 4096], F32)
        ONE_F = sbt("ONE_F", [128, 1], F32)
        ONE_B = sbt("ONE_B", [128, 1], BF16)
        ONE_R = sbt("ONE_R", [1, 128], F32)
        BIAS = sbt("BIAS", [128, 1], F32)
        CST = sbt("CST", [1, 2], F32)
        SM_S1 = SM[0:1, 0:1024]
        SM_S2 = SM[0:1, 1024:2048]
        SM_M = SM[0:1, 2048:3072]
        SM_MSQ = SM[0:1, 0:1024]
        SM_VAR = SM[0:1, 3072:4096]
        SM_SUM = [SM[0:1, 512 * i:512 * (i + 1)] for i in range(2)]
        SM_REC = [SM[0:1, 1024 + 512 * i:1024 + 512 * (i + 1)] for i in range(2)]
        SM_KMX = SM[0:1, 2048:2064]
        SM_QMX = SM[0:1, 2064:2080]
        SM_SC = SM[0:1, 2080:2096]

        PSL = [stk.enter_context(nc.psum_tensor(f"PSL{i}", [128, 1024], F32))
               for i in range(2)]
        PSO = [stk.enter_context(nc.psum_tensor(f"PSO{i}", [128, 512], F32))
               for i in range(2)]
        PSS = [stk.enter_context(nc.psum_tensor(f"PSS{i}", [128, 512], F32))
               for i in range(2)]
        STAT = [PSO[0][0:1, :], PSO[1][0:1, :], PSS[0][0:1, :], PSS[1][0:1, :]]

        sPE = Sem("sPE")
        sACT = Sem("sACT")
        sDVE = Sem("sDVE")
        sPOOL = Sem("sPOOL")
        sCC = Sem("sCC")
        XL = [Sem(f"xl{i}") for i in range(8)]
        WL = [Sem(f"wl{i}") for i in range(8)]
        HL = [Sem(f"hl{i}") for i in range(4)]
        for s in (sPE, sACT, sDVE, sPOOL, sCC, *XL, *WL, *HL):
            s.h = stk.enter_context(nc.semaphore(s.name))

        xseq = [0]

        def xdma(build, waits=()):
            s = XL[xseq[0] % len(XL)]
            xseq[0] += 1
            p.op("sync", build, waits=list(waits) + [(s, s.n)], inc=s, inc_n=16)
            return (s, s.n)

        hseq = [0]

        def hdma(build, waits=()):
            s = HL[hseq[0] % len(HL)]
            hseq[0] += 1
            p.op("sync", build, waits=list(waits) + [(s, s.n)], inc=s, inc_n=16)
            return (s, s.n)

        wseq = [0]

        def wdma(build, waits=()):
            s = WL[wseq[0] % len(WL)]
            wseq[0] += 1
            p.op("gpsimd", build, waits=list(waits) + [(s, s.n)], inc=s, inc_n=16)
            return (s, s.n)

        def xall():
            return [(s, s.n) for s in XL]

        def mw(pairs):
            best = {}
            for (s, v) in pairs:
                if s not in best or v > best[s]:
                    best[s] = v
            return list(best.items())

        def barrier():
            marks = [(sPE, sPE.n), (sACT, sACT.n), (sDVE, sDVE.n),
                     (sPOOL, sPOOL.n)] + xall()
            for e in ("tensor", "scalar", "vector", "sync"):
                p.wait(e, marks)

        wb_state = {"seq": 0, "pe": []}

        def wslice(src, c0):
            i = wb_state["seq"]
            wb_state["seq"] += 1
            waits = []
            if i >= 3:
                waits.append(wb_state["pe"][i - 3])
            ap = src[:, c0:c0 + SLC].rearrange("(dc p) e -> p dc e", p=128)
            buf = WB[i % 3][:, :].rearrange("p (dc e) -> p dc e", e=SLC)
            sd = wdma(lambda g, _a=ap, _b=buf: g.dma_start(out=_b, in_=_a),
                      waits=waits)
            return i, sd

        def wslice_consumed(guard):
            wb_state["pe"].append(guard)

        wd_state = {"seq": 0, "g": [], "g0": None}

        def wdslice(g_, dc):
            i = wd_state["seq"]
            wd_state["seq"] += 1
            waits = []
            if i >= 4:
                waits.append(wd_state["g"][i - 4])
            elif i == 0:
                waits.append(wd_state["g0"])
            ap = wdT[g_ * 1024:(g_ + 1) * 1024, dc * 128:(dc + 1) * 128].rearrange(
                "(hc p) e -> p hc e", p=128)
            buf = D[:, 8192 + (i % 4) * 1024:8192 + (i % 4) * 1024 + 1024].rearrange(
                "p (hc e) -> p hc e", e=128)
            sd = wdma(lambda g, _a=ap, _b=buf: g.dma_start(out=_b, in_=_a),
                      waits=waits)
            return i, sd

        p.op("vector", lambda v: v.memset(ONE_F[:, :], 1.0))
        p.op("vector", lambda v: v.memset(ONE_B[:, :], 1.0))
        p.op("vector", lambda v: v.memset(ONE_R[:, :], 1.0))
        p.op("vector", lambda v: v.memset(CST[0:1, 0:1], EPS))
        p.op("vector", lambda v: v.memset(CST[0:1, 1:2], float(np.log(SCL))),
             inc=sDVE)
        ones_ready = sDVE.n

        ldx = {}
        for q4 in range(4):
            ldx[q4] = xdma(lambda s, _q=q4: s.dma_start(
                out=A[:, _q * 4 * T:(_q + 1) * 4 * T].rearrange("p (c t) -> p c t", t=T),
                in_=xc[_q * 512:(_q + 1) * 512, :].rearrange("(c p) t -> p c t", p=128)))

        ln1_pe = {}
        for dc in range(NDC):
            cw = [ldx[dc // 4]]
            if dc >= 2:
                cw.append((sPE, ln1_pe[dc - 2]))
            p.op("scalar", lambda s, _d=dc: s.activation(
                SG[_d % 2][:, :], A[:, _d * T:(_d + 1) * T], AF.Square),
                waits=cw, inc=sACT)
            sq = sACT.n
            st, sp = (dc == 0), (dc == NDC - 1)
            for n in range(2):
                p.op("tensor", lambda t, _d=dc, _n=n, _s=st, _e=sp: t.matmul(
                    STAT[_n], ONE_F[:, :],
                    A[:, _d * T + _n * 512:_d * T + (_n + 1) * 512], start=_s, stop=_e),
                    waits=([(sACT, sq), (sDVE, ones_ready)] if n == 0 else ()))
            for n in range(2):
                p.op("tensor", lambda t, _d=dc, _n=n, _s=st, _e=sp: t.matmul(
                    STAT[2 + _n], ONE_B[:, :],
                    SG[_d % 2][:, _n * 512:(_n + 1) * 512], start=_s, stop=_e),
                    inc=(sPE if n == 1 else None))
            ln1_pe[dc] = sPE.n

        def ln_junction():
            for j in range(2):
                p.op("vector", lambda v, _j=j: v.tensor_copy(
                    SM_S1[:, _j * 512:(_j + 1) * 512], STAT[_j]),
                    waits=([(sPE, sPE.n)] if j == 0 else ()))
            for j in range(2):
                p.op("vector", lambda v, _j=j: v.tensor_copy(
                    SM_S2[:, _j * 512:(_j + 1) * 512], STAT[2 + _j]),
                    inc=(sDVE if j == 1 else None))
            p.op("vector", lambda v: v.tensor_scalar_mul(SM_M[:, :], SM_S1[:, :], 1.0 / DM),
                 waits=[(sDVE, sDVE.n)], inc=sDVE)
            p.op("vector", lambda v: v.tensor_mul(SM_MSQ[:, :], SM_M[:, :], SM_M[:, :]),
                 waits=[(sDVE, sDVE.n)], inc=sDVE)
            p.op("vector", lambda v: v.scalar_tensor_tensor(
                SM_VAR[:, :], SM_S2[:, :], 1.0 / DM, SM_MSQ[:, :], ALU.mult, ALU.subtract),
                waits=[(sDVE, sDVE.n)], inc=sDVE)
            mres = sDVE.n
            p.op("scalar", lambda s: s.activation(SM_MSQ[:, :], SM_VAR[:, :], AF.Ln,
                                                  bias=CST[0:1, 0:1]),
                 waits=[(sDVE, mres)], inc=sACT)
            p.op("scalar", lambda s: s.activation(SM_VAR[:, :], SM_MSQ[:, :], AF.Exp,
                                                  scale=-0.5),
                 waits=[(sACT, sACT.n)], inc=sACT)
            rres = sACT.n
            for n in range(2):
                p.op("tensor", lambda t, _n=n: t.matmul(
                    PSL[0][:, _n * 512:(_n + 1) * 512], ONE_R[:, :],
                    SM_M[:, _n * 512:(_n + 1) * 512], start=True, stop=True),
                    waits=([(sDVE, mres)] if n == 0 else ()))
            for n in range(2):
                p.op("tensor", lambda t, _n=n: t.matmul(
                    PSL[1][:, _n * 512:(_n + 1) * 512], ONE_R[:, :],
                    SM_VAR[:, _n * 512:(_n + 1) * 512], start=True, stop=True),
                    waits=([(sACT, rres)] if n == 0 else ()),
                    inc=(sPE if n == 1 else None))
            bc = sPE.n
            p.op("vector", lambda v: v.tensor_copy(XIO[0][:, :], PSL[0][:, :]),
                 waits=[(sPE, bc)], inc=sDVE)
            p.op("vector", lambda v: v.tensor_copy(XIO[1][:, :], PSL[1][:, :]),
                 inc=sDVE)
            return sDVE.n

        bc1 = ln_junction()

        PT = [D[:, 2048 * i:2048 * (i + 1)].bitcast(F32) for i in range(2)]
        norm_done = {}
        last_norm = {"vector": None, "gpsimd": None}
        for dc in range(NDC):
            pooled = (dc % 3 == 2)
            eng, sem = ("gpsimd", sPOOL) if pooled else ("vector", sDVE)
            scr = PT[0] if pooled else SCR[1][:, :]
            w = [(sDVE, bc1)]
            if last_norm[eng] is not None:
                w.append(last_norm[eng])
            p.op(eng, lambda v, _d=dc, _s=scr: v.tensor_tensor(
                _s, A[:, _d * T:(_d + 1) * T], XIO[0][:, :], ALU.subtract),
                waits=w, inc=sem)
            t1 = sem.n
            p.op(eng, lambda v, _d=dc, _s=scr: v.tensor_tensor(
                B[:, _d * T:(_d + 1) * T], _s, XIO[1][:, :], ALU.mult),
                waits=[(sem, t1)], inc=sem)
            norm_done[dc] = (sem, sem.n)
            last_norm[eng] = norm_done[dc]
        xh1_done = sDVE.n
        xh1_pool = sPOOL.n
        barrier()
        p.wait("tensor", [(sPOOL, xh1_pool)])

        dbg_attn_gate = None
        if DEBUG:
            dbg_attn_gate = xdma(lambda s: s.dma_start(out=taps["tap_xh1"], in_=B[:, :]),
                                 waits=[(sDVE, xh1_done)])

        kq = {"i": 0, "evt": {}, "pend": None}

        def kq_bounds_flush():
            if kq["pend"] is None:
                return
            i, par, sq, slot = kq["pend"]
            kq["pend"] = None
            bmw = [(sACT, sq)]
            if i >= 2:
                bmw.append((sDVE, kq["evt"][i - 2][4]))
            for n in range(2):
                p.op("tensor", lambda t, _n=n, _par=par: t.matmul(
                    PSS[_par][0:1, :], ONE_B[:, :],
                    SG[_par][:, _n * 512:(_n + 1) * 512],
                    start=(_n == 0), stop=(_n == 1)),
                    waits=(bmw if n == 0 else ()), inc=(sPE if n == 1 else None))
            bm = sPE.n
            p.op("vector", lambda v, _par=par, _slot=slot, _i=i: v.reduce_max(
                _slot[:, _i % 16:_i % 16 + 1], PSS[_par][0:1, :], axis=AX.X),
                waits=[(sPE, bm)], inc=sDVE)
            ev, sq_, st_, _, _ = kq["evt"][i]
            kq["evt"][i] = (ev, sq_, st_, bm, sDVE.n)

        def kq_chunk(eh, wb_i, ecl, slice_done, dest_dram, bounds_slot,
                     extra_first=(), extra_evac=()):
            i = kq["i"]
            kq["i"] += 1
            par = i % 2
            w0 = [slice_done, (sDVE, xh1_done)] + list(extra_first)
            if i >= 2:
                w0.append((sDVE, kq["evt"][i - 2][0]))
            for dc in range(NDC):
                for n in range(2):
                    p.op("tensor", lambda t, _dc=dc, _n=n, _w=wb_i, _ecl=ecl,
                         _par=par, _s=(dc == 0), _e=(dc == NDC - 1): t.matmul(
                        PSL[_par][:, _n * 512:(_n + 1) * 512],
                        WB[_w % 3][:, _dc * SLC + _ecl * 128:_dc * SLC + (_ecl + 1) * 128],
                        B[:, _dc * T + _n * 512:_dc * T + (_n + 1) * 512],
                        start=_s, stop=_e),
                        waits=(w0 if (dc == 0 and n == 0) else ()),
                        inc=(sPE if (dc == NDC - 1 and n == 1) else None))
            mm = sPE.n
            kq_bounds_flush()
            kst_w = [(sPE, mm)] + list(extra_evac)
            if i >= 2:
                kst_w += [(sACT, kq["evt"][i - 2][1]), kq["evt"][i - 2][2]]
            for n in range(2):
                p.op("vector", lambda v, _n=n, _par=par: v.tensor_copy(
                    KST[_par][:, _n * 512:(_n + 1) * 512],
                    PSL[_par][:, _n * 512:(_n + 1) * 512]),
                    waits=(kst_w if n == 0 else ()), inc=(sDVE if n == 1 else None))
            ev = sDVE.n
            bw = [(sDVE, ev)]
            if i >= 2:
                bw.append((sPE, kq["evt"][i - 2][3]))
            p.op("scalar", lambda s, _par=par: s.activation(
                SG[_par][:, :], KST[_par][:, :], AF.Square), waits=bw, inc=sACT)
            sq = sACT.n
            st_ = xdma(lambda s, _par=par, _eh=eh, _dst=dest_dram: s.dma_start(
                out=_dst.ap()[_eh * 128:(_eh + 1) * 128, :], in_=KST[_par][:, :]),
                waits=[(sDVE, ev)])
            kq["evt"][i] = (ev, sq, st_, sPE.n, sDVE.n)
            kq["pend"] = (i, par, sq, bounds_slot)
            return mm

        for si in range(8):
            wb_i, sd = wslice(wqkvT, 2048 + si * SLC)
            mm_last = 0
            for ecl in range(2):
                mm_last = kq_chunk(si * 2 + ecl, wb_i, ecl, sd, kout, SM_KMX)
            wslice_consumed((sPE, mm_last))
        kq_bounds_flush()
        k_stage_all = xall()
        k_sq_end = kq["evt"][15][1]
        p.op("gpsimd", lambda g: g.collective_compute(
            "AllGather", ALU.bypass, replica_groups=[[0, 1, 2, 3], [4, 5, 6, 7]],
            ins=[kout.ap().opt()], outs=[kin.ap().opt()]),
            waits=k_stage_all, inc=sCC)
        cc_k = sCC.n

        VPS = [PSO[0], PSO[1], PSS[0], PSS[1]]
        v_i = 0
        v_evt = {}
        v_stage = {}
        v_first_w = k_stage_all + [(sACT, k_sq_end), (sDVE, sDVE.n)]
        for si in range(8):
            wb_i, sd = wslice(wqkvT, 4096 + si * SLC)
            mm_last = 0
            for tc in range(8):
                i = v_i
                v_i += 1
                r4 = i % 4
                kcol = r4 * SLC
                w0 = [sd]
                if i >= 4:
                    w0.append((sDVE, v_evt[i - 4]))
                elif i == 0:
                    w0 += v_first_w
                for dc in range(NDC):
                    p.op("tensor", lambda t, _dc=dc, _tc=tc, _r=r4, _w=wb_i,
                         _s=(dc == 0), _e=(dc == NDC - 1): t.matmul(
                        VPS[_r][:, 0:SLC],
                        B[:, _dc * T + _tc * 128:_dc * T + (_tc + 1) * 128],
                        WB[_w % 3][:, _dc * SLC:(_dc + 1) * SLC],
                        start=_s, stop=_e),
                        waits=(w0 if dc == 0 else ()),
                        inc=(sPE if dc == NDC - 1 else None))
                mm_last = sPE.n
                evw = [(sPE, mm_last)]
                if i >= 8:
                    evw.append(v_stage[(i - 8) // 4])
                p.op("vector", lambda v, _i=i: v.tensor_copy(
                    KST[(_i // 4) % 2][:, (_i % 4) * SLC:(_i % 4) * SLC + SLC],
                    VPS[_i % 4][:, 0:SLC]),
                    waits=mw(evw), inc=sDVE)
                v_evt[i] = sDVE.n
                if i % 4 == 3:
                    v_stage[i // 4] = xdma(
                        lambda s, _i=i, _tc=tc, _si=si: s.dma_start(
                            out=vout.ap()[(_tc - 3) * 128:(_tc + 1) * 128,
                                          _si * SLC:(_si + 1) * SLC].rearrange(
                                "(c p) v -> p c v", p=128),
                            in_=KST[(_i // 4) % 2][:, :].rearrange(
                                "p (c v) -> p c v", v=SLC)),
                        waits=[(sDVE, v_evt[i])])
            wslice_consumed((sPE, mm_last))
        v_stage_all = xall()
        v_evt_end = sDVE.n
        if DEBUG:
            xdma(lambda s: s.dma_start(out=taps["tap_vout"], in_=vout.ap()[:, :]),
                 waits=[(sCC, cc_v)])
            xdma(lambda s: s.dma_start(out=taps["tap_kin"], in_=kin.ap()[:, :]))
            xdma(lambda s: s.dma_start(out=taps["tap_vin"], in_=vin.ap()[:, :]))

        head_dma = {}
        head_vdma = {}
        head_pe_done = {}

        def head_loads(h, with_qs=True):
            ph = h % 2
            base = ph * 8192
            waits = [(sCC, cc_k)]
            if h >= 2:
                waits.append((sPE, head_pe_done[h - 2]))
            pairs = []
            for r in range(4):
                pairs.append(hdma(lambda s, _h=h, _r=r, _b=base: s.dma_start(
                    out=D[:, _b + _r * T:_b + (_r + 1) * T],
                    in_=kin.ap()[_r * 2048 + _h * 128:_r * 2048 + (_h + 1) * 128, :]),
                    waits=(waits if r == 0 else ())))
            if with_qs:
                pairs.append(qs_load(h))
            vpairs = []
            vw = [(sCC, cc_v)]
            for r in range(4):
                vpairs.append(hdma(lambda s, _h=h, _r=r, _b=base: s.dma_start(
                    out=D[:, _b + 4096 + _r * T:_b + 4096 + (_r + 1) * T].rearrange(
                        "p (c v) -> p c v", v=128),
                    in_=vin.ap()[_r * T:(_r + 1) * T, _h * 128:(_h + 1) * 128].rearrange(
                        "(c p) v -> p c v", p=128)),
                    waits=(vw if r == 0 else ())))
            head_dma[h] = mw(pairs)
            head_vdma[h] = mw(vpairs)

        def qs_load(h):
            qw = [kq["evt"][16 + h][2]]
            if h >= 2:
                qw.append((sPE, head_pe_done[h - 2]))
            return hdma(lambda s, _h=h, _ph=h % 2: s.dma_start(
                out=QS[_ph][:, :], in_=qout.ap()[_h * 128:(_h + 1) * 128, :]),
                waits=qw)


        q_first_evac = [True]
        for si in range(8):
            wb_i, sd = wslice(wqkvT, 0 + si * SLC)
            mm_last = 0
            for ecl in range(2):
                first = (si == 0 and ecl == 0)
                mm_last = kq_chunk(si * 2 + ecl, wb_i, ecl, sd, qout, SM_QMX,
                                   extra_first=([(sDVE, v_evt_end)] if first else ()),
                                   extra_evac=(mw(v_stage_all) if si == 0 else ()))
            wslice_consumed((sPE, mm_last))
        kq_bounds_flush()
        qproj_pe_done = sPE.n
        p.op("gpsimd", lambda g: g.collective_compute(
            "AllGather", ALU.bypass, replica_groups=[[0, 1, 2, 3], [4, 5, 6, 7]],
            ins=[vout.ap().opt()], outs=[vin.ap().opt()]),
            waits=v_stage_all, inc=sCC)
        cc_v = sCC.n
        if DEBUG:
            xdma(lambda s: s.dma_start(out=taps["tap_q"], in_=qout.ap()[:, :]),
                 waits=xall())

        p.op("vector", lambda v: v.reduce_max(SM_SC[:, 0:1], SM_KMX[:, :], axis=AX.X),
             waits=[(sDVE, sDVE.n)], inc=sDVE)
        p.op("vector", lambda v: v.reduce_max(SM_SC[:, 1:2], SM_QMX[:, :], axis=AX.X),
             waits=[(sDVE, sDVE.n)], inc=sDVE)
        p.op("vector", lambda v: v.tensor_mul(SM_SC[:, 2:3], SM_SC[:, 0:1], SM_SC[:, 1:2]),
             waits=[(sDVE, sDVE.n)], inc=sDVE)
        p.op("scalar", lambda s: s.activation(SM_SC[:, 3:4], SM_SC[:, 2:3], AF.Ln),
             waits=[(sDVE, sDVE.n)], inc=sACT)
        p.op("scalar", lambda s: s.activation(SM_SC[:, 4:5], SM_SC[:, 3:4], AF.Exp,
                                              scale=0.5, bias=CST[0:1, 1:2]),
             waits=[(sACT, sACT.n)], inc=sACT)
        p.op("vector", lambda v: v.tensor_scalar(SM_SC[:, 5:6], SM_SC[:, 4:5],
                                                 1.01, 1.0, ALU.mult, ALU.add),
             waits=[(sACT, sACT.n)], inc=sDVE)
        p.op("tensor", lambda t: t.matmul(PSS[0][:, 0:1], ONE_R[:, :], SM_SC[:, 5:6],
                                          start=True, stop=True),
             waits=[(sDVE, sDVE.n), (sPE, qproj_pe_done)], inc=sPE)
        p.op("vector", lambda v: v.tensor_scalar_mul(BIAS[:, :], PSS[0][:, 0:1], -1.0),
             waits=[(sPE, sPE.n)], inc=sDVE)
        bias_ready = sDVE.n
        barrier()

        np_done = {0: None, 1: None}
        head_loads(0)
        head_loads(1)

        exp_done = {}
        pvpair_done = {}
        pend_ep = [None]

        def emit_epilogue():
            if pend_ep[0] is None:
                return
            h_, n_, npi, acc = pend_ep[0]
            pend_ep[0] = None
            p.op("vector", lambda v, _np=npi: v.tensor_copy(
                SM_SUM[_np][:, :], PSS[_np][0:1, :]),
                waits=[(sPE, acc)], inc=sDVE)
            p.op("vector", lambda v, _np=npi: v.reciprocal(
                SM_REC[_np][:, :], SM_SUM[_np][:, :]),
                waits=[(sDVE, sDVE.n)], inc=sDVE)
            rec_ = sDVE.n
            p.op("tensor", lambda t, _np=npi: t.matmul(
                PSS[_np][:, :], ONE_R[:, :], SM_REC[_np][:, :],
                start=True, stop=True),
                waits=[(sDVE, rec_)], inc=sPE)
            p.op("vector", lambda v, _np=npi: v.tensor_copy(RB[_np][:, :], PSS[_np][:, :]),
                 waits=[(sPE, sPE.n)], inc=sDVE)
            p.op("vector", lambda v, _np=npi, _h=h_, _n=n_: v.tensor_tensor(
                B[:, _h * T + _n * 512:_h * T + (_n + 1) * 512],
                PSO[_np][:, :], RB[_np][:, :], ALU.mult),
                waits=[(sDVE, sDVE.n)], inc=sDVE)
            np_done[npi] = sDVE.n

        NPAIR = NKC // 2
        for h in range(H):
            ph = h % 2
            base = ph * 8192
            for n in range(2):
                npi = (2 * h + n) % 2
                gj0 = (2 * h + n) * NPAIR
                qk_pe = {}

                def emit_qk_pair(j):
                    gj = gj0 + j
                    qk_w = []
                    if j == 0:
                        qk_w.extend(head_dma[h])
                        if h == 0 and n == 0:
                            qk_w.append((sDVE, bias_ready))
                    if gj >= 2:
                        qk_w.append((sACT, exp_done[gj - 2]))
                    for half in range(2):
                        kc = 2 * j + half
                        p.op("tensor", lambda t, _kc=kc, _b=base, _ph=ph, _n=n,
                             _g=gj: t.matmul(
                            PSL[_g % 2][:, (_kc % 2) * 512:(_kc % 2) * 512 + 512],
                            D[:, _b + _kc * 128:_b + (_kc + 1) * 128],
                            QS[_ph][:, _n * 512:(_n + 1) * 512],
                            start=True, stop=True),
                            waits=(qk_w if half == 0 else ()),
                            inc=(sPE if half == 1 else None))
                    qk_pe[gj] = sPE.n

                def emit_exp(j):
                    gj = gj0 + j
                    ew = [(sPE, qk_pe[gj]), (sPE, pvpair_done.get(gj - 2, 0))]
                    p.op("scalar", lambda s, _g=gj: s.activation(
                        SG[_g % 2][:, :], PSL[_g % 2][:, :], AF.Exp,
                        bias=BIAS[:, :], scale=SCL),
                        waits=mw(ew), inc=sACT)
                    exp_done[gj] = sACT.n

                def emit_sum_pv(j):
                    gj = gj0 + j
                    sw = [(sACT, exp_done[gj])]
                    if j == 0 and np_done[npi] is not None:
                        sw.append((sDVE, np_done[npi]))
                    if j == 0 and n == 0:
                        sw.extend(head_vdma[h])
                    for half in range(2):
                        p.op("tensor", lambda t, _np=npi, _g=gj, _h2=half,
                             _s=(j == 0 and half == 0),
                             _e=(j == NPAIR - 1 and half == 1): t.matmul(
                            PSS[_np][0:1, :], ONE_B[:, :],
                            SG[_g % 2][:, _h2 * 512:(_h2 + 1) * 512],
                            start=_s, stop=_e),
                            waits=(sw if half == 0 else ()))
                    for half in range(2):
                        kc = 2 * j + half
                        p.op("tensor", lambda t, _kc=kc, _b=base, _np=npi, _g=gj,
                             _h2=half, _s=(j == 0 and half == 0),
                             _e=(j == NPAIR - 1 and half == 1): t.matmul(
                            PSO[_np][:, :],
                            D[:, _b + 4096 + _kc * 128:_b + 4096 + (_kc + 1) * 128],
                            SG[_g % 2][:, _h2 * 512:(_h2 + 1) * 512],
                            start=_s, stop=_e),
                            inc=(sPE if half == 1 else None))
                    pvpair_done[gj] = sPE.n

                emit_qk_pair(0)
                emit_exp(0)
                emit_qk_pair(1)
                emit_exp(1)
                for j in range(2, NPAIR):
                    emit_qk_pair(j)
                    if j == 6:
                        emit_epilogue()
                    emit_sum_pv(j - 2)
                    emit_exp(j)
                emit_sum_pv(NPAIR - 2)
                emit_sum_pv(NPAIR - 1)
                pend_ep[0] = (h, n, npi, pvpair_done[gj0 + NPAIR - 1])
            head_pe_done[h] = sPE.n
            if h + 2 < H:
                head_loads(h + 2)
        emit_epilogue()
        attn_done = sDVE.n
        barrier()

        dbg_x2_gate = None
        if DEBUG:
            dbg_x2_gate = xdma(lambda s: s.dma_start(out=taps["tap_attn"], in_=B[:, :]),
                               waits=[(sDVE, attn_done)])

        y_evt = {}
        ln2_pend = [None]
        ln2_pe = {}

        SQB = [SG[0], SG[1], KST[0], KST[1]]
        XB = [D[:, 8192 + 1024 * i:8192 + 1024 * (i + 1)] for i in range(4)]

        def stat_flush(pend, pe_map):
            if pend[0] is None:
                return
            dc, sq = pend[0]
            pend[0] = None
            st, sp = (dc == 0), (dc == NDC - 1)
            for n in range(2):
                p.op("tensor", lambda t, _dc=dc, _n=n, _s=st, _e=sp: t.matmul(
                    STAT[_n], ONE_B[:, :],
                    XB[_dc % 4][:, _n * 512:(_n + 1) * 512],
                    start=_s, stop=_e),
                    waits=([(sACT, sq)] if n == 0 else ()))
            for n in range(2):
                p.op("tensor", lambda t, _dc=dc, _n=n, _s=st, _e=sp: t.matmul(
                    STAT[2 + _n], ONE_B[:, :],
                    SQB[_dc % 4][:, _n * 512:(_n + 1) * 512], start=_s, stop=_e),
                    inc=(sPE if n == 1 else None))
            pe_map[dc] = sPE.n

        for si in range(8):
            wb_i, sd = wslice(woutT, si * SLC)
            mm_last = 0
            for dcl in range(2):
                dc = si * 2 + dcl
                par = dc % 2
                w0 = [sd, (sDVE, attn_done)]
                if dc >= 2:
                    w0.append((sDVE, y_evt[dc - 2]))
                for ec in range(NDC):
                    for n in range(2):
                        p.op("tensor", lambda t, _ec=ec, _n=n, _dcl=dcl, _w=wb_i,
                             _par=par, _s=(ec == 0), _e=(ec == NDC - 1): t.matmul(
                            PSL[_par][:, _n * 512:(_n + 1) * 512],
                            WB[_w % 3][:, _ec * SLC + _dcl * 128:_ec * SLC + (_dcl + 1) * 128],
                            B[:, _ec * T + _n * 512:_ec * T + (_n + 1) * 512],
                            start=_s, stop=_e),
                            waits=(w0 if (ec == 0 and n == 0) else ()),
                            inc=(sPE if (ec == NDC - 1 and n == 1) else None))
                mm_last = sPE.n
                stat_flush(ln2_pend, ln2_pe)
                for n in range(2):
                    p.op("vector", lambda v, _dc=dc, _n=n, _par=par: v.tensor_copy(
                        A[:, _dc * T + _n * 512:_dc * T + (_n + 1) * 512],
                        PSL[_par][:, _n * 512:(_n + 1) * 512]),
                        waits=([(sPE, mm_last)] if n == 0 else ()),
                        inc=(sDVE if n == 1 else None))
                y_evt[dc] = sDVE.n
                cw = [(sDVE, y_evt[dc])]
                if dc >= 4:
                    cw.append((sPE, ln2_pe[dc - 4]))
                p.op("scalar", lambda s, _dc=dc: s.activation(
                    SQB[_dc % 4][:, :], A[:, _dc * T:(_dc + 1) * T], AF.Square),
                    waits=cw, inc=sACT)
                p.op("scalar", lambda s, _dc=dc: s.copy(
                    XB[_dc % 4][:, :], A[:, _dc * T:(_dc + 1) * T]),
                    inc=sACT)
                ln2_pend[0] = (dc, sACT.n)
            wslice_consumed((sPE, mm_last))
        stat_flush(ln2_pend, ln2_pe)
        barrier()
        if DEBUG:
            dbg_y_gate = xdma(lambda s: s.dma_start(out=taps["tap_y"], in_=A[:, :]),
                              waits=[(sDVE, sDVE.n)])
        bc2 = ln_junction()

        PT2 = [D[:, 4096 + 2048 * i:4096 + 2048 * (i + 1)].bitcast(F32)
               for i in range(2)]
        x2_evt = {}
        last_x2 = {"vector": None, "gpsimd": None}
        ln3_pend = [None]
        ln3_pe = {}
        for dc in range(NDC):
            pooled = (dc % 3 == 2)
            eng, sem = ("gpsimd", sPOOL) if pooled else ("vector", sDVE)
            s1 = PT[0] if pooled else SCR[1][:, :]
            s2 = PT[1] if pooled else SCR[2][:, :]
            s0 = PT2[0] if pooled else SCR[0][:, :]
            waits = [(sDVE, bc2)]
            if last_x2[eng] is not None:
                waits = waits + [last_x2[eng]]
            if DEBUG and dc == 0:
                waits = waits + [dbg_y_gate, dbg_x2_gate]
            ld = xdma(lambda s, _d=dc, _b=s0: s.dma_start(
                out=_b, in_=xc[_d * 128:(_d + 1) * 128, :]),
                waits=waits)
            w = [ld, (sDVE, bc2)]
            if last_x2[eng] is not None:
                w.append(last_x2[eng])
            p.op(eng, lambda v, _d=dc, _s=s1: v.tensor_tensor(
                _s, A[:, _d * T:(_d + 1) * T], XIO[0][:, :], ALU.subtract),
                waits=w, inc=sem)
            t1 = sem.n
            p.op(eng, lambda v, _a=s1, _b=s2: v.tensor_tensor(
                _b, _a, XIO[1][:, :], ALU.mult),
                waits=[(sem, t1)], inc=sem)
            t2 = sem.n
            p.op(eng, lambda v, _d=dc, _b=s2, _x=s0: v.tensor_tensor(
                A[:, _d * T:(_d + 1) * T], _x, _b, ALU.add),
                waits=[(sem, t2)], inc=sem)
            x2v = (sem, sem.n)
            last_x2[eng] = x2v
            xdma(lambda s, _d=dc: s.dma_start(
                out=x2d.ap()[_d * 128:(_d + 1) * 128, :], in_=A[:, _d * T:(_d + 1) * T]),
                waits=[x2v])
            cw = [x2v]
            if dc >= 4:
                cw.append((sPE, ln3_pe.get(dc - 4, 0)))
            p.op("scalar", lambda s, _dc=dc: s.activation(
                SQB[_dc % 4][:, :], A[:, _dc * T:(_dc + 1) * T], AF.Square),
                waits=cw, inc=sACT)
            p.op("scalar", lambda s, _dc=dc: s.copy(
                XB[_dc % 4][:, :], A[:, _dc * T:(_dc + 1) * T]),
                inc=sACT)
            sq = sACT.n
            stat_flush(ln3_pend, ln3_pe)
            ln3_pend[0] = (dc, sq)
            x2_evt[dc] = x2v
        stat_flush(ln3_pend, ln3_pe)
        barrier()
        bc3 = ln_junction()
        xh3_evt = {}
        last_x3 = {"vector": None, "gpsimd": None}
        for dc in range(NDC):
            pooled = (dc % 3 == 2)
            eng, sem = ("gpsimd", sPOOL) if pooled else ("vector", sDVE)
            scr = PT[0] if pooled else SCR[1][:, :]
            w = [(sDVE, bc3)]
            if last_x3[eng] is not None:
                w.append(last_x3[eng])
            p.op(eng, lambda v, _d=dc, _s=scr: v.tensor_tensor(
                _s, A[:, _d * T:(_d + 1) * T], XIO[0][:, :], ALU.subtract),
                waits=w, inc=sem)
            t1 = sem.n
            p.op(eng, lambda v, _d=dc, _s=scr: v.tensor_tensor(
                B[:, _d * T:(_d + 1) * T], _s, XIO[1][:, :], ALU.mult),
                waits=[(sem, t1)], inc=sem)
            xh3_evt[dc] = (sem, sem.n)
            last_x3[eng] = xh3_evt[dc]
        xh3_done = sDVE.n
        xh3_pool = sPOOL.n
        barrier()
        if DEBUG:
            xdma(lambda s: s.dma_start(out=taps["tap_x2"], in_=x2d.ap()[:, :]),
                 waits=xall())
            xdma(lambda s: s.dma_start(out=taps["tap_xh3"], in_=B[:, :]),
                 waits=[(sDVE, xh3_done)])

        wd_state["g0"] = (sPE, sPE.n)
        mult_hist = {}
        down_done_pe = 0
        dacc = {}
        for g in range(8):
            for si in range(4):
                wbg_i, sdg = wslice(wgT, g * 1024 + si * SLC)
                wbu_i, sdu = wslice(wuT, g * 1024 + si * SLC)
                gate_last = up_last = 0
                for hcl in range(2):
                    ahc = g * 8 + si * 2 + hcl
                    gw = [sdg, (sDVE, xh3_done), (sACT, sACT.n)]
                    if g > 0 and si == 0 and hcl == 0:
                        gw.append((sPE, down_done_pe))
                    for dc in range(NDC):
                        for n in range(2):
                            p.op("tensor", lambda t, _dc=dc, _n=n, _hcl=hcl, _w=wbg_i,
                                 _s=(dc == 0), _e=(dc == NDC - 1): t.matmul(
                                PSL[0][:, _n * 512:(_n + 1) * 512],
                                WB[_w % 3][:, _dc * SLC + _hcl * 128:_dc * SLC + (_hcl + 1) * 128],
                                B[:, _dc * T + _n * 512:_dc * T + (_n + 1) * 512],
                                start=_s, stop=_e),
                                waits=(gw if (dc == 0 and n == 0) else ()),
                                inc=(sPE if (dc == NDC - 1 and n == 1) else None))
                    gate_last = sPE.n
                    sil_w = [(sPE, gate_last)]
                    if ahc >= 2:
                        sil_w.append((sDVE, mult_hist[ahc - 2]))
                    p.op("scalar", lambda s, _a=ahc: s.activation(
                        SG[_a % 2][:, :], PSL[0][:, :], AF.Silu),
                        waits=sil_w, inc=sACT)
                    sil = sACT.n
                    uw = [sdu, (sACT, sil)]
                    if ahc >= 1:
                        uw.append((sDVE, mult_hist[ahc - 1]))
                    for dc in range(NDC):
                        for n in range(2):
                            p.op("tensor", lambda t, _dc=dc, _n=n, _hcl=hcl, _w=wbu_i,
                                 _s=(dc == 0), _e=(dc == NDC - 1): t.matmul(
                                PSL[1][:, _n * 512:(_n + 1) * 512],
                                WB[_w % 3][:, _dc * SLC + _hcl * 128:_dc * SLC + (_hcl + 1) * 128],
                                B[:, _dc * T + _n * 512:_dc * T + (_n + 1) * 512],
                                start=_s, stop=_e),
                                waits=(uw if (dc == 0 and n == 0) else ()),
                                inc=(sPE if (dc == NDC - 1 and n == 1) else None))
                    up_last = sPE.n
                    for n in range(2):
                        p.op("vector", lambda v, _n=n, _a=ahc: v.tensor_tensor(
                            D[:, (_a % 8) * T + _n * 512:(_a % 8) * T + (_n + 1) * 512],
                            SG[_a % 2][:, _n * 512:(_n + 1) * 512],
                            PSL[1][:, _n * 512:(_n + 1) * 512], ALU.mult),
                            waits=([(sPE, up_last)] if n == 0 else ()),
                            inc=(sDVE if n == 1 else None))
                    mult_hist[ahc] = sDVE.n
                wslice_consumed((sPE, gate_last))
                wslice_consumed((sPE, up_last))
            gu_done = sDVE.n
            for dc in range(NDC):
                gdc = g * NDC + dc
                wd_i, sdd = wdslice(g, dc)
                par = dc % 2
                dw = [sdd, (sDVE, gu_done)]
                if gdc >= 2:
                    dw.append((sDVE, dacc[gdc - 2]))
                for hcl in range(8):
                    for n in range(2):
                        p.op("tensor", lambda t, _h=hcl, _n=n, _w=wd_i, _par=par,
                             _s=(hcl == 0), _e=(hcl == 7): t.matmul(
                            (PSO[_par] if _n == 0 else PSS[_par])[:, :],
                            D[:, 8192 + (_w % 4) * 1024 + _h * 128:
                               8192 + (_w % 4) * 1024 + (_h + 1) * 128],
                            D[:, _h * T + _n * 512:_h * T + (_n + 1) * 512],
                            start=_s, stop=_e),
                            waits=(dw if (hcl == 0 and n == 0) else ()),
                            inc=(sPE if (hcl == 7 and n == 1) else None))
                dmm = sPE.n
                wd_state["g"].append((sPE, dmm))
                for n in range(2):
                    if g == 0:
                        p.op("vector", lambda v, _dc=dc, _n=n, _par=par: v.tensor_copy(
                            A[:, _dc * T + _n * 512:_dc * T + (_n + 1) * 512],
                            (PSO[_par] if _n == 0 else PSS[_par])[:, :]),
                            waits=([(sPE, dmm)] if n == 0 else ()),
                            inc=(sDVE if n == 1 else None))
                    else:
                        p.op("vector", lambda v, _dc=dc, _n=n, _par=par: v.tensor_tensor(
                            A[:, _dc * T + _n * 512:_dc * T + (_n + 1) * 512],
                            A[:, _dc * T + _n * 512:_dc * T + (_n + 1) * 512],
                            (PSO[_par] if _n == 0 else PSS[_par])[:, :], ALU.add),
                            waits=([(sPE, dmm)] if n == 0 else ()),
                            inc=(sDVE if n == 1 else None))
                dacc[gdc] = sDVE.n
            down_done_pe = sPE.n
        barrier()

        if DEBUG:
            xdma(lambda s: s.dma_start(out=taps["tap_y2"], in_=A[:, :]),
                 waits=[(sDVE, sDVE.n)])
        ln4_pe = {}
        for dc in range(NDC):
            cw = []
            if dc >= 4:
                cw.append((sPE, ln4_pe[dc - 4]))
            p.op("scalar", lambda s, _dc=dc: s.activation(
                SQB[_dc % 4][:, :], A[:, _dc * T:(_dc + 1) * T], AF.Square),
                waits=cw, inc=sACT)
            p.op("scalar", lambda s, _dc=dc: s.copy(
                XB[_dc % 4][:, :], A[:, _dc * T:(_dc + 1) * T]),
                inc=sACT)
            sq = sACT.n
            st, sp = (dc == 0), (dc == NDC - 1)
            for n in range(2):
                p.op("tensor", lambda t, _dc=dc, _n=n, _s=st, _e=sp: t.matmul(
                    STAT[_n], ONE_B[:, :],
                    XB[_dc % 4][:, _n * 512:(_n + 1) * 512],
                    start=_s, stop=_e),
                    waits=([(sACT, sq)] if n == 0 else ()))
            for n in range(2):
                p.op("tensor", lambda t, _dc=dc, _n=n, _s=st, _e=sp: t.matmul(
                    STAT[2 + _n], ONE_B[:, :],
                    SQB[_dc % 4][:, _n * 512:(_n + 1) * 512], start=_s, stop=_e),
                    inc=(sPE if n == 1 else None))
            ln4_pe[dc] = sPE.n
        barrier()
        bc4 = ln_junction()
        out_dma = {}
        acc_dma = {}
        prev_t1 = [None]
        for dc in range(NDC):
            sl = 0 if dc % 2 == 0 else 2
            w = [(sDVE, bc4)]
            if prev_t1[0] is not None:
                w.append((sDVE, prev_t1[0]))
            p.op("vector", lambda v, _d=dc: v.tensor_tensor(
                SCR[1][:, :], A[:, _d * T:(_d + 1) * T], XIO[0][:, :], ALU.subtract),
                waits=w, inc=sDVE)
            t1 = sDVE.n
            w2 = [(sDVE, t1)] + ([out_dma[dc - 2]] if dc >= 2 else [])
            p.op("vector", lambda v, _sl=sl: v.tensor_tensor(
                SCR[_sl][:, :], SCR[1][:, :], XIO[1][:, :], ALU.mult),
                waits=w2, inc=sDVE)
            t2 = sDVE.n
            prev_t1[0] = t2
            acc_dma[dc] = wdma(lambda g, _d=dc, _sl=sl: g.dma_start(
                out=SCR[_sl][:, :], in_=x2d.ap()[_d * 128:(_d + 1) * 128, :],
                accum_op=ALU.add),
                waits=[(sDVE, t2)])
            out_dma[dc] = xdma(lambda s, _d=dc, _sl=sl: s.dma_start(
                out=outc[_d * 128:(_d + 1) * 128, :], in_=SCR[_sl][:, :]),
                waits=[acc_dma[dc]])

        p.wait("sync", xall() + [(s, s.n) for s in HL] + [(s, s.n) for s in WL])
        p.wait("gpsimd", xall())

        with nc.Block() as block:
            @block.tensor
            def _(eng):
                for f in p.ops["tensor"]:
                    f(eng)

            @block.scalar
            def _(eng):
                for f in p.ops["scalar"]:
                    f(eng)

            @block.vector
            def _(eng):
                for f in p.ops["vector"]:
                    f(eng)

            @block.gpsimd
            def _(eng):
                for f in p.ops["gpsimd"]:
                    f(eng)

            @block.sync
            def _(eng):
                for f in p.ops["sync"]:
                    f(eng)

    return nc


_NC_CACHE = None


def kernel(L0_attn_x, L0_attn_Wqkv, L0_attn_Wout, L0_ffn_w_gate, L0_ffn_w_up,
           L0_ffn_w_down):
    global _NC_CACHE, LAST_TAPS
    x = np.asarray(L0_attn_x, np.float32)
    WqkvT = np.ascontiguousarray(np.asarray(L0_attn_Wqkv, np.float32).T)
    WoutT = np.ascontiguousarray(np.asarray(L0_attn_Wout, np.float32).T)
    WgT = np.ascontiguousarray(np.asarray(L0_ffn_w_gate, np.float32).T)
    WuT = np.ascontiguousarray(np.asarray(L0_ffn_w_up, np.float32).T)
    WdT = np.ascontiguousarray(np.asarray(L0_ffn_w_down, np.float32).T)

    if DEBUG:
        nc = build_nc()
    else:
        global _NC_CACHE
        if _NC_CACHE is None:
            _NC_CACHE = build_nc()
        nc = _NC_CACHE

    in_maps = []
    for c in range(8):
        b, s = c // 4, c % 4
        xcv = np.ascontiguousarray(x[b, s * T:(s + 1) * T, :].T)
        in_maps.append({"xc": xcv, "wqkvT": WqkvT, "woutT": WoutT,
                        "wgT": WgT, "wuT": WuT, "wdT": WdT})
    res = run_bass_kernel_spmd(nc, in_maps, core_ids=list(range(8)))
    if DEBUG:
        LAST_TAPS = res.results
    out = np.empty((2, S, DM), np.float32)
    for c in range(8):
        b, s = c // 4, c % 4
        out[b, s * T:(s + 1) * T, :] = res.results[c]["outc"].T
    return out


# version 21
# speedup vs baseline: 1.1213x; 1.0381x over previous
import numpy as np
import concourse.bass as bass
import concourse.mybir as mybir
from concourse.bass_utils import run_bass_kernel_spmd
from contextlib import ExitStack

F32 = mybir.dt.float32
BF16 = mybir.dt.bfloat16
AF = mybir.ActivationFunctionType
ALU = mybir.AluOpType
AX = mybir.AxisListType

DM = 2048
NDC = 16
T = 1024
S = 4096
NKC = 32
H = 16
HID = 8192
EQKV = 6144
EPS = 1e-5
SCL = float(1.0 / np.sqrt(128.0))
SLC = 256

ENGINES = ("tensor", "scalar", "vector", "gpsimd", "sync")

DEBUG = False
LAST_TAPS = None


class Sem:
    __slots__ = ("name", "n", "h")

    def __init__(self, name):
        self.name = name
        self.n = 0
        self.h = None

    def inc(self, k=1):
        self.n += k
        return self.n


class Prog:
    def __init__(self):
        self.ops = {e: [] for e in ENGINES}

    def op(self, eng, build, waits=(), inc=None, inc_n=1):
        waits = [(s, v) for (s, v) in waits]
        if inc is not None:
            inc.inc(inc_n)

        def emit(engine, _b=build, _w=waits, _i=inc, _n=inc_n):
            for (s, v) in _w:
                engine.wait_ge(s.h, v)
            ins = _b(engine)
            if _i is not None:
                ins.then_inc(_i.h, _n)
        self.ops[eng].append(emit)

    def wait(self, eng, waits):
        waits = [(s, v) for (s, v) in waits]

        def emit(engine, _w=waits):
            for (s, v) in _w:
                engine.wait_ge(s.h, v)
        self.ops[eng].append(emit)


def build_nc():
    nc = bass.Bass("TRN2", target_bir_lowering=False)

    xc = nc.dram_tensor("xc", [DM, T], F32, kind="ExternalInput").ap()
    wqkvT = nc.dram_tensor("wqkvT", [DM, EQKV], F32, kind="ExternalInput").ap()
    woutT = nc.dram_tensor("woutT", [DM, DM], F32, kind="ExternalInput").ap()
    wgT = nc.dram_tensor("wgT", [DM, HID], F32, kind="ExternalInput").ap()
    wuT = nc.dram_tensor("wuT", [DM, HID], F32, kind="ExternalInput").ap()
    wdT = nc.dram_tensor("wdT", [HID, DM], F32, kind="ExternalInput").ap()
    outc = nc.dram_tensor("outc", [DM, T], F32, kind="ExternalOutput").ap()

    taps = {}
    if DEBUG:
        def mktap(name, shape, dt):
            taps[name] = nc.dram_tensor(name, shape, dt, kind="ExternalOutput").ap()
        mktap("tap_xh1", [128, NDC * T], BF16)
        mktap("tap_vout", [T, DM], BF16)
        mktap("tap_kin", [4 * DM, T], BF16)
        mktap("tap_vin", [4 * T, DM], BF16)
        mktap("tap_q", [DM, T], BF16)
        mktap("tap_attn", [128, NDC * T], BF16)
        mktap("tap_y", [128, NDC * T], F32)
        mktap("tap_x2", [DM, T], F32)
        mktap("tap_xh3", [128, NDC * T], BF16)
        mktap("tap_y2", [128, NDC * T], F32)

    kout = nc.dram_tensor("kout", [DM, T], BF16, kind="Internal")
    qout = nc.dram_tensor("qout", [DM, T], BF16, kind="Internal")
    vout = nc.dram_tensor("vout", [T, DM], BF16, kind="Internal")
    kin = nc.dram_tensor("kin", [4 * DM, T], BF16, kind="Internal")
    vin = nc.dram_tensor("vin", [4 * T, DM], BF16, kind="Internal")
    x2d = nc.dram_tensor("x2d", [DM, T], F32, kind="Internal")

    p = Prog()
    stk = ExitStack()
    sbt = lambda name, shape, dt: stk.enter_context(nc.sbuf_tensor(name, shape, dt))

    with stk:
        A = sbt("A", [128, NDC * T], F32)
        B = sbt("B", [128, NDC * T], BF16)
        D = sbt("D", [128, 2 * 8192], BF16)
        WB = [sbt(f"WB{i}", [128, NDC * SLC], BF16) for i in range(3)]
        QS = [sbt(f"QS{i}", [128, T], BF16) for i in range(2)]
        SG = [sbt(f"SG{i}", [128, T], BF16) for i in range(2)]
        XIO = [sbt(f"XIO{i}", [128, T], F32) for i in range(2)]
        SCR = [sbt(f"SCR{i}", [128, T], F32) for i in range(3)]
        KST = [sbt(f"KST{i}", [128, T], BF16) for i in range(2)]
        RB = [sbt(f"RB{i}", [128, 512], F32) for i in range(2)]
        SM = sbt("SM", [1, 4096], F32)
        ONE_F = sbt("ONE_F", [128, 1], F32)
        ONE_B = sbt("ONE_B", [128, 1], BF16)
        ONE_R = sbt("ONE_R", [1, 128], F32)
        BIAS = sbt("BIAS", [128, 1], F32)
        CST = sbt("CST", [1, 2], F32)
        SM_S1 = SM[0:1, 0:1024]
        SM_S2 = SM[0:1, 1024:2048]
        SM_M = SM[0:1, 2048:3072]
        SM_MSQ = SM[0:1, 0:1024]
        SM_VAR = SM[0:1, 3072:4096]
        SM_SUM = [SM[0:1, 512 * i:512 * (i + 1)] for i in range(2)]
        SM_REC = [SM[0:1, 1024 + 512 * i:1024 + 512 * (i + 1)] for i in range(2)]
        SM_KMX = SM[0:1, 2048:2064]
        SM_QMX = SM[0:1, 2064:2080]
        SM_SC = SM[0:1, 2080:2096]

        PSL = [stk.enter_context(nc.psum_tensor(f"PSL{i}", [128, 1024], F32))
               for i in range(2)]
        PSO = [stk.enter_context(nc.psum_tensor(f"PSO{i}", [128, 512], F32))
               for i in range(2)]
        PSS = [stk.enter_context(nc.psum_tensor(f"PSS{i}", [128, 512], F32))
               for i in range(2)]
        STAT = [PSO[0][0:1, :], PSO[1][0:1, :], PSS[0][0:1, :], PSS[1][0:1, :]]

        sPE = Sem("sPE")
        sACT = Sem("sACT")
        sDVE = Sem("sDVE")
        sPOOL = Sem("sPOOL")
        sCC = Sem("sCC")
        XL = [Sem(f"xl{i}") for i in range(8)]
        WL = [Sem(f"wl{i}") for i in range(8)]
        HL = [Sem(f"hl{i}") for i in range(4)]
        for s in (sPE, sACT, sDVE, sPOOL, sCC, *XL, *WL, *HL):
            s.h = stk.enter_context(nc.semaphore(s.name))

        xseq = [0]

        def xdma(build, waits=()):
            s = XL[xseq[0] % len(XL)]
            xseq[0] += 1
            p.op("sync", build, waits=list(waits) + [(s, s.n)], inc=s, inc_n=16)
            return (s, s.n)

        hseq = [0]

        def hdma(build, waits=()):
            s = HL[hseq[0] % len(HL)]
            hseq[0] += 1
            p.op("sync", build, waits=list(waits) + [(s, s.n)], inc=s, inc_n=16)
            return (s, s.n)

        wseq = [0]

        def wdma(build, waits=()):
            s = WL[wseq[0] % len(WL)]
            wseq[0] += 1
            p.op("gpsimd", build, waits=list(waits) + [(s, s.n)], inc=s, inc_n=16)
            return (s, s.n)

        def xall():
            return [(s, s.n) for s in XL]

        def mw(pairs):
            best = {}
            for (s, v) in pairs:
                if s not in best or v > best[s]:
                    best[s] = v
            return list(best.items())

        def barrier():
            marks = [(sPE, sPE.n), (sACT, sACT.n), (sDVE, sDVE.n),
                     (sPOOL, sPOOL.n)] + xall()
            for e in ("tensor", "scalar", "vector", "sync"):
                p.wait(e, marks)

        wb_state = {"seq": 0, "pe": []}

        def wslice(src, c0):
            i = wb_state["seq"]
            wb_state["seq"] += 1
            waits = []
            if i >= 3:
                waits.append(wb_state["pe"][i - 3])
            ap = src[:, c0:c0 + SLC].rearrange("(dc p) e -> p dc e", p=128)
            buf = WB[i % 3][:, :].rearrange("p (dc e) -> p dc e", e=SLC)
            sd = wdma(lambda g, _a=ap, _b=buf: g.dma_start(out=_b, in_=_a),
                      waits=waits)
            return i, sd

        def wslice_consumed(guard):
            wb_state["pe"].append(guard)

        wd_state = {"seq": 0, "g": [], "g0": None}

        def wdslice(g_, dc):
            i = wd_state["seq"]
            wd_state["seq"] += 1
            waits = []
            if i >= 4:
                waits.append(wd_state["g"][i - 4])
            elif i == 0:
                waits.append(wd_state["g0"])
            ap = wdT[g_ * 1024:(g_ + 1) * 1024, dc * 128:(dc + 1) * 128].rearrange(
                "(hc p) e -> p hc e", p=128)
            buf = D[:, 8192 + (i % 4) * 1024:8192 + (i % 4) * 1024 + 1024].rearrange(
                "p (hc e) -> p hc e", e=128)
            sd = wdma(lambda g, _a=ap, _b=buf: g.dma_start(out=_b, in_=_a),
                      waits=waits)
            return i, sd

        p.op("vector", lambda v: v.memset(ONE_F[:, :], 1.0))
        p.op("vector", lambda v: v.memset(ONE_B[:, :], 1.0))
        p.op("vector", lambda v: v.memset(ONE_R[:, :], 1.0))
        p.op("vector", lambda v: v.memset(CST[0:1, 0:1], EPS))
        p.op("vector", lambda v: v.memset(CST[0:1, 1:2], float(np.log(SCL))),
             inc=sDVE)
        ones_ready = sDVE.n

        ldx = {}
        for q4 in range(4):
            ldx[q4] = xdma(lambda s, _q=q4: s.dma_start(
                out=A[:, _q * 4 * T:(_q + 1) * 4 * T].rearrange("p (c t) -> p c t", t=T),
                in_=xc[_q * 512:(_q + 1) * 512, :].rearrange("(c p) t -> p c t", p=128)))

        ln1_pe = {}
        for dc in range(NDC):
            cw = [ldx[dc // 4]]
            if dc >= 2:
                cw.append((sPE, ln1_pe[dc - 2]))
            p.op("scalar", lambda s, _d=dc: s.activation(
                SG[_d % 2][:, :], A[:, _d * T:(_d + 1) * T], AF.Square),
                waits=cw, inc=sACT)
            sq = sACT.n
            st, sp = (dc == 0), (dc == NDC - 1)
            for n in range(2):
                p.op("tensor", lambda t, _d=dc, _n=n, _s=st, _e=sp: t.matmul(
                    STAT[_n], ONE_F[:, :],
                    A[:, _d * T + _n * 512:_d * T + (_n + 1) * 512], start=_s, stop=_e),
                    waits=([(sACT, sq), (sDVE, ones_ready)] if n == 0 else ()))
            for n in range(2):
                p.op("tensor", lambda t, _d=dc, _n=n, _s=st, _e=sp: t.matmul(
                    STAT[2 + _n], ONE_B[:, :],
                    SG[_d % 2][:, _n * 512:(_n + 1) * 512], start=_s, stop=_e),
                    inc=(sPE if n == 1 else None))
            ln1_pe[dc] = sPE.n

        def ln_junction():
            for j in range(2):
                p.op("vector", lambda v, _j=j: v.tensor_copy(
                    SM_S1[:, _j * 512:(_j + 1) * 512], STAT[_j]),
                    waits=([(sPE, sPE.n)] if j == 0 else ()))
            for j in range(2):
                p.op("vector", lambda v, _j=j: v.tensor_copy(
                    SM_S2[:, _j * 512:(_j + 1) * 512], STAT[2 + _j]),
                    inc=(sDVE if j == 1 else None))
            p.op("vector", lambda v: v.tensor_scalar_mul(SM_M[:, :], SM_S1[:, :], 1.0 / DM),
                 waits=[(sDVE, sDVE.n)], inc=sDVE)
            p.op("vector", lambda v: v.tensor_mul(SM_MSQ[:, :], SM_M[:, :], SM_M[:, :]),
                 waits=[(sDVE, sDVE.n)], inc=sDVE)
            p.op("vector", lambda v: v.scalar_tensor_tensor(
                SM_VAR[:, :], SM_S2[:, :], 1.0 / DM, SM_MSQ[:, :], ALU.mult, ALU.subtract),
                waits=[(sDVE, sDVE.n)], inc=sDVE)
            mres = sDVE.n
            p.op("scalar", lambda s: s.activation(SM_MSQ[:, :], SM_VAR[:, :], AF.Ln,
                                                  bias=CST[0:1, 0:1]),
                 waits=[(sDVE, mres)], inc=sACT)
            p.op("scalar", lambda s: s.activation(SM_VAR[:, :], SM_MSQ[:, :], AF.Exp,
                                                  scale=-0.5),
                 waits=[(sACT, sACT.n)], inc=sACT)
            rres = sACT.n
            for n in range(2):
                p.op("tensor", lambda t, _n=n: t.matmul(
                    PSL[0][:, _n * 512:(_n + 1) * 512], ONE_R[:, :],
                    SM_M[:, _n * 512:(_n + 1) * 512], start=True, stop=True),
                    waits=([(sDVE, mres)] if n == 0 else ()))
            for n in range(2):
                p.op("tensor", lambda t, _n=n: t.matmul(
                    PSL[1][:, _n * 512:(_n + 1) * 512], ONE_R[:, :],
                    SM_VAR[:, _n * 512:(_n + 1) * 512], start=True, stop=True),
                    waits=([(sACT, rres)] if n == 0 else ()),
                    inc=(sPE if n == 1 else None))
            bc = sPE.n
            p.op("vector", lambda v: v.tensor_copy(XIO[0][:, :], PSL[0][:, :]),
                 waits=[(sPE, bc)], inc=sDVE)
            p.op("vector", lambda v: v.tensor_copy(XIO[1][:, :], PSL[1][:, :]),
                 inc=sDVE)
            return sDVE.n

        bc1 = ln_junction()

        PT = [D[:, 2048 * i:2048 * (i + 1)].bitcast(F32) for i in range(2)]
        norm_done = {}
        last_norm = {"vector": None, "gpsimd": None}
        for dc in range(NDC):
            pooled = (dc % 3 == 2)
            eng, sem = ("gpsimd", sPOOL) if pooled else ("vector", sDVE)
            scr = PT[0] if pooled else SCR[1][:, :]
            w = [(sDVE, bc1)]
            if last_norm[eng] is not None:
                w.append(last_norm[eng])
            p.op(eng, lambda v, _d=dc, _s=scr: v.tensor_tensor(
                _s, A[:, _d * T:(_d + 1) * T], XIO[0][:, :], ALU.subtract),
                waits=w, inc=sem)
            t1 = sem.n
            p.op(eng, lambda v, _d=dc, _s=scr: v.tensor_tensor(
                B[:, _d * T:(_d + 1) * T], _s, XIO[1][:, :], ALU.mult),
                waits=[(sem, t1)], inc=sem)
            norm_done[dc] = (sem, sem.n)
            last_norm[eng] = norm_done[dc]
        xh1_done = sDVE.n
        xh1_pool = sPOOL.n
        barrier()
        p.wait("tensor", [(sPOOL, xh1_pool)])

        dbg_attn_gate = None
        if DEBUG:
            dbg_attn_gate = xdma(lambda s: s.dma_start(out=taps["tap_xh1"], in_=B[:, :]),
                                 waits=[(sDVE, xh1_done)])

        kq = {"i": 0, "evt": {}, "pend": None}

        def kq_bounds_flush():
            if kq["pend"] is None:
                return
            i, par, sq, slot = kq["pend"]
            kq["pend"] = None
            bmw = [(sACT, sq)]
            if i >= 2:
                bmw.append((sDVE, kq["evt"][i - 2][4]))
            for n in range(2):
                p.op("tensor", lambda t, _n=n, _par=par: t.matmul(
                    PSS[_par][0:1, :], ONE_B[:, :],
                    SG[_par][:, _n * 512:(_n + 1) * 512],
                    start=(_n == 0), stop=(_n == 1)),
                    waits=(bmw if n == 0 else ()), inc=(sPE if n == 1 else None))
            bm = sPE.n
            p.op("vector", lambda v, _par=par, _slot=slot, _i=i: v.reduce_max(
                _slot[:, _i % 16:_i % 16 + 1], PSS[_par][0:1, :], axis=AX.X),
                waits=[(sPE, bm)], inc=sDVE)
            ev, sq_, st_, _, _ = kq["evt"][i]
            kq["evt"][i] = (ev, sq_, st_, bm, sDVE.n)

        def kq_chunk(eh, wb_i, ecl, slice_done, dest_dram, bounds_slot,
                     extra_first=(), extra_evac=()):
            i = kq["i"]
            kq["i"] += 1
            par = i % 2
            w0 = [slice_done, (sDVE, xh1_done)] + list(extra_first)
            if i >= 2:
                w0.append((sDVE, kq["evt"][i - 2][0]))
            for dc in range(NDC):
                for n in range(2):
                    p.op("tensor", lambda t, _dc=dc, _n=n, _w=wb_i, _ecl=ecl,
                         _par=par, _s=(dc == 0), _e=(dc == NDC - 1): t.matmul(
                        PSL[_par][:, _n * 512:(_n + 1) * 512],
                        WB[_w % 3][:, _dc * SLC + _ecl * 128:_dc * SLC + (_ecl + 1) * 128],
                        B[:, _dc * T + _n * 512:_dc * T + (_n + 1) * 512],
                        start=_s, stop=_e),
                        waits=(w0 if (dc == 0 and n == 0) else ()),
                        inc=(sPE if (dc == NDC - 1 and n == 1) else None))
            mm = sPE.n
            kq_bounds_flush()
            kst_w = [(sPE, mm)] + list(extra_evac)
            if i >= 2:
                kst_w += [(sACT, kq["evt"][i - 2][1]), kq["evt"][i - 2][2]]
            for n in range(2):
                p.op("vector", lambda v, _n=n, _par=par: v.tensor_copy(
                    KST[_par][:, _n * 512:(_n + 1) * 512],
                    PSL[_par][:, _n * 512:(_n + 1) * 512]),
                    waits=(kst_w if n == 0 else ()), inc=(sDVE if n == 1 else None))
            ev = sDVE.n
            bw = [(sDVE, ev)]
            if i >= 2:
                bw.append((sPE, kq["evt"][i - 2][3]))
            p.op("scalar", lambda s, _par=par: s.activation(
                SG[_par][:, :], KST[_par][:, :], AF.Square), waits=bw, inc=sACT)
            sq = sACT.n
            st_ = xdma(lambda s, _par=par, _eh=eh, _dst=dest_dram: s.dma_start(
                out=_dst.ap()[_eh * 128:(_eh + 1) * 128, :], in_=KST[_par][:, :]),
                waits=[(sDVE, ev)])
            kq["evt"][i] = (ev, sq, st_, sPE.n, sDVE.n)
            kq["pend"] = (i, par, sq, bounds_slot)
            return mm

        for si in range(8):
            wb_i, sd = wslice(wqkvT, 2048 + si * SLC)
            mm_last = 0
            for ecl in range(2):
                mm_last = kq_chunk(si * 2 + ecl, wb_i, ecl, sd, kout, SM_KMX)
            wslice_consumed((sPE, mm_last))
        kq_bounds_flush()
        k_stage_all = xall()
        k_sq_end = kq["evt"][15][1]
        p.op("gpsimd", lambda g: g.collective_compute(
            "AllGather", ALU.bypass, replica_groups=[[0, 1, 2, 3], [4, 5, 6, 7]],
            ins=[kout.ap().opt()], outs=[kin.ap().opt()]),
            waits=k_stage_all, inc=sCC)
        cc_k = sCC.n

        VPS = [PSO[0], PSO[1], PSS[0], PSS[1]]
        v_i = 0
        v_evt = {}
        v_stage = {}
        v_first_w = k_stage_all + [(sACT, k_sq_end), (sDVE, sDVE.n)]
        for si in range(8):
            wb_i, sd = wslice(wqkvT, 4096 + si * SLC)
            mm_last = 0
            for tc in range(8):
                i = v_i
                v_i += 1
                r4 = i % 4
                kcol = r4 * SLC
                w0 = [sd]
                if i >= 4:
                    w0.append((sDVE, v_evt[i - 4]))
                elif i == 0:
                    w0 += v_first_w
                for dc in range(NDC):
                    p.op("tensor", lambda t, _dc=dc, _tc=tc, _r=r4, _w=wb_i,
                         _s=(dc == 0), _e=(dc == NDC - 1): t.matmul(
                        VPS[_r][:, 0:SLC],
                        B[:, _dc * T + _tc * 128:_dc * T + (_tc + 1) * 128],
                        WB[_w % 3][:, _dc * SLC:(_dc + 1) * SLC],
                        start=_s, stop=_e),
                        waits=(w0 if dc == 0 else ()),
                        inc=(sPE if dc == NDC - 1 else None))
                mm_last = sPE.n
                evw = [(sPE, mm_last)]
                if i >= 8:
                    evw.append(v_stage[(i - 8) // 4])
                p.op("vector", lambda v, _i=i: v.tensor_copy(
                    KST[(_i // 4) % 2][:, (_i % 4) * SLC:(_i % 4) * SLC + SLC],
                    VPS[_i % 4][:, 0:SLC]),
                    waits=mw(evw), inc=sDVE)
                v_evt[i] = sDVE.n
                if i % 4 == 3:
                    v_stage[i // 4] = xdma(
                        lambda s, _i=i, _tc=tc, _si=si: s.dma_start(
                            out=vout.ap()[(_tc - 3) * 128:(_tc + 1) * 128,
                                          _si * SLC:(_si + 1) * SLC].rearrange(
                                "(c p) v -> p c v", p=128),
                            in_=KST[(_i // 4) % 2][:, :].rearrange(
                                "p (c v) -> p c v", v=SLC)),
                        waits=[(sDVE, v_evt[i])])
            wslice_consumed((sPE, mm_last))
        v_stage_all = xall()
        v_evt_end = sDVE.n
        if DEBUG:
            xdma(lambda s: s.dma_start(out=taps["tap_vout"], in_=vout.ap()[:, :]),
                 waits=[(sCC, cc_v)])
            xdma(lambda s: s.dma_start(out=taps["tap_kin"], in_=kin.ap()[:, :]))
            xdma(lambda s: s.dma_start(out=taps["tap_vin"], in_=vin.ap()[:, :]))

        head_dma = {}
        head_vdma = {}
        head_pe_done = {}

        def head_loads(h, with_qs=True):
            ph = h % 2
            base = ph * 8192
            waits = [(sCC, cc_k)]
            if h >= 2:
                waits.append((sPE, head_pe_done[h - 2]))
            pairs = []
            for r in range(4):
                pairs.append(hdma(lambda s, _h=h, _r=r, _b=base: s.dma_start(
                    out=D[:, _b + _r * T:_b + (_r + 1) * T],
                    in_=kin.ap()[_r * 2048 + _h * 128:_r * 2048 + (_h + 1) * 128, :]),
                    waits=(waits if r == 0 else ())))
            if with_qs:
                pairs.append(qs_load(h))
            vpairs = []
            vw = [(sCC, cc_v)]
            for r in range(4):
                vpairs.append(hdma(lambda s, _h=h, _r=r, _b=base: s.dma_start(
                    out=D[:, _b + 4096 + _r * T:_b + 4096 + (_r + 1) * T].rearrange(
                        "p (c v) -> p c v", v=128),
                    in_=vin.ap()[_r * T:(_r + 1) * T, _h * 128:(_h + 1) * 128].rearrange(
                        "(c p) v -> p c v", p=128)),
                    waits=(vw if r == 0 else ())))
            head_dma[h] = mw(pairs)
            head_vdma[h] = mw(vpairs)

        def qs_load(h):
            qw = [kq["evt"][16 + h][2]]
            if h >= 2:
                qw.append((sPE, head_pe_done[h - 2]))
            return hdma(lambda s, _h=h, _ph=h % 2: s.dma_start(
                out=QS[_ph][:, :], in_=qout.ap()[_h * 128:(_h + 1) * 128, :]),
                waits=qw)


        q_first_evac = [True]
        for si in range(8):
            wb_i, sd = wslice(wqkvT, 0 + si * SLC)
            mm_last = 0
            for ecl in range(2):
                first = (si == 0 and ecl == 0)
                mm_last = kq_chunk(si * 2 + ecl, wb_i, ecl, sd, qout, SM_QMX,
                                   extra_first=([(sDVE, v_evt_end)] if first else ()),
                                   extra_evac=(mw(v_stage_all) if si == 0 else ()))
            wslice_consumed((sPE, mm_last))
        kq_bounds_flush()
        qproj_pe_done = sPE.n
        p.op("gpsimd", lambda g: g.collective_compute(
            "AllGather", ALU.bypass, replica_groups=[[0, 1, 2, 3], [4, 5, 6, 7]],
            ins=[vout.ap().opt()], outs=[vin.ap().opt()]),
            waits=v_stage_all, inc=sCC)
        cc_v = sCC.n
        if DEBUG:
            xdma(lambda s: s.dma_start(out=taps["tap_q"], in_=qout.ap()[:, :]),
                 waits=xall())

        p.op("vector", lambda v: v.reduce_max(SM_SC[:, 0:1], SM_KMX[:, :], axis=AX.X),
             waits=[(sDVE, sDVE.n)], inc=sDVE)
        p.op("vector", lambda v: v.reduce_max(SM_SC[:, 1:2], SM_QMX[:, :], axis=AX.X),
             waits=[(sDVE, sDVE.n)], inc=sDVE)
        p.op("vector", lambda v: v.tensor_mul(SM_SC[:, 2:3], SM_SC[:, 0:1], SM_SC[:, 1:2]),
             waits=[(sDVE, sDVE.n)], inc=sDVE)
        p.op("scalar", lambda s: s.activation(SM_SC[:, 3:4], SM_SC[:, 2:3], AF.Ln),
             waits=[(sDVE, sDVE.n)], inc=sACT)
        p.op("scalar", lambda s: s.activation(SM_SC[:, 4:5], SM_SC[:, 3:4], AF.Exp,
                                              scale=0.5, bias=CST[0:1, 1:2]),
             waits=[(sACT, sACT.n)], inc=sACT)
        p.op("vector", lambda v: v.tensor_scalar(SM_SC[:, 5:6], SM_SC[:, 4:5],
                                                 1.01, 1.0, ALU.mult, ALU.add),
             waits=[(sACT, sACT.n)], inc=sDVE)
        p.op("tensor", lambda t: t.matmul(PSS[0][:, 0:1], ONE_R[:, :], SM_SC[:, 5:6],
                                          start=True, stop=True),
             waits=[(sDVE, sDVE.n), (sPE, qproj_pe_done)], inc=sPE)
        p.op("vector", lambda v: v.tensor_scalar_mul(BIAS[:, :], PSS[0][:, 0:1], -1.0),
             waits=[(sPE, sPE.n)], inc=sDVE)
        bias_ready = sDVE.n
        barrier()

        np_done = {0: None, 1: None}
        head_loads(0)
        head_loads(1)

        exp_done = {}
        pvpair_done = {}
        pend_ep = [None]

        def emit_epilogue():
            if pend_ep[0] is None:
                return
            h_, n_, npi, acc = pend_ep[0]
            pend_ep[0] = None
            p.op("vector", lambda v, _np=npi: v.tensor_copy(
                SM_SUM[_np][:, :], PSS[_np][0:1, :]),
                waits=[(sPE, acc)], inc=sDVE)
            p.op("vector", lambda v, _np=npi: v.reciprocal(
                SM_REC[_np][:, :], SM_SUM[_np][:, :]),
                waits=[(sDVE, sDVE.n)], inc=sDVE)
            rec_ = sDVE.n
            p.op("tensor", lambda t, _np=npi: t.matmul(
                PSS[_np][:, :], ONE_R[:, :], SM_REC[_np][:, :],
                start=True, stop=True),
                waits=[(sDVE, rec_)], inc=sPE)
            p.op("vector", lambda v, _np=npi: v.tensor_copy(RB[_np][:, :], PSS[_np][:, :]),
                 waits=[(sPE, sPE.n)], inc=sDVE)
            p.op("vector", lambda v, _np=npi, _h=h_, _n=n_: v.tensor_tensor(
                B[:, _h * T + _n * 512:_h * T + (_n + 1) * 512],
                PSO[_np][:, :], RB[_np][:, :], ALU.mult),
                waits=[(sDVE, sDVE.n)], inc=sDVE)
            np_done[npi] = sDVE.n

        NPAIR = NKC // 2
        allpairs = [(h, n, j) for h in range(H) for n in range(2)
                    for j in range(NPAIR)]
        qk_pe = {}

        def emit_qk_pair(pp):
            h, n, j = allpairs[pp]
            ph = h % 2
            base = ph * 8192
            qk_w = []
            if j == 0 and n == 0:
                qk_w.extend(head_dma[h])
                if h == 0:
                    qk_w.append((sDVE, bias_ready))
            if pp >= 2:
                qk_w.append((sACT, exp_done[pp - 2]))
            for half in range(2):
                kc = 2 * j + half
                p_op("tensor", lambda t, _kc=kc, _b=base, _ph=ph, _n=n,
                     _g=pp: t.matmul(
                    PSL[_g % 2][:, (_kc % 2) * 512:(_kc % 2) * 512 + 512],
                    D[:, _b + _kc * 128:_b + (_kc + 1) * 128],
                    QS[_ph][:, _n * 512:(_n + 1) * 512],
                    start=True, stop=True),
                    waits=(qk_w if half == 0 else ()),
                    inc=(sPE if half == 1 else None))
            qk_pe[pp] = sPE.n

        def emit_exp(pp):
            ew = [(sPE, qk_pe[pp]), (sPE, pvpair_done.get(pp - 2, 0))]
            p_op("scalar", lambda s, _g=pp: s.activation(
                SG[_g % 2][:, :], PSL[_g % 2][:, :], AF.Exp,
                bias=BIAS[:, :], scale=SCL),
                waits=mw(ew), inc=sACT)
            exp_done[pp] = sACT.n

        def emit_sum_pv(pp):
            h, n, j = allpairs[pp]
            npi = (2 * h + n) % 2
            base = (h % 2) * 8192
            sw = [(sACT, exp_done[pp])]
            if j == 0 and np_done[npi] is not None:
                sw.append((sDVE, np_done[npi]))
            if j == 0 and n == 0:
                sw.extend(head_vdma[h])
            for half in range(2):
                p_op("tensor", lambda t, _np=npi, _g=pp, _h2=half,
                     _s=(j == 0 and half == 0),
                     _e=(j == NPAIR - 1 and half == 1): t.matmul(
                    PSS[_np][0:1, :], ONE_B[:, :],
                    SG[_g % 2][:, _h2 * 512:(_h2 + 1) * 512],
                    start=_s, stop=_e),
                    waits=(sw if half == 0 else ()))
            for half in range(2):
                kc = 2 * j + half
                p_op("tensor", lambda t, _kc=kc, _b=base, _np=npi, _g=pp,
                     _h2=half, _s=(j == 0 and half == 0),
                     _e=(j == NPAIR - 1 and half == 1): t.matmul(
                    PSO[_np][:, :],
                    D[:, _b + 4096 + _kc * 128:_b + 4096 + (_kc + 1) * 128],
                    SG[_g % 2][:, _h2 * 512:(_h2 + 1) * 512],
                    start=_s, stop=_e),
                    inc=(sPE if half == 1 else None))
            pvpair_done[pp] = sPE.n
            if j == NPAIR - 1:
                pend_ep[0] = (h, n, npi, sPE.n)
                if n == 1:
                    head_pe_done[h] = sPE.n
                    if h + 2 < H:
                        head_loads(h + 2)

        p_op = p.op
        for pp in range(len(allpairs)):
            emit_qk_pair(pp)
            if pp >= 2:
                emit_sum_pv(pp - 2)
            h, n, j = allpairs[pp]
            if j == 6:
                emit_epilogue()
            emit_exp(pp)
        emit_sum_pv(len(allpairs) - 2)
        emit_sum_pv(len(allpairs) - 1)
        emit_epilogue()
        attn_done = sDVE.n
        barrier()

        dbg_x2_gate = None
        if DEBUG:
            dbg_x2_gate = xdma(lambda s: s.dma_start(out=taps["tap_attn"], in_=B[:, :]),
                               waits=[(sDVE, attn_done)])

        y_evt = {}
        ln2_pend = [None]
        ln2_pe = {}

        SQB = [SG[0], SG[1], KST[0], KST[1]]
        XB = [D[:, 8192 + 1024 * i:8192 + 1024 * (i + 1)] for i in range(4)]

        def stat_flush(pend, pe_map):
            if pend[0] is None:
                return
            dc, sq = pend[0]
            pend[0] = None
            st, sp = (dc == 0), (dc == NDC - 1)
            for n in range(2):
                p.op("tensor", lambda t, _dc=dc, _n=n, _s=st, _e=sp: t.matmul(
                    STAT[_n], ONE_B[:, :],
                    XB[_dc % 4][:, _n * 512:(_n + 1) * 512],
                    start=_s, stop=_e),
                    waits=([(sACT, sq)] if n == 0 else ()))
            for n in range(2):
                p.op("tensor", lambda t, _dc=dc, _n=n, _s=st, _e=sp: t.matmul(
                    STAT[2 + _n], ONE_B[:, :],
                    SQB[_dc % 4][:, _n * 512:(_n + 1) * 512], start=_s, stop=_e),
                    inc=(sPE if n == 1 else None))
            pe_map[dc] = sPE.n

        for si in range(8):
            wb_i, sd = wslice(woutT, si * SLC)
            mm_last = 0
            for dcl in range(2):
                dc = si * 2 + dcl
                par = dc % 2
                w0 = [sd, (sDVE, attn_done)]
                if dc >= 2:
                    w0.append((sDVE, y_evt[dc - 2]))
                for ec in range(NDC):
                    for n in range(2):
                        p.op("tensor", lambda t, _ec=ec, _n=n, _dcl=dcl, _w=wb_i,
                             _par=par, _s=(ec == 0), _e=(ec == NDC - 1): t.matmul(
                            PSL[_par][:, _n * 512:(_n + 1) * 512],
                            WB[_w % 3][:, _ec * SLC + _dcl * 128:_ec * SLC + (_dcl + 1) * 128],
                            B[:, _ec * T + _n * 512:_ec * T + (_n + 1) * 512],
                            start=_s, stop=_e),
                            waits=(w0 if (ec == 0 and n == 0) else ()),
                            inc=(sPE if (ec == NDC - 1 and n == 1) else None))
                mm_last = sPE.n
                stat_flush(ln2_pend, ln2_pe)
                for n in range(2):
                    p.op("vector", lambda v, _dc=dc, _n=n, _par=par: v.tensor_copy(
                        A[:, _dc * T + _n * 512:_dc * T + (_n + 1) * 512],
                        PSL[_par][:, _n * 512:(_n + 1) * 512]),
                        waits=([(sPE, mm_last)] if n == 0 else ()),
                        inc=(sDVE if n == 1 else None))
                y_evt[dc] = sDVE.n
                cw = [(sDVE, y_evt[dc])]
                if dc >= 4:
                    cw.append((sPE, ln2_pe[dc - 4]))
                p.op("scalar", lambda s, _dc=dc: s.activation(
                    SQB[_dc % 4][:, :], A[:, _dc * T:(_dc + 1) * T], AF.Square),
                    waits=cw, inc=sACT)
                p.op("scalar", lambda s, _dc=dc: s.copy(
                    XB[_dc % 4][:, :], A[:, _dc * T:(_dc + 1) * T]),
                    inc=sACT)
                ln2_pend[0] = (dc, sACT.n)
            wslice_consumed((sPE, mm_last))
        stat_flush(ln2_pend, ln2_pe)
        barrier()
        if DEBUG:
            dbg_y_gate = xdma(lambda s: s.dma_start(out=taps["tap_y"], in_=A[:, :]),
                              waits=[(sDVE, sDVE.n)])
        bc2 = ln_junction()

        PT2 = [D[:, 4096 + 2048 * i:4096 + 2048 * (i + 1)].bitcast(F32)
               for i in range(2)]
        x2_evt = {}
        last_x2 = {"vector": None, "gpsimd": None}
        ln3_pend = [None]
        ln3_pe = {}
        for dc in range(NDC):
            pooled = (dc % 3 == 2)
            eng, sem = ("gpsimd", sPOOL) if pooled else ("vector", sDVE)
            s1 = PT[0] if pooled else SCR[1][:, :]
            s2 = PT[1] if pooled else SCR[2][:, :]
            s0 = PT2[0] if pooled else SCR[0][:, :]
            waits = [(sDVE, bc2)]
            if last_x2[eng] is not None:
                waits = waits + [last_x2[eng]]
            if DEBUG and dc == 0:
                waits = waits + [dbg_y_gate, dbg_x2_gate]
            ld = xdma(lambda s, _d=dc, _b=s0: s.dma_start(
                out=_b, in_=xc[_d * 128:(_d + 1) * 128, :]),
                waits=waits)
            w = [ld, (sDVE, bc2)]
            if last_x2[eng] is not None:
                w.append(last_x2[eng])
            p.op(eng, lambda v, _d=dc, _s=s1: v.tensor_tensor(
                _s, A[:, _d * T:(_d + 1) * T], XIO[0][:, :], ALU.subtract),
                waits=w, inc=sem)
            t1 = sem.n
            p.op(eng, lambda v, _a=s1, _b=s2: v.tensor_tensor(
                _b, _a, XIO[1][:, :], ALU.mult),
                waits=[(sem, t1)], inc=sem)
            t2 = sem.n
            p.op(eng, lambda v, _d=dc, _b=s2, _x=s0: v.tensor_tensor(
                A[:, _d * T:(_d + 1) * T], _x, _b, ALU.add),
                waits=[(sem, t2)], inc=sem)
            x2v = (sem, sem.n)
            last_x2[eng] = x2v
            xdma(lambda s, _d=dc: s.dma_start(
                out=x2d.ap()[_d * 128:(_d + 1) * 128, :], in_=A[:, _d * T:(_d + 1) * T]),
                waits=[x2v])
            cw = [x2v]
            if dc >= 4:
                cw.append((sPE, ln3_pe.get(dc - 4, 0)))
            p.op("scalar", lambda s, _dc=dc: s.activation(
                SQB[_dc % 4][:, :], A[:, _dc * T:(_dc + 1) * T], AF.Square),
                waits=cw, inc=sACT)
            p.op("scalar", lambda s, _dc=dc: s.copy(
                XB[_dc % 4][:, :], A[:, _dc * T:(_dc + 1) * T]),
                inc=sACT)
            sq = sACT.n
            stat_flush(ln3_pend, ln3_pe)
            ln3_pend[0] = (dc, sq)
            x2_evt[dc] = x2v
        stat_flush(ln3_pend, ln3_pe)
        barrier()
        bc3 = ln_junction()
        xh3_evt = {}
        last_x3 = {"vector": None, "gpsimd": None}
        for dc in range(NDC):
            pooled = (dc % 3 == 2)
            eng, sem = ("gpsimd", sPOOL) if pooled else ("vector", sDVE)
            scr = PT[0] if pooled else SCR[1][:, :]
            w = [(sDVE, bc3)]
            if last_x3[eng] is not None:
                w.append(last_x3[eng])
            p.op(eng, lambda v, _d=dc, _s=scr: v.tensor_tensor(
                _s, A[:, _d * T:(_d + 1) * T], XIO[0][:, :], ALU.subtract),
                waits=w, inc=sem)
            t1 = sem.n
            p.op(eng, lambda v, _d=dc, _s=scr: v.tensor_tensor(
                B[:, _d * T:(_d + 1) * T], _s, XIO[1][:, :], ALU.mult),
                waits=[(sem, t1)], inc=sem)
            xh3_evt[dc] = (sem, sem.n)
            last_x3[eng] = xh3_evt[dc]
        xh3_done = sDVE.n
        xh3_pool = sPOOL.n
        barrier()
        if DEBUG:
            xdma(lambda s: s.dma_start(out=taps["tap_x2"], in_=x2d.ap()[:, :]),
                 waits=xall())
            xdma(lambda s: s.dma_start(out=taps["tap_xh3"], in_=B[:, :]),
                 waits=[(sDVE, xh3_done)])

        wd_state["g0"] = (sPE, sPE.n)
        mult_hist = {}
        down_done_pe = 0
        dacc = {}
        for g in range(8):
            for si in range(4):
                wbg_i, sdg = wslice(wgT, g * 1024 + si * SLC)
                wbu_i, sdu = wslice(wuT, g * 1024 + si * SLC)
                gate_last = up_last = 0
                for hcl in range(2):
                    ahc = g * 8 + si * 2 + hcl
                    gw = [sdg, (sDVE, xh3_done), (sACT, sACT.n)]
                    if g > 0 and si == 0 and hcl == 0:
                        gw.append((sPE, down_done_pe))
                    for dc in range(NDC):
                        for n in range(2):
                            p.op("tensor", lambda t, _dc=dc, _n=n, _hcl=hcl, _w=wbg_i,
                                 _s=(dc == 0), _e=(dc == NDC - 1): t.matmul(
                                PSL[0][:, _n * 512:(_n + 1) * 512],
                                WB[_w % 3][:, _dc * SLC + _hcl * 128:_dc * SLC + (_hcl + 1) * 128],
                                B[:, _dc * T + _n * 512:_dc * T + (_n + 1) * 512],
                                start=_s, stop=_e),
                                waits=(gw if (dc == 0 and n == 0) else ()),
                                inc=(sPE if (dc == NDC - 1 and n == 1) else None))
                    gate_last = sPE.n
                    sil_w = [(sPE, gate_last)]
                    if ahc >= 2:
                        sil_w.append((sDVE, mult_hist[ahc - 2]))
                    p.op("scalar", lambda s, _a=ahc: s.activation(
                        SG[_a % 2][:, :], PSL[0][:, :], AF.Silu),
                        waits=sil_w, inc=sACT)
                    sil = sACT.n
                    uw = [sdu]
                    if ahc >= 1:
                        uw.append((sDVE, mult_hist[ahc - 1]))
                    for dc in range(NDC):
                        for n in range(2):
                            p.op("tensor", lambda t, _dc=dc, _n=n, _hcl=hcl, _w=wbu_i,
                                 _s=(dc == 0), _e=(dc == NDC - 1): t.matmul(
                                PSL[1][:, _n * 512:(_n + 1) * 512],
                                WB[_w % 3][:, _dc * SLC + _hcl * 128:_dc * SLC + (_hcl + 1) * 128],
                                B[:, _dc * T + _n * 512:_dc * T + (_n + 1) * 512],
                                start=_s, stop=_e),
                                waits=(uw if (dc == 0 and n == 0) else ()),
                                inc=(sPE if (dc == NDC - 1 and n == 1) else None))
                    up_last = sPE.n
                    for n in range(2):
                        p.op("vector", lambda v, _n=n, _a=ahc: v.tensor_tensor(
                            D[:, (_a % 8) * T + _n * 512:(_a % 8) * T + (_n + 1) * 512],
                            SG[_a % 2][:, _n * 512:(_n + 1) * 512],
                            PSL[1][:, _n * 512:(_n + 1) * 512], ALU.mult),
                            waits=([(sPE, up_last), (sACT, sil)] if n == 0 else ()),
                            inc=(sDVE if n == 1 else None))
                    mult_hist[ahc] = sDVE.n
                wslice_consumed((sPE, gate_last))
                wslice_consumed((sPE, up_last))
            gu_done = sDVE.n
            for dc in range(NDC):
                gdc = g * NDC + dc
                wd_i, sdd = wdslice(g, dc)
                par = dc % 2
                dw = [sdd, (sDVE, gu_done)]
                if gdc >= 2:
                    dw.append((sDVE, dacc[gdc - 2]))
                for hcl in range(8):
                    for n in range(2):
                        p.op("tensor", lambda t, _h=hcl, _n=n, _w=wd_i, _par=par,
                             _s=(hcl == 0), _e=(hcl == 7): t.matmul(
                            (PSO[_par] if _n == 0 else PSS[_par])[:, :],
                            D[:, 8192 + (_w % 4) * 1024 + _h * 128:
                               8192 + (_w % 4) * 1024 + (_h + 1) * 128],
                            D[:, _h * T + _n * 512:_h * T + (_n + 1) * 512],
                            start=_s, stop=_e),
                            waits=(dw if (hcl == 0 and n == 0) else ()),
                            inc=(sPE if (hcl == 7 and n == 1) else None))
                dmm = sPE.n
                wd_state["g"].append((sPE, dmm))
                for n in range(2):
                    if g == 0:
                        p.op("vector", lambda v, _dc=dc, _n=n, _par=par: v.tensor_copy(
                            A[:, _dc * T + _n * 512:_dc * T + (_n + 1) * 512],
                            (PSO[_par] if _n == 0 else PSS[_par])[:, :]),
                            waits=([(sPE, dmm)] if n == 0 else ()),
                            inc=(sDVE if n == 1 else None))
                    else:
                        p.op("vector", lambda v, _dc=dc, _n=n, _par=par: v.tensor_tensor(
                            A[:, _dc * T + _n * 512:_dc * T + (_n + 1) * 512],
                            A[:, _dc * T + _n * 512:_dc * T + (_n + 1) * 512],
                            (PSO[_par] if _n == 0 else PSS[_par])[:, :], ALU.add),
                            waits=([(sPE, dmm)] if n == 0 else ()),
                            inc=(sDVE if n == 1 else None))
                dacc[gdc] = sDVE.n
            down_done_pe = sPE.n
        barrier()

        if DEBUG:
            xdma(lambda s: s.dma_start(out=taps["tap_y2"], in_=A[:, :]),
                 waits=[(sDVE, sDVE.n)])
        ln4_pe = {}
        for dc in range(NDC):
            cw = []
            if dc >= 4:
                cw.append((sPE, ln4_pe[dc - 4]))
            p.op("scalar", lambda s, _dc=dc: s.activation(
                SQB[_dc % 4][:, :], A[:, _dc * T:(_dc + 1) * T], AF.Square),
                waits=cw, inc=sACT)
            p.op("scalar", lambda s, _dc=dc: s.copy(
                XB[_dc % 4][:, :], A[:, _dc * T:(_dc + 1) * T]),
                inc=sACT)
            sq = sACT.n
            st, sp = (dc == 0), (dc == NDC - 1)
            for n in range(2):
                p.op("tensor", lambda t, _dc=dc, _n=n, _s=st, _e=sp: t.matmul(
                    STAT[_n], ONE_B[:, :],
                    XB[_dc % 4][:, _n * 512:(_n + 1) * 512],
                    start=_s, stop=_e),
                    waits=([(sACT, sq)] if n == 0 else ()))
            for n in range(2):
                p.op("tensor", lambda t, _dc=dc, _n=n, _s=st, _e=sp: t.matmul(
                    STAT[2 + _n], ONE_B[:, :],
                    SQB[_dc % 4][:, _n * 512:(_n + 1) * 512], start=_s, stop=_e),
                    inc=(sPE if n == 1 else None))
            ln4_pe[dc] = sPE.n
        barrier()
        bc4 = ln_junction()
        out_dma = {}
        acc_dma = {}
        prev_t1 = [None]
        for dc in range(NDC):
            sl = 0 if dc % 2 == 0 else 2
            w = [(sDVE, bc4)]
            if prev_t1[0] is not None:
                w.append((sDVE, prev_t1[0]))
            p.op("vector", lambda v, _d=dc: v.tensor_tensor(
                SCR[1][:, :], A[:, _d * T:(_d + 1) * T], XIO[0][:, :], ALU.subtract),
                waits=w, inc=sDVE)
            t1 = sDVE.n
            w2 = [(sDVE, t1)] + ([out_dma[dc - 2]] if dc >= 2 else [])
            p.op("vector", lambda v, _sl=sl: v.tensor_tensor(
                SCR[_sl][:, :], SCR[1][:, :], XIO[1][:, :], ALU.mult),
                waits=w2, inc=sDVE)
            t2 = sDVE.n
            prev_t1[0] = t2
            acc_dma[dc] = wdma(lambda g, _d=dc, _sl=sl: g.dma_start(
                out=SCR[_sl][:, :], in_=x2d.ap()[_d * 128:(_d + 1) * 128, :],
                accum_op=ALU.add),
                waits=[(sDVE, t2)])
            out_dma[dc] = xdma(lambda s, _d=dc, _sl=sl: s.dma_start(
                out=outc[_d * 128:(_d + 1) * 128, :], in_=SCR[_sl][:, :]),
                waits=[acc_dma[dc]])

        p.wait("sync", xall() + [(s, s.n) for s in HL] + [(s, s.n) for s in WL])
        p.wait("gpsimd", xall())

        with nc.Block() as block:
            @block.tensor
            def _(eng):
                for f in p.ops["tensor"]:
                    f(eng)

            @block.scalar
            def _(eng):
                for f in p.ops["scalar"]:
                    f(eng)

            @block.vector
            def _(eng):
                for f in p.ops["vector"]:
                    f(eng)

            @block.gpsimd
            def _(eng):
                for f in p.ops["gpsimd"]:
                    f(eng)

            @block.sync
            def _(eng):
                for f in p.ops["sync"]:
                    f(eng)

    return nc


_NC_CACHE = None


def kernel(L0_attn_x, L0_attn_Wqkv, L0_attn_Wout, L0_ffn_w_gate, L0_ffn_w_up,
           L0_ffn_w_down):
    global _NC_CACHE, LAST_TAPS
    x = np.asarray(L0_attn_x, np.float32)
    WqkvT = np.ascontiguousarray(np.asarray(L0_attn_Wqkv, np.float32).T)
    WoutT = np.ascontiguousarray(np.asarray(L0_attn_Wout, np.float32).T)
    WgT = np.ascontiguousarray(np.asarray(L0_ffn_w_gate, np.float32).T)
    WuT = np.ascontiguousarray(np.asarray(L0_ffn_w_up, np.float32).T)
    WdT = np.ascontiguousarray(np.asarray(L0_ffn_w_down, np.float32).T)

    if DEBUG:
        nc = build_nc()
    else:
        global _NC_CACHE
        if _NC_CACHE is None:
            _NC_CACHE = build_nc()
        nc = _NC_CACHE

    in_maps = []
    for c in range(8):
        b, s = c // 4, c % 4
        xcv = np.ascontiguousarray(x[b, s * T:(s + 1) * T, :].T)
        in_maps.append({"xc": xcv, "wqkvT": WqkvT, "woutT": WoutT,
                        "wgT": WgT, "wuT": WuT, "wdT": WdT})
    res = run_bass_kernel_spmd(nc, in_maps, core_ids=list(range(8)))
    if DEBUG:
        LAST_TAPS = res.results
    out = np.empty((2, S, DM), np.float32)
    for c in range(8):
        b, s = c // 4, c % 4
        out[b, s * T:(s + 1) * T, :] = res.results[c]["outc"].T
    return out


# version 22
# speedup vs baseline: 1.1472x; 1.0231x over previous
import numpy as np
import concourse.bass as bass
import concourse.mybir as mybir
from concourse.bass_utils import run_bass_kernel_spmd
from contextlib import ExitStack

F32 = mybir.dt.float32
BF16 = mybir.dt.bfloat16
AF = mybir.ActivationFunctionType
ALU = mybir.AluOpType
AX = mybir.AxisListType

DM = 2048
NDC = 16
T = 1024
S = 4096
NKC = 32
H = 16
HID = 8192
EQKV = 6144
EPS = 1e-5
SCL = float(1.0 / np.sqrt(128.0))
SLC = 256

ENGINES = ("tensor", "scalar", "vector", "gpsimd", "sync")

DEBUG = False
LAST_TAPS = None


class Sem:
    __slots__ = ("name", "n", "h")

    def __init__(self, name):
        self.name = name
        self.n = 0
        self.h = None

    def inc(self, k=1):
        self.n += k
        return self.n


class Prog:
    def __init__(self):
        self.ops = {e: [] for e in ENGINES}

    def op(self, eng, build, waits=(), inc=None, inc_n=1):
        waits = [(s, v) for (s, v) in waits]
        if inc is not None:
            inc.inc(inc_n)

        def emit(engine, _b=build, _w=waits, _i=inc, _n=inc_n):
            for (s, v) in _w:
                engine.wait_ge(s.h, v)
            ins = _b(engine)
            if _i is not None:
                ins.then_inc(_i.h, _n)
        self.ops[eng].append(emit)

    def wait(self, eng, waits):
        waits = [(s, v) for (s, v) in waits]

        def emit(engine, _w=waits):
            for (s, v) in _w:
                engine.wait_ge(s.h, v)
        self.ops[eng].append(emit)


def build_nc():
    nc = bass.Bass("TRN2", target_bir_lowering=False)

    xc = nc.dram_tensor("xc", [DM, T], F32, kind="ExternalInput").ap()
    wqkvT = nc.dram_tensor("wqkvT", [DM, EQKV], F32, kind="ExternalInput").ap()
    woutT = nc.dram_tensor("woutT", [DM, DM], F32, kind="ExternalInput").ap()
    wgT = nc.dram_tensor("wgT", [DM, HID], F32, kind="ExternalInput").ap()
    wuT = nc.dram_tensor("wuT", [DM, HID], F32, kind="ExternalInput").ap()
    wdT = nc.dram_tensor("wdT", [HID, DM], F32, kind="ExternalInput").ap()
    outc = nc.dram_tensor("outc", [DM, T], F32, kind="ExternalOutput").ap()

    taps = {}
    if DEBUG:
        def mktap(name, shape, dt):
            taps[name] = nc.dram_tensor(name, shape, dt, kind="ExternalOutput").ap()
        mktap("tap_xh1", [128, NDC * T], BF16)
        mktap("tap_vout", [T, DM], BF16)
        mktap("tap_kin", [4 * DM, T], BF16)
        mktap("tap_vin", [4 * T, DM], BF16)
        mktap("tap_q", [DM, T], BF16)
        mktap("tap_attn", [128, NDC * T], BF16)
        mktap("tap_y", [128, NDC * T], F32)
        mktap("tap_x2", [DM, T], F32)
        mktap("tap_xh3", [128, NDC * T], BF16)
        mktap("tap_y2", [128, NDC * T], F32)

    kout = nc.dram_tensor("kout", [DM, T], BF16, kind="Internal")
    qout = nc.dram_tensor("qout", [DM, T], BF16, kind="Internal")
    vout = nc.dram_tensor("vout", [T, DM], BF16, kind="Internal")
    kin = nc.dram_tensor("kin", [4 * DM, T], BF16, kind="Internal")
    vin = nc.dram_tensor("vin", [4 * T, DM], BF16, kind="Internal")
    x2d = nc.dram_tensor("x2d", [DM, T], F32, kind="Internal")

    p = Prog()
    stk = ExitStack()
    sbt = lambda name, shape, dt: stk.enter_context(nc.sbuf_tensor(name, shape, dt))

    with stk:
        A = sbt("A", [128, NDC * T], F32)
        B = sbt("B", [128, NDC * T], BF16)
        D = sbt("D", [128, 2 * 8192], BF16)
        WB = [sbt(f"WB{i}", [128, NDC * SLC], BF16) for i in range(3)]
        QS = [sbt(f"QS{i}", [128, T], BF16) for i in range(2)]
        SG = [sbt(f"SG{i}", [128, T], BF16) for i in range(2)]
        XIO = [sbt(f"XIO{i}", [128, T], F32) for i in range(2)]
        SCR = [sbt(f"SCR{i}", [128, T], F32) for i in range(3)]
        KST = [sbt(f"KST{i}", [128, T], BF16) for i in range(2)]
        RB = [sbt(f"RB{i}", [128, 512], F32) for i in range(2)]
        SM = sbt("SM", [1, 4096], F32)
        ONE_F = sbt("ONE_F", [128, 1], F32)
        ONE_B = sbt("ONE_B", [128, 1], BF16)
        ONE_R = sbt("ONE_R", [1, 128], F32)
        BIAS = sbt("BIAS", [128, 1], F32)
        CST = sbt("CST", [1, 2], F32)
        SM_S1 = SM[0:1, 0:1024]
        SM_S2 = SM[0:1, 1024:2048]
        SM_M = SM[0:1, 2048:3072]
        SM_MSQ = SM[0:1, 0:1024]
        SM_VAR = SM[0:1, 3072:4096]
        SM_SUM = [SM[0:1, 512 * i:512 * (i + 1)] for i in range(2)]
        SM_REC = [SM[0:1, 1024 + 512 * i:1024 + 512 * (i + 1)] for i in range(2)]
        SM_KMX = SM[0:1, 2048:2064]
        SM_QMX = SM[0:1, 2064:2080]
        SM_SC = SM[0:1, 2080:2096]

        PSL = [stk.enter_context(nc.psum_tensor(f"PSL{i}", [128, 1024], F32))
               for i in range(2)]
        PSO = [stk.enter_context(nc.psum_tensor(f"PSO{i}", [128, 512], F32))
               for i in range(2)]
        PSS = [stk.enter_context(nc.psum_tensor(f"PSS{i}", [128, 512], F32))
               for i in range(2)]
        STAT = [PSO[0][0:1, :], PSO[1][0:1, :], PSS[0][0:1, :], PSS[1][0:1, :]]

        sPE = Sem("sPE")
        sACT = Sem("sACT")
        sDVE = Sem("sDVE")
        sPOOL = Sem("sPOOL")
        sCC = Sem("sCC")
        XL = [Sem(f"xl{i}") for i in range(8)]
        WL = [Sem(f"wl{i}") for i in range(8)]
        HL = [Sem(f"hl{i}") for i in range(4)]
        for s in (sPE, sACT, sDVE, sPOOL, sCC, *XL, *WL, *HL):
            s.h = stk.enter_context(nc.semaphore(s.name))

        xseq = [0]

        def xdma(build, waits=()):
            s = XL[xseq[0] % len(XL)]
            xseq[0] += 1
            p.op("sync", build, waits=list(waits) + [(s, s.n)], inc=s, inc_n=16)
            return (s, s.n)

        hseq = [0]

        def hdma(build, waits=()):
            s = HL[hseq[0] % len(HL)]
            hseq[0] += 1
            p.op("sync", build, waits=list(waits) + [(s, s.n)], inc=s, inc_n=16)
            return (s, s.n)

        wseq = [0]

        def wdma(build, waits=()):
            s = WL[wseq[0] % len(WL)]
            wseq[0] += 1
            p.op("gpsimd", build, waits=list(waits) + [(s, s.n)], inc=s, inc_n=16)
            return (s, s.n)

        def xall():
            return [(s, s.n) for s in XL]

        def mw(pairs):
            best = {}
            for (s, v) in pairs:
                if s not in best or v > best[s]:
                    best[s] = v
            return list(best.items())

        def barrier():
            marks = [(sPE, sPE.n), (sACT, sACT.n), (sDVE, sDVE.n),
                     (sPOOL, sPOOL.n)] + xall()
            for e in ("tensor", "scalar", "vector", "sync"):
                p.wait(e, marks)

        wb_state = {"seq": 0, "pe": []}

        def wslice(src, c0):
            i = wb_state["seq"]
            wb_state["seq"] += 1
            waits = []
            if i >= 3:
                waits.append(wb_state["pe"][i - 3])
            ap = src[:, c0:c0 + SLC].rearrange("(dc p) e -> p dc e", p=128)
            buf = WB[i % 3][:, :].rearrange("p (dc e) -> p dc e", e=SLC)
            sd = wdma(lambda g, _a=ap, _b=buf: g.dma_start(out=_b, in_=_a),
                      waits=waits)
            return i, sd

        def wslice_consumed(guard):
            wb_state["pe"].append(guard)

        wd_state = {"seq": 0, "g": [], "g0": None}

        def wdslice(g_, dc):
            i = wd_state["seq"]
            wd_state["seq"] += 1
            waits = []
            if i >= 4:
                waits.append(wd_state["g"][i - 4])
            elif i == 0:
                waits.append(wd_state["g0"])
            ap = wdT[g_ * 1024:(g_ + 1) * 1024, dc * 128:(dc + 1) * 128].rearrange(
                "(hc p) e -> p hc e", p=128)
            buf = D[:, 8192 + (i % 4) * 1024:8192 + (i % 4) * 1024 + 1024].rearrange(
                "p (hc e) -> p hc e", e=128)
            sd = wdma(lambda g, _a=ap, _b=buf: g.dma_start(out=_b, in_=_a),
                      waits=waits)
            return i, sd

        p.op("vector", lambda v: v.memset(ONE_F[:, :], 1.0))
        p.op("vector", lambda v: v.memset(ONE_B[:, :], 1.0))
        p.op("vector", lambda v: v.memset(ONE_R[:, :], 1.0))
        p.op("vector", lambda v: v.memset(CST[0:1, 0:1], EPS))
        p.op("vector", lambda v: v.memset(CST[0:1, 1:2], float(np.log(SCL))),
             inc=sDVE)
        ones_ready = sDVE.n

        ldx = {}
        for q4 in range(4):
            ldx[q4] = xdma(lambda s, _q=q4: s.dma_start(
                out=A[:, _q * 4 * T:(_q + 1) * 4 * T].rearrange("p (c t) -> p c t", t=T),
                in_=xc[_q * 512:(_q + 1) * 512, :].rearrange("(c p) t -> p c t", p=128)))

        ln1_pe = {}
        for dc in range(NDC):
            cw = [ldx[dc // 4]]
            if dc >= 2:
                cw.append((sPE, ln1_pe[dc - 2]))
            p.op("scalar", lambda s, _d=dc: s.activation(
                SG[_d % 2][:, :], A[:, _d * T:(_d + 1) * T], AF.Square),
                waits=cw, inc=sACT)
            sq = sACT.n
            st, sp = (dc == 0), (dc == NDC - 1)
            for n in range(2):
                p.op("tensor", lambda t, _d=dc, _n=n, _s=st, _e=sp: t.matmul(
                    STAT[_n], ONE_F[:, :],
                    A[:, _d * T + _n * 512:_d * T + (_n + 1) * 512], start=_s, stop=_e),
                    waits=([(sACT, sq), (sDVE, ones_ready)] if n == 0 else ()))
            for n in range(2):
                p.op("tensor", lambda t, _d=dc, _n=n, _s=st, _e=sp: t.matmul(
                    STAT[2 + _n], ONE_B[:, :],
                    SG[_d % 2][:, _n * 512:(_n + 1) * 512], start=_s, stop=_e),
                    inc=(sPE if n == 1 else None))
            ln1_pe[dc] = sPE.n

        def ln_junction():
            for j in range(2):
                p.op("vector", lambda v, _j=j: v.tensor_copy(
                    SM_S1[:, _j * 512:(_j + 1) * 512], STAT[_j]),
                    waits=([(sPE, sPE.n)] if j == 0 else ()))
            for j in range(2):
                p.op("vector", lambda v, _j=j: v.tensor_copy(
                    SM_S2[:, _j * 512:(_j + 1) * 512], STAT[2 + _j]),
                    inc=(sDVE if j == 1 else None))
            p.op("vector", lambda v: v.tensor_scalar_mul(SM_M[:, :], SM_S1[:, :], 1.0 / DM),
                 waits=[(sDVE, sDVE.n)], inc=sDVE)
            p.op("vector", lambda v: v.tensor_mul(SM_MSQ[:, :], SM_M[:, :], SM_M[:, :]),
                 waits=[(sDVE, sDVE.n)], inc=sDVE)
            p.op("vector", lambda v: v.scalar_tensor_tensor(
                SM_VAR[:, :], SM_S2[:, :], 1.0 / DM, SM_MSQ[:, :], ALU.mult, ALU.subtract),
                waits=[(sDVE, sDVE.n)], inc=sDVE)
            mres = sDVE.n
            p.op("scalar", lambda s: s.activation(SM_MSQ[:, :], SM_VAR[:, :], AF.Ln,
                                                  bias=CST[0:1, 0:1]),
                 waits=[(sDVE, mres)], inc=sACT)
            p.op("scalar", lambda s: s.activation(SM_VAR[:, :], SM_MSQ[:, :], AF.Exp,
                                                  scale=-0.5),
                 waits=[(sACT, sACT.n)], inc=sACT)
            rres = sACT.n
            for n in range(2):
                p.op("tensor", lambda t, _n=n: t.matmul(
                    PSL[0][:, _n * 512:(_n + 1) * 512], ONE_R[:, :],
                    SM_M[:, _n * 512:(_n + 1) * 512], start=True, stop=True),
                    waits=([(sDVE, mres)] if n == 0 else ()))
            for n in range(2):
                p.op("tensor", lambda t, _n=n: t.matmul(
                    PSL[1][:, _n * 512:(_n + 1) * 512], ONE_R[:, :],
                    SM_VAR[:, _n * 512:(_n + 1) * 512], start=True, stop=True),
                    waits=([(sACT, rres)] if n == 0 else ()),
                    inc=(sPE if n == 1 else None))
            bc = sPE.n
            p.op("vector", lambda v: v.tensor_copy(XIO[0][:, :], PSL[0][:, :]),
                 waits=[(sPE, bc)], inc=sDVE)
            p.op("vector", lambda v: v.tensor_copy(XIO[1][:, :], PSL[1][:, :]),
                 inc=sDVE)
            return sDVE.n

        bc1 = ln_junction()

        PT = [D[:, 2048 * i:2048 * (i + 1)].bitcast(F32) for i in range(2)]
        norm_done = {}
        last_norm = {"vector": None, "gpsimd": None}
        for dc in range(NDC):
            pooled = (dc % 3 == 2)
            eng, sem = ("gpsimd", sPOOL) if pooled else ("vector", sDVE)
            scr = PT[0] if pooled else SCR[1][:, :]
            w = [(sDVE, bc1)]
            if last_norm[eng] is not None:
                w.append(last_norm[eng])
            p.op(eng, lambda v, _d=dc, _s=scr: v.tensor_tensor(
                _s, A[:, _d * T:(_d + 1) * T], XIO[0][:, :], ALU.subtract),
                waits=w, inc=sem)
            t1 = sem.n
            p.op(eng, lambda v, _d=dc, _s=scr: v.tensor_tensor(
                B[:, _d * T:(_d + 1) * T], _s, XIO[1][:, :], ALU.mult),
                waits=[(sem, t1)], inc=sem)
            norm_done[dc] = (sem, sem.n)
            last_norm[eng] = norm_done[dc]
        xh1_done = sDVE.n
        xh1_pool = sPOOL.n
        barrier()
        p.wait("tensor", [(sPOOL, xh1_pool)])

        dbg_attn_gate = None
        if DEBUG:
            dbg_attn_gate = xdma(lambda s: s.dma_start(out=taps["tap_xh1"], in_=B[:, :]),
                                 waits=[(sDVE, xh1_done)])

        kq = {"i": 0, "evt": {}, "pend": None}

        def kq_bounds_flush():
            if kq["pend"] is None:
                return
            i, par, sq, slot = kq["pend"]
            kq["pend"] = None
            bmw = [(sACT, sq)]
            if i >= 2:
                bmw.append((sDVE, kq["evt"][i - 2][4]))
            for n in range(2):
                p.op("tensor", lambda t, _n=n, _par=par: t.matmul(
                    PSS[_par][0:1, :], ONE_B[:, :],
                    SG[_par][:, _n * 512:(_n + 1) * 512],
                    start=(_n == 0), stop=(_n == 1)),
                    waits=(bmw if n == 0 else ()), inc=(sPE if n == 1 else None))
            bm = sPE.n
            p.op("vector", lambda v, _par=par, _slot=slot, _i=i: v.reduce_max(
                _slot[:, _i % 16:_i % 16 + 1], PSS[_par][0:1, :], axis=AX.X),
                waits=[(sPE, bm)], inc=sDVE)
            ev, sq_, st_, _, _ = kq["evt"][i]
            kq["evt"][i] = (ev, sq_, st_, bm, sDVE.n)

        def kq_chunk(eh, wb_i, ecl, slice_done, dest_dram, bounds_slot,
                     extra_first=(), extra_evac=()):
            i = kq["i"]
            kq["i"] += 1
            par = i % 2
            w0 = [slice_done, (sDVE, xh1_done)] + list(extra_first)
            if i >= 2:
                w0.append((sDVE, kq["evt"][i - 2][0]))
            for dc in range(NDC):
                for n in range(2):
                    p.op("tensor", lambda t, _dc=dc, _n=n, _w=wb_i, _ecl=ecl,
                         _par=par, _s=(dc == 0), _e=(dc == NDC - 1): t.matmul(
                        PSL[_par][:, _n * 512:(_n + 1) * 512],
                        WB[_w % 3][:, _dc * SLC + _ecl * 128:_dc * SLC + (_ecl + 1) * 128],
                        B[:, _dc * T + _n * 512:_dc * T + (_n + 1) * 512],
                        start=_s, stop=_e),
                        waits=(w0 if (dc == 0 and n == 0) else ()),
                        inc=(sPE if (dc == NDC - 1 and n == 1) else None))
            mm = sPE.n
            kq_bounds_flush()
            kst_w = [(sPE, mm)] + list(extra_evac)
            if i >= 2:
                kst_w += [(sACT, kq["evt"][i - 2][1]), kq["evt"][i - 2][2]]
            for n in range(2):
                p.op("vector", lambda v, _n=n, _par=par: v.tensor_copy(
                    KST[_par][:, _n * 512:(_n + 1) * 512],
                    PSL[_par][:, _n * 512:(_n + 1) * 512]),
                    waits=(kst_w if n == 0 else ()), inc=(sDVE if n == 1 else None))
            ev = sDVE.n
            bw = [(sDVE, ev)]
            if i >= 2:
                bw.append((sPE, kq["evt"][i - 2][3]))
            p.op("scalar", lambda s, _par=par: s.activation(
                SG[_par][:, :], KST[_par][:, :], AF.Square), waits=bw, inc=sACT)
            sq = sACT.n
            st_ = xdma(lambda s, _par=par, _eh=eh, _dst=dest_dram: s.dma_start(
                out=_dst.ap()[_eh * 128:(_eh + 1) * 128, :], in_=KST[_par][:, :]),
                waits=[(sDVE, ev)])
            kq["evt"][i] = (ev, sq, st_, sPE.n, sDVE.n)
            kq["pend"] = (i, par, sq, bounds_slot)
            return mm

        for si in range(8):
            wb_i, sd = wslice(wqkvT, 2048 + si * SLC)
            mm_last = 0
            for ecl in range(2):
                mm_last = kq_chunk(si * 2 + ecl, wb_i, ecl, sd, kout, SM_KMX)
            wslice_consumed((sPE, mm_last))
        kq_bounds_flush()
        k_stage_all = xall()
        k_sq_end = kq["evt"][15][1]
        p.op("gpsimd", lambda g: g.collective_compute(
            "AllGather", ALU.bypass, replica_groups=[[0, 1, 2, 3], [4, 5, 6, 7]],
            ins=[kout.ap().opt()], outs=[kin.ap().opt()]),
            waits=k_stage_all, inc=sCC)
        cc_k = sCC.n

        VPS = [PSO[0], PSO[1], PSS[0], PSS[1]]
        v_i = 0
        v_evt = {}
        v_stage = {}
        v_first_w = k_stage_all + [(sACT, k_sq_end), (sDVE, sDVE.n)]
        for si in range(8):
            wb_i, sd = wslice(wqkvT, 4096 + si * SLC)
            mm_last = 0
            for tc in range(8):
                i = v_i
                v_i += 1
                r4 = i % 4
                kcol = r4 * SLC
                w0 = [sd]
                if i >= 4:
                    w0.append((sDVE, v_evt[i - 4]))
                elif i == 0:
                    w0 += v_first_w
                for dc in range(NDC):
                    p.op("tensor", lambda t, _dc=dc, _tc=tc, _r=r4, _w=wb_i,
                         _s=(dc == 0), _e=(dc == NDC - 1): t.matmul(
                        VPS[_r][:, 0:SLC],
                        B[:, _dc * T + _tc * 128:_dc * T + (_tc + 1) * 128],
                        WB[_w % 3][:, _dc * SLC:(_dc + 1) * SLC],
                        start=_s, stop=_e),
                        waits=(w0 if dc == 0 else ()),
                        inc=(sPE if dc == NDC - 1 else None))
                mm_last = sPE.n
                evw = [(sPE, mm_last)]
                if i >= 8:
                    evw.append(v_stage[(i - 8) // 4])
                p.op("vector", lambda v, _i=i: v.tensor_copy(
                    KST[(_i // 4) % 2][:, (_i % 4) * SLC:(_i % 4) * SLC + SLC],
                    VPS[_i % 4][:, 0:SLC]),
                    waits=mw(evw), inc=sDVE)
                v_evt[i] = sDVE.n
                if i % 4 == 3:
                    v_stage[i // 4] = xdma(
                        lambda s, _i=i, _tc=tc, _si=si: s.dma_start(
                            out=vout.ap()[(_tc - 3) * 128:(_tc + 1) * 128,
                                          _si * SLC:(_si + 1) * SLC].rearrange(
                                "(c p) v -> p c v", p=128),
                            in_=KST[(_i // 4) % 2][:, :].rearrange(
                                "p (c v) -> p c v", v=SLC)),
                        waits=[(sDVE, v_evt[i])])
            wslice_consumed((sPE, mm_last))
        v_stage_all = xall()
        v_evt_end = sDVE.n

        head_dma = {}
        head_vdma = {}
        head_pe_done = {}

        head_kdma = {}

        def head_k_prefetch(h):
            ph = h % 2
            base = ph * 8192
            waits = [(sCC, cc_k)]
            if h >= 2:
                waits.append((sPE, head_pe_done[h - 2]))
            kpairs = []
            for r in range(4):
                kpairs.append(hdma(lambda s, _h=h, _r=r, _b=base: s.dma_start(
                    out=D[:, _b + _r * T:_b + (_r + 1) * T],
                    in_=kin.ap()[_r * 2048 + _h * 128:_r * 2048 + (_h + 1) * 128, :]),
                    waits=(waits if r == 0 else ())))
            head_kdma[h] = kpairs

        def head_loads(h, with_qs=True):
            ph = h % 2
            base = ph * 8192
            if h not in head_kdma:
                head_k_prefetch(h)
            pairs = list(head_kdma[h])
            if with_qs:
                pairs.append(qs_load(h))
            vpairs = []
            vw = [(sCC, cc_v)]
            for r in range(4):
                vpairs.append(hdma(lambda s, _h=h, _r=r, _b=base: s.dma_start(
                    out=D[:, _b + 4096 + _r * T:_b + 4096 + (_r + 1) * T].rearrange(
                        "p (c v) -> p c v", v=128),
                    in_=vin.ap()[_r * T:(_r + 1) * T, _h * 128:(_h + 1) * 128].rearrange(
                        "(c p) v -> p c v", p=128)),
                    waits=(vw if r == 0 else ())))
            head_dma[h] = mw(pairs)
            head_vdma[h] = mw(vpairs)

        def qs_load(h):
            qw = [kq["evt"][16 + h][2]]
            if h >= 2:
                qw.append((sPE, head_pe_done[h - 2]))
            return hdma(lambda s, _h=h, _ph=h % 2: s.dma_start(
                out=QS[_ph][:, :], in_=qout.ap()[_h * 128:(_h + 1) * 128, :]),
                waits=qw)

        head_k_prefetch(0)
        head_k_prefetch(1)

        q_first_evac = [True]
        for si in range(8):
            wb_i, sd = wslice(wqkvT, 0 + si * SLC)
            if si == 0:
                p.op("gpsimd", lambda g: g.collective_compute(
                    "AllGather", ALU.bypass,
                    replica_groups=[[0, 1, 2, 3], [4, 5, 6, 7]],
                    ins=[vout.ap().opt()], outs=[vin.ap().opt()]),
                    waits=v_stage_all, inc=sCC)
                cc_v = sCC.n
            mm_last = 0
            for ecl in range(2):
                first = (si == 0 and ecl == 0)
                mm_last = kq_chunk(si * 2 + ecl, wb_i, ecl, sd, qout, SM_QMX,
                                   extra_first=([(sDVE, v_evt_end)] if first else ()),
                                   extra_evac=(mw(v_stage_all) if si == 0 else ()))
            wslice_consumed((sPE, mm_last))
        kq_bounds_flush()
        qproj_pe_done = sPE.n
        if DEBUG:
            xdma(lambda s: s.dma_start(out=taps["tap_vout"], in_=vout.ap()[:, :]),
                 waits=[(sCC, cc_v)])
            xdma(lambda s: s.dma_start(out=taps["tap_kin"], in_=kin.ap()[:, :]))
            xdma(lambda s: s.dma_start(out=taps["tap_vin"], in_=vin.ap()[:, :]))
            xdma(lambda s: s.dma_start(out=taps["tap_q"], in_=qout.ap()[:, :]),
                 waits=xall())

        p.op("vector", lambda v: v.reduce_max(SM_SC[:, 0:1], SM_KMX[:, :], axis=AX.X),
             waits=[(sDVE, sDVE.n)], inc=sDVE)
        p.op("vector", lambda v: v.reduce_max(SM_SC[:, 1:2], SM_QMX[:, :], axis=AX.X),
             waits=[(sDVE, sDVE.n)], inc=sDVE)
        p.op("vector", lambda v: v.tensor_mul(SM_SC[:, 2:3], SM_SC[:, 0:1], SM_SC[:, 1:2]),
             waits=[(sDVE, sDVE.n)], inc=sDVE)
        p.op("scalar", lambda s: s.activation(SM_SC[:, 3:4], SM_SC[:, 2:3], AF.Ln),
             waits=[(sDVE, sDVE.n)], inc=sACT)
        p.op("scalar", lambda s: s.activation(SM_SC[:, 4:5], SM_SC[:, 3:4], AF.Exp,
                                              scale=0.5, bias=CST[0:1, 1:2]),
             waits=[(sACT, sACT.n)], inc=sACT)
        p.op("vector", lambda v: v.tensor_scalar(SM_SC[:, 5:6], SM_SC[:, 4:5],
                                                 1.01, 1.0, ALU.mult, ALU.add),
             waits=[(sACT, sACT.n)], inc=sDVE)
        p.op("tensor", lambda t: t.matmul(PSS[0][:, 0:1], ONE_R[:, :], SM_SC[:, 5:6],
                                          start=True, stop=True),
             waits=[(sDVE, sDVE.n), (sPE, qproj_pe_done)], inc=sPE)
        p.op("vector", lambda v: v.tensor_scalar_mul(BIAS[:, :], PSS[0][:, 0:1], -1.0),
             waits=[(sPE, sPE.n)], inc=sDVE)
        bias_ready = sDVE.n
        barrier()

        np_done = {0: None, 1: None}
        head_loads(0)
        head_loads(1)

        exp_done = {}
        pvpair_done = {}
        pend_ep = [None]

        def emit_epilogue():
            if pend_ep[0] is None:
                return
            h_, n_, npi, acc = pend_ep[0]
            pend_ep[0] = None
            p.op("vector", lambda v, _np=npi: v.tensor_copy(
                SM_SUM[_np][:, :], PSS[_np][0:1, :]),
                waits=[(sPE, acc)], inc=sDVE)
            p.op("vector", lambda v, _np=npi: v.reciprocal(
                SM_REC[_np][:, :], SM_SUM[_np][:, :]),
                waits=[(sDVE, sDVE.n)], inc=sDVE)
            rec_ = sDVE.n
            p.op("tensor", lambda t, _np=npi: t.matmul(
                PSS[_np][:, :], ONE_R[:, :], SM_REC[_np][:, :],
                start=True, stop=True),
                waits=[(sDVE, rec_)], inc=sPE)
            p.op("vector", lambda v, _np=npi: v.tensor_copy(RB[_np][:, :], PSS[_np][:, :]),
                 waits=[(sPE, sPE.n)], inc=sDVE)
            p.op("vector", lambda v, _np=npi, _h=h_, _n=n_: v.tensor_tensor(
                B[:, _h * T + _n * 512:_h * T + (_n + 1) * 512],
                PSO[_np][:, :], RB[_np][:, :], ALU.mult),
                waits=[(sDVE, sDVE.n)], inc=sDVE)
            np_done[npi] = sDVE.n

        NPAIR = NKC // 2
        allpairs = [(h, n, j) for h in range(H) for n in range(2)
                    for j in range(NPAIR)]
        qk_pe = {}

        def emit_qk_pair(pp):
            h, n, j = allpairs[pp]
            ph = h % 2
            base = ph * 8192
            qk_w = []
            if j == 0 and n == 0:
                qk_w.extend(head_dma[h])
                if h == 0:
                    qk_w.append((sDVE, bias_ready))
            if pp >= 2:
                qk_w.append((sACT, exp_done[pp - 2]))
            for half in range(2):
                kc = 2 * j + half
                p_op("tensor", lambda t, _kc=kc, _b=base, _ph=ph, _n=n,
                     _g=pp: t.matmul(
                    PSL[_g % 2][:, (_kc % 2) * 512:(_kc % 2) * 512 + 512],
                    D[:, _b + _kc * 128:_b + (_kc + 1) * 128],
                    QS[_ph][:, _n * 512:(_n + 1) * 512],
                    start=True, stop=True),
                    waits=(qk_w if half == 0 else ()),
                    inc=(sPE if half == 1 else None))
            qk_pe[pp] = sPE.n

        def emit_exp(pp):
            ew = [(sPE, qk_pe[pp]), (sPE, pvpair_done.get(pp - 2, 0))]
            p_op("scalar", lambda s, _g=pp: s.activation(
                SG[_g % 2][:, :], PSL[_g % 2][:, :], AF.Exp,
                bias=BIAS[:, :], scale=SCL),
                waits=mw(ew), inc=sACT)
            exp_done[pp] = sACT.n

        def emit_sum_pv(pp):
            h, n, j = allpairs[pp]
            npi = (2 * h + n) % 2
            base = (h % 2) * 8192
            sw = [(sACT, exp_done[pp])]
            if j == 0 and np_done[npi] is not None:
                sw.append((sDVE, np_done[npi]))
            if j == 0 and n == 0:
                sw.extend(head_vdma[h])
            for half in range(2):
                p_op("tensor", lambda t, _np=npi, _g=pp, _h2=half,
                     _s=(j == 0 and half == 0),
                     _e=(j == NPAIR - 1 and half == 1): t.matmul(
                    PSS[_np][0:1, :], ONE_B[:, :],
                    SG[_g % 2][:, _h2 * 512:(_h2 + 1) * 512],
                    start=_s, stop=_e),
                    waits=(sw if half == 0 else ()))
            for half in range(2):
                kc = 2 * j + half
                p_op("tensor", lambda t, _kc=kc, _b=base, _np=npi, _g=pp,
                     _h2=half, _s=(j == 0 and half == 0),
                     _e=(j == NPAIR - 1 and half == 1): t.matmul(
                    PSO[_np][:, :],
                    D[:, _b + 4096 + _kc * 128:_b + 4096 + (_kc + 1) * 128],
                    SG[_g % 2][:, _h2 * 512:(_h2 + 1) * 512],
                    start=_s, stop=_e),
                    inc=(sPE if half == 1 else None))
            pvpair_done[pp] = sPE.n
            if j == NPAIR - 1:
                pend_ep[0] = (h, n, npi, sPE.n)
                if n == 1:
                    head_pe_done[h] = sPE.n
                    if h + 2 < H:
                        head_loads(h + 2)

        p_op = p.op
        for pp in range(len(allpairs)):
            emit_qk_pair(pp)
            if pp >= 2:
                emit_sum_pv(pp - 2)
            h, n, j = allpairs[pp]
            if j == 6:
                emit_epilogue()
            emit_exp(pp)
        emit_sum_pv(len(allpairs) - 2)
        emit_sum_pv(len(allpairs) - 1)
        emit_epilogue()
        attn_done = sDVE.n
        barrier()

        dbg_x2_gate = None
        if DEBUG:
            dbg_x2_gate = xdma(lambda s: s.dma_start(out=taps["tap_attn"], in_=B[:, :]),
                               waits=[(sDVE, attn_done)])

        y_evt = {}
        ln2_pend = [None]
        ln2_pe = {}

        SQB = [SG[0], SG[1], KST[0], KST[1]]
        XB = [D[:, 8192 + 1024 * i:8192 + 1024 * (i + 1)] for i in range(4)]

        def stat_flush(pend, pe_map):
            if pend[0] is None:
                return
            dc, sq = pend[0]
            pend[0] = None
            st, sp = (dc == 0), (dc == NDC - 1)
            for n in range(2):
                p.op("tensor", lambda t, _dc=dc, _n=n, _s=st, _e=sp: t.matmul(
                    STAT[_n], ONE_B[:, :],
                    XB[_dc % 4][:, _n * 512:(_n + 1) * 512],
                    start=_s, stop=_e),
                    waits=([(sACT, sq)] if n == 0 else ()))
            for n in range(2):
                p.op("tensor", lambda t, _dc=dc, _n=n, _s=st, _e=sp: t.matmul(
                    STAT[2 + _n], ONE_B[:, :],
                    SQB[_dc % 4][:, _n * 512:(_n + 1) * 512], start=_s, stop=_e),
                    inc=(sPE if n == 1 else None))
            pe_map[dc] = sPE.n

        for si in range(8):
            wb_i, sd = wslice(woutT, si * SLC)
            mm_last = 0
            for dcl in range(2):
                dc = si * 2 + dcl
                par = dc % 2
                w0 = [sd, (sDVE, attn_done)]
                if dc >= 2:
                    w0.append((sDVE, y_evt[dc - 2]))
                for ec in range(NDC):
                    for n in range(2):
                        p.op("tensor", lambda t, _ec=ec, _n=n, _dcl=dcl, _w=wb_i,
                             _par=par, _s=(ec == 0), _e=(ec == NDC - 1): t.matmul(
                            PSL[_par][:, _n * 512:(_n + 1) * 512],
                            WB[_w % 3][:, _ec * SLC + _dcl * 128:_ec * SLC + (_dcl + 1) * 128],
                            B[:, _ec * T + _n * 512:_ec * T + (_n + 1) * 512],
                            start=_s, stop=_e),
                            waits=(w0 if (ec == 0 and n == 0) else ()),
                            inc=(sPE if (ec == NDC - 1 and n == 1) else None))
                mm_last = sPE.n
                stat_flush(ln2_pend, ln2_pe)
                for n in range(2):
                    p.op("vector", lambda v, _dc=dc, _n=n, _par=par: v.tensor_copy(
                        A[:, _dc * T + _n * 512:_dc * T + (_n + 1) * 512],
                        PSL[_par][:, _n * 512:(_n + 1) * 512]),
                        waits=([(sPE, mm_last)] if n == 0 else ()),
                        inc=(sDVE if n == 1 else None))
                y_evt[dc] = sDVE.n
                cw = [(sDVE, y_evt[dc])]
                if dc >= 4:
                    cw.append((sPE, ln2_pe[dc - 4]))
                p.op("scalar", lambda s, _dc=dc: s.activation(
                    SQB[_dc % 4][:, :], A[:, _dc * T:(_dc + 1) * T], AF.Square),
                    waits=cw, inc=sACT)
                p.op("scalar", lambda s, _dc=dc: s.copy(
                    XB[_dc % 4][:, :], A[:, _dc * T:(_dc + 1) * T]),
                    inc=sACT)
                ln2_pend[0] = (dc, sACT.n)
            wslice_consumed((sPE, mm_last))
        stat_flush(ln2_pend, ln2_pe)
        barrier()
        if DEBUG:
            dbg_y_gate = xdma(lambda s: s.dma_start(out=taps["tap_y"], in_=A[:, :]),
                              waits=[(sDVE, sDVE.n)])
        bc2 = ln_junction()

        PT2 = [D[:, 4096 + 2048 * i:4096 + 2048 * (i + 1)].bitcast(F32)
               for i in range(2)]
        x2_evt = {}
        last_x2 = {"vector": None, "gpsimd": None}
        ln3_pend = [None]
        ln3_pe = {}
        for dc in range(NDC):
            pooled = (dc % 3 == 2)
            eng, sem = ("gpsimd", sPOOL) if pooled else ("vector", sDVE)
            s1 = PT[0] if pooled else SCR[1][:, :]
            s2 = PT[1] if pooled else SCR[2][:, :]
            s0 = PT2[0] if pooled else SCR[0][:, :]
            waits = [(sDVE, bc2)]
            if last_x2[eng] is not None:
                waits = waits + [last_x2[eng]]
            if DEBUG and dc == 0:
                waits = waits + [dbg_y_gate, dbg_x2_gate]
            ld = xdma(lambda s, _d=dc, _b=s0: s.dma_start(
                out=_b, in_=xc[_d * 128:(_d + 1) * 128, :]),
                waits=waits)
            w = [ld, (sDVE, bc2)]
            if last_x2[eng] is not None:
                w.append(last_x2[eng])
            p.op(eng, lambda v, _d=dc, _s=s1: v.tensor_tensor(
                _s, A[:, _d * T:(_d + 1) * T], XIO[0][:, :], ALU.subtract),
                waits=w, inc=sem)
            t1 = sem.n
            p.op(eng, lambda v, _a=s1, _b=s2: v.tensor_tensor(
                _b, _a, XIO[1][:, :], ALU.mult),
                waits=[(sem, t1)], inc=sem)
            t2 = sem.n
            p.op(eng, lambda v, _d=dc, _b=s2, _x=s0: v.tensor_tensor(
                A[:, _d * T:(_d + 1) * T], _x, _b, ALU.add),
                waits=[(sem, t2)], inc=sem)
            x2v = (sem, sem.n)
            last_x2[eng] = x2v
            xdma(lambda s, _d=dc: s.dma_start(
                out=x2d.ap()[_d * 128:(_d + 1) * 128, :], in_=A[:, _d * T:(_d + 1) * T]),
                waits=[x2v])
            cw = [x2v]
            if dc >= 4:
                cw.append((sPE, ln3_pe.get(dc - 4, 0)))
            p.op("scalar", lambda s, _dc=dc: s.activation(
                SQB[_dc % 4][:, :], A[:, _dc * T:(_dc + 1) * T], AF.Square),
                waits=cw, inc=sACT)
            p.op("scalar", lambda s, _dc=dc: s.copy(
                XB[_dc % 4][:, :], A[:, _dc * T:(_dc + 1) * T]),
                inc=sACT)
            sq = sACT.n
            stat_flush(ln3_pend, ln3_pe)
            ln3_pend[0] = (dc, sq)
            x2_evt[dc] = x2v
        stat_flush(ln3_pend, ln3_pe)
        barrier()
        bc3 = ln_junction()
        xh3_evt = {}
        last_x3 = {"vector": None, "gpsimd": None}
        for dc in range(NDC):
            pooled = (dc % 3 == 2)
            eng, sem = ("gpsimd", sPOOL) if pooled else ("vector", sDVE)
            scr = PT[0] if pooled else SCR[1][:, :]
            w = [(sDVE, bc3)]
            if last_x3[eng] is not None:
                w.append(last_x3[eng])
            p.op(eng, lambda v, _d=dc, _s=scr: v.tensor_tensor(
                _s, A[:, _d * T:(_d + 1) * T], XIO[0][:, :], ALU.subtract),
                waits=w, inc=sem)
            t1 = sem.n
            p.op(eng, lambda v, _d=dc, _s=scr: v.tensor_tensor(
                B[:, _d * T:(_d + 1) * T], _s, XIO[1][:, :], ALU.mult),
                waits=[(sem, t1)], inc=sem)
            xh3_evt[dc] = (sem, sem.n)
            last_x3[eng] = xh3_evt[dc]
        xh3_done = sDVE.n
        xh3_pool = sPOOL.n
        barrier()
        if DEBUG:
            xdma(lambda s: s.dma_start(out=taps["tap_x2"], in_=x2d.ap()[:, :]),
                 waits=xall())
            xdma(lambda s: s.dma_start(out=taps["tap_xh3"], in_=B[:, :]),
                 waits=[(sDVE, xh3_done)])

        wd_state["g0"] = (sPE, sPE.n)
        mult_hist = {}
        down_done_pe = 0
        dacc = {}
        for g in range(8):
            for si in range(4):
                wbg_i, sdg = wslice(wgT, g * 1024 + si * SLC)
                wbu_i, sdu = wslice(wuT, g * 1024 + si * SLC)
                gate_last = up_last = 0
                for hcl in range(2):
                    ahc = g * 8 + si * 2 + hcl
                    gw = [sdg, (sDVE, xh3_done), (sACT, sACT.n)]
                    if g > 0 and si == 0 and hcl == 0:
                        gw.append((sPE, down_done_pe))
                    for dc in range(NDC):
                        for n in range(2):
                            p.op("tensor", lambda t, _dc=dc, _n=n, _hcl=hcl, _w=wbg_i,
                                 _s=(dc == 0), _e=(dc == NDC - 1): t.matmul(
                                PSL[0][:, _n * 512:(_n + 1) * 512],
                                WB[_w % 3][:, _dc * SLC + _hcl * 128:_dc * SLC + (_hcl + 1) * 128],
                                B[:, _dc * T + _n * 512:_dc * T + (_n + 1) * 512],
                                start=_s, stop=_e),
                                waits=(gw if (dc == 0 and n == 0) else ()),
                                inc=(sPE if (dc == NDC - 1 and n == 1) else None))
                    gate_last = sPE.n
                    sil_w = [(sPE, gate_last)]
                    if ahc >= 2:
                        sil_w.append((sDVE, mult_hist[ahc - 2]))
                    p.op("scalar", lambda s, _a=ahc: s.activation(
                        SG[_a % 2][:, :], PSL[0][:, :], AF.Silu),
                        waits=sil_w, inc=sACT)
                    sil = sACT.n
                    uw = [sdu]
                    if ahc >= 1:
                        uw.append((sDVE, mult_hist[ahc - 1]))
                    for dc in range(NDC):
                        for n in range(2):
                            p.op("tensor", lambda t, _dc=dc, _n=n, _hcl=hcl, _w=wbu_i,
                                 _s=(dc == 0), _e=(dc == NDC - 1): t.matmul(
                                PSL[1][:, _n * 512:(_n + 1) * 512],
                                WB[_w % 3][:, _dc * SLC + _hcl * 128:_dc * SLC + (_hcl + 1) * 128],
                                B[:, _dc * T + _n * 512:_dc * T + (_n + 1) * 512],
                                start=_s, stop=_e),
                                waits=(uw if (dc == 0 and n == 0) else ()),
                                inc=(sPE if (dc == NDC - 1 and n == 1) else None))
                    up_last = sPE.n
                    for n in range(2):
                        p.op("vector", lambda v, _n=n, _a=ahc: v.tensor_tensor(
                            D[:, (_a % 8) * T + _n * 512:(_a % 8) * T + (_n + 1) * 512],
                            SG[_a % 2][:, _n * 512:(_n + 1) * 512],
                            PSL[1][:, _n * 512:(_n + 1) * 512], ALU.mult),
                            waits=([(sPE, up_last), (sACT, sil)] if n == 0 else ()),
                            inc=(sDVE if n == 1 else None))
                    mult_hist[ahc] = sDVE.n
                wslice_consumed((sPE, gate_last))
                wslice_consumed((sPE, up_last))
            gu_done = sDVE.n
            for dc in range(NDC):
                gdc = g * NDC + dc
                wd_i, sdd = wdslice(g, dc)
                par = dc % 2
                dw = [sdd, (sDVE, gu_done)]
                if gdc >= 2:
                    dw.append((sDVE, dacc[gdc - 2]))
                for hcl in range(8):
                    for n in range(2):
                        p.op("tensor", lambda t, _h=hcl, _n=n, _w=wd_i, _par=par,
                             _s=(hcl == 0), _e=(hcl == 7): t.matmul(
                            (PSO[_par] if _n == 0 else PSS[_par])[:, :],
                            D[:, 8192 + (_w % 4) * 1024 + _h * 128:
                               8192 + (_w % 4) * 1024 + (_h + 1) * 128],
                            D[:, _h * T + _n * 512:_h * T + (_n + 1) * 512],
                            start=_s, stop=_e),
                            waits=(dw if (hcl == 0 and n == 0) else ()),
                            inc=(sPE if (hcl == 7 and n == 1) else None))
                dmm = sPE.n
                wd_state["g"].append((sPE, dmm))
                for n in range(2):
                    if g == 0:
                        p.op("vector", lambda v, _dc=dc, _n=n, _par=par: v.tensor_copy(
                            A[:, _dc * T + _n * 512:_dc * T + (_n + 1) * 512],
                            (PSO[_par] if _n == 0 else PSS[_par])[:, :]),
                            waits=([(sPE, dmm)] if n == 0 else ()),
                            inc=(sDVE if n == 1 else None))
                    else:
                        p.op("vector", lambda v, _dc=dc, _n=n, _par=par: v.tensor_tensor(
                            A[:, _dc * T + _n * 512:_dc * T + (_n + 1) * 512],
                            A[:, _dc * T + _n * 512:_dc * T + (_n + 1) * 512],
                            (PSO[_par] if _n == 0 else PSS[_par])[:, :], ALU.add),
                            waits=([(sPE, dmm)] if n == 0 else ()),
                            inc=(sDVE if n == 1 else None))
                dacc[gdc] = sDVE.n
            down_done_pe = sPE.n
        barrier()

        if DEBUG:
            xdma(lambda s: s.dma_start(out=taps["tap_y2"], in_=A[:, :]),
                 waits=[(sDVE, sDVE.n)])
        ln4_pe = {}
        for dc in range(NDC):
            cw = []
            if dc >= 4:
                cw.append((sPE, ln4_pe[dc - 4]))
            p.op("scalar", lambda s, _dc=dc: s.activation(
                SQB[_dc % 4][:, :], A[:, _dc * T:(_dc + 1) * T], AF.Square),
                waits=cw, inc=sACT)
            p.op("scalar", lambda s, _dc=dc: s.copy(
                XB[_dc % 4][:, :], A[:, _dc * T:(_dc + 1) * T]),
                inc=sACT)
            sq = sACT.n
            st, sp = (dc == 0), (dc == NDC - 1)
            for n in range(2):
                p.op("tensor", lambda t, _dc=dc, _n=n, _s=st, _e=sp: t.matmul(
                    STAT[_n], ONE_B[:, :],
                    XB[_dc % 4][:, _n * 512:(_n + 1) * 512],
                    start=_s, stop=_e),
                    waits=([(sACT, sq)] if n == 0 else ()))
            for n in range(2):
                p.op("tensor", lambda t, _dc=dc, _n=n, _s=st, _e=sp: t.matmul(
                    STAT[2 + _n], ONE_B[:, :],
                    SQB[_dc % 4][:, _n * 512:(_n + 1) * 512], start=_s, stop=_e),
                    inc=(sPE if n == 1 else None))
            ln4_pe[dc] = sPE.n
        barrier()
        bc4 = ln_junction()
        out_dma = {}
        acc_dma = {}
        prev_t1 = [None]
        for dc in range(NDC):
            sl = 0 if dc % 2 == 0 else 2
            w = [(sDVE, bc4)]
            if prev_t1[0] is not None:
                w.append((sDVE, prev_t1[0]))
            p.op("vector", lambda v, _d=dc: v.tensor_tensor(
                SCR[1][:, :], A[:, _d * T:(_d + 1) * T], XIO[0][:, :], ALU.subtract),
                waits=w, inc=sDVE)
            t1 = sDVE.n
            w2 = [(sDVE, t1)] + ([out_dma[dc - 2]] if dc >= 2 else [])
            p.op("vector", lambda v, _sl=sl: v.tensor_tensor(
                SCR[_sl][:, :], SCR[1][:, :], XIO[1][:, :], ALU.mult),
                waits=w2, inc=sDVE)
            t2 = sDVE.n
            prev_t1[0] = t2
            acc_dma[dc] = wdma(lambda g, _d=dc, _sl=sl: g.dma_start(
                out=SCR[_sl][:, :], in_=x2d.ap()[_d * 128:(_d + 1) * 128, :],
                accum_op=ALU.add),
                waits=[(sDVE, t2)])
            out_dma[dc] = xdma(lambda s, _d=dc, _sl=sl: s.dma_start(
                out=outc[_d * 128:(_d + 1) * 128, :], in_=SCR[_sl][:, :]),
                waits=[acc_dma[dc]])

        p.wait("sync", xall() + [(s, s.n) for s in HL] + [(s, s.n) for s in WL])
        p.wait("gpsimd", xall())

        with nc.Block() as block:
            @block.tensor
            def _(eng):
                for f in p.ops["tensor"]:
                    f(eng)

            @block.scalar
            def _(eng):
                for f in p.ops["scalar"]:
                    f(eng)

            @block.vector
            def _(eng):
                for f in p.ops["vector"]:
                    f(eng)

            @block.gpsimd
            def _(eng):
                for f in p.ops["gpsimd"]:
                    f(eng)

            @block.sync
            def _(eng):
                for f in p.ops["sync"]:
                    f(eng)

    return nc


_NC_CACHE = None


def kernel(L0_attn_x, L0_attn_Wqkv, L0_attn_Wout, L0_ffn_w_gate, L0_ffn_w_up,
           L0_ffn_w_down):
    global _NC_CACHE, LAST_TAPS
    x = np.asarray(L0_attn_x, np.float32)
    WqkvT = np.ascontiguousarray(np.asarray(L0_attn_Wqkv, np.float32).T)
    WoutT = np.ascontiguousarray(np.asarray(L0_attn_Wout, np.float32).T)
    WgT = np.ascontiguousarray(np.asarray(L0_ffn_w_gate, np.float32).T)
    WuT = np.ascontiguousarray(np.asarray(L0_ffn_w_up, np.float32).T)
    WdT = np.ascontiguousarray(np.asarray(L0_ffn_w_down, np.float32).T)

    if DEBUG:
        nc = build_nc()
    else:
        global _NC_CACHE
        if _NC_CACHE is None:
            _NC_CACHE = build_nc()
        nc = _NC_CACHE

    in_maps = []
    for c in range(8):
        b, s = c // 4, c % 4
        xcv = np.ascontiguousarray(x[b, s * T:(s + 1) * T, :].T)
        in_maps.append({"xc": xcv, "wqkvT": WqkvT, "woutT": WoutT,
                        "wgT": WgT, "wuT": WuT, "wdT": WdT})
    res = run_bass_kernel_spmd(nc, in_maps, core_ids=list(range(8)))
    if DEBUG:
        LAST_TAPS = res.results
    out = np.empty((2, S, DM), np.float32)
    for c in range(8):
        b, s = c // 4, c % 4
        out[b, s * T:(s + 1) * T, :] = res.results[c]["outc"].T
    return out
